# version 12
# speedup vs baseline: 1.0408x; 1.0042x over previous
import numpy as np
import ml_dtypes
import concourse.bass as bass
import concourse.mybir as mybir
from concourse.bass_utils import run_bass_kernel_spmd
from contextlib import ExitStack

F32 = mybir.dt.float32
BF16 = mybir.dt.bfloat16
AF = mybir.ActivationFunctionType
ALU = mybir.AluOpType
AX = mybir.AxisListType
BFNP = ml_dtypes.bfloat16

NCORES = 8
T = 2048
NT = T // 128
D = 1024
ND = D // 128
NE = 8
EPS = 1e-5
SWI_A = 1.702
WSCALE = 0.5 / SWI_A
C_M1 = -1.0
C_MA = -SWI_A
C_A7 = SWI_A * 7.0

CONST_VALS = [C_M1, C_MA, C_A7, EPS]



class Ctx:
    def __init__(self, nc, stack, block):
        self.nc = nc
        self.stack = stack
        self.block = block
        self.semh = {}
        self.semn = {}

    def sem(self, name):
        if name not in self.semh:
            self.semh[name] = self.stack.enter_context(self.nc.semaphore(name))
            self.semn[name] = 0
        return self.semh[name]

    def dsem(self, name):
        if not hasattr(self, "dma_sems"):
            self.dma_sems = set()
        self.dma_sems.add(name)
        return self.sem(name)

    def inc(self, name, instr, k=1):
        instr.then_inc(self.sem(name), k)
        self.semn[name] += k
        return self.semn[name]

    def n(self, name):
        self.sem(name)
        return self.semn[name]

    prefix = ""

    def sb(self, name, shape, dt, stk=None, side=None):
        name = self.prefix + name
        if side:
            return (stk or self.stack).enter_context(self.nc.sbuf_tensor(name, shape, dt, side=side))
        return (stk or self.stack).enter_context(self.nc.sbuf_tensor(name, shape, dt))

    def ps(self, name, shape, dt=F32, stk=None):
        return (stk or self.stack).enter_context(self.nc.psum_tensor(self.prefix + name, shape, dt))


ENG_NAMES = ("sync", "gpsimd", "scalar", "vector", "tensor")


def barrier(ctx):
    target = ctx.n("PH") + len(ENG_NAMES)
    for en in ENG_NAMES:
        sec = getattr(ctx.block, en)

        def mk(en):
            def _(eng):
                if en == "sync":
                    for s in sorted(getattr(ctx, "dma_sems", ())):
                        eng.wait_ge(ctx.sem(s), ctx.n(s))
                ctx.inc("PH", eng.drain())
                eng.wait_ge(ctx.sem("PH"), target)
            return _
        sec(mk(en))
    ctx.semn["PH"] = target


def cst(ctx, v):
    return ctx.ctile[:, ctx.cidx[v]:ctx.cidx[v] + 1]


def emit_common(ctx, io):
    ctx.cidx = {v: i for i, v in enumerate(CONST_VALS)}
    ctx.ctile = ctx.sb("ctile", [128, len(CONST_VALS)], F32)
    ctx.iden = ctx.sb("sb_iden", [128, 128], F32)
    ctx.idenb = ctx.sb("sb_idenb", [128, 128], BF16)

    @ctx.block.gpsimd
    def _(gpsimd):
        for v, i in ctx.cidx.items():
            gpsimd.memset(ctx.ctile[:, i:i + 1], float(v))
        ctx.inc("GP", gpsimd.drain())
    ctx.gp_consts = ctx.n("GP")

    @ctx.block.sync
    def _(sync):
        ctx.inc("DCM", sync.dma_start(out=ctx.iden[:, :], in_=io["iden"][:, :]), 16)
    ctx.dsem("DCM")

    @ctx.block.gpsimd
    def _(gpsimd):
        ctx.inc("DCMG", gpsimd.dma_start(out=ctx.idenb[:, :], in_=io["iden"][:, :]), 16)
    ctx.dsem("DCMG")


def emit_load_resid(ctx, io, name="x"):
    ctx.resid = ctx.sb("resid", [128, NT * D], F32)
    src = io[name].rearrange("(t p) d -> p t d", p=128)
    dst = ctx.resid[:, :].rearrange("p (t d) -> p t d", t=NT)

    @ctx.block.sync
    def _(sync):
        ctx.inc("DRES", sync.dma_start(out=dst, in_=src), 16)
    ctx.dsem("DRES")


def emit_store_resid(ctx, io, name="y"):
    dst = io[name].rearrange("(t p) d -> p t d", p=128)
    src = ctx.resid[:, :].rearrange("p (t d) -> p t d", t=NT)

    @ctx.block.sync
    def _(sync):
        sync.wait_ge(ctx.sem("PH"), ctx.n("PH"))
        ctx.inc("ST", sync.dma_start(out=dst, in_=src), 16)
        sync.wait_ge(ctx.sem("ST"), ctx.n("ST"))



def emit_ln(ctx):
    with ExitStack() as stk:
        sq = ctx.sb("ln_sq", [128, D], F32, stk)
        S1 = ctx.sb("ln_s1", [128, NT], F32, stk)
        S2 = ctx.sb("ln_s2", [128, NT], F32, stk)
        MU = ctx.sb("ln_mu", [128, NT], F32, stk)
        T1 = ctx.sb("ln_t1", [128, NT], F32, stk)
        RS = ctx.sb("ln_rs", [128, NT], F32, stk)
        NMU = ctx.sb("ln_nmu", [128, NT], F32, stk)
        lnmaj = ctx.sb("ln_maj", [128, 4 * D], BF16, stk)
        psT = [ctx.ps(f"ln_psT{i}", [128, 1024], BF16, stk) for i in range(2)]

        @ctx.block.scalar
        def _(scalar):
            scalar.wait_ge(ctx.sem("GP"), ctx.gp_consts)
            for t in range(NT):
                scalar.activation(out=sq[:, :], in_=ctx.resid[:, t * D:(t + 1) * D],
                                  func=AF.Square, accum_out=S2[:, t:t + 1])
                scalar.drain()
            ctx.inc("SC", scalar.drain())
        sc_stats = ctx.n("SC")

        @ctx.block.vector
        def _(vector):
            for t in range(NT):
                vector.reduce_sum(S1[:, t:t + 1], ctx.resid[:, t * D:(t + 1) * D], axis=AX.X)
            vector.drain()
            vector.tensor_scalar_mul(MU[:, :], S1[:, :], 1.0 / D)
            vector.wait_ge(ctx.sem("SC"), sc_stats)
            vector.tensor_scalar_mul(S2[:, :], S2[:, :], 1.0 / D)
            vector.drain()
            vector.tensor_mul(T1[:, :], MU[:, :], MU[:, :])
            vector.drain()
            vector.tensor_sub(S2[:, :], S2[:, :], T1[:, :])
            ctx.inc("VC", vector.drain())
        vc_var = ctx.n("VC")

        @ctx.block.scalar
        def _(scalar):
            scalar.wait_ge(ctx.sem("VC"), vc_var)
            scalar.activation(out=RS[:, :], in_=S2[:, :], func=AF.Sqrt, bias=cst(ctx, EPS))
            ctx.inc("SC", scalar.drain())
        sc_std = ctx.n("SC")

        @ctx.block.vector
        def _(vector):
            vector.wait_ge(ctx.sem("SC"), sc_std)
            vector.reciprocal(T1[:, :], RS[:, :])
            vector.drain()
            vector.scalar_tensor_tensor(NMU[:, :], MU[:, :], -1.0, T1[:, :], ALU.mult, ALU.mult)
            ctx.inc("VC", vector.drain())
        vc_rstd = ctx.n("VC")

        sc_base = ctx.n("SC")
        tc_base = ctx.n("TC")
        vc_base = ctx.n("VC")
        for g4 in range(4):
            @ctx.block.scalar
            def _(scalar, g4=g4):
                if g4 == 0:
                    scalar.wait_ge(ctx.sem("VC"), vc_rstd)
                if g4 >= 1:
                    scalar.wait_ge(ctx.sem("TC"), tc_base + g4 * 8)
                for tl in range(4):
                    t = g4 * 4 + tl
                    scalar.activation(out=lnmaj[:, tl * D:(tl + 1) * D],
                                      in_=ctx.resid[:, t * D:(t + 1) * D],
                                      func=AF.Identity, scale=T1[:, t:t + 1], bias=NMU[:, t:t + 1])
                ctx.inc("SC", scalar.drain())

            @ctx.block.tensor
            def _(tensor, g4=g4):
                tensor.wait_ge(ctx.sem("SC"), sc_base + g4 + 1)
                for r in range(ND):
                    par = r % 2
                    if g4 * 8 + r >= 2:
                        tensor.wait_ge(ctx.sem("VC"), vc_base + g4 * 8 + r - 1)
                    last = None
                    for tl in range(4):
                        last = tensor.transpose(psT[par][:, tl * 128:(tl + 1) * 128],
                                                lnmaj[:, tl * D + r * 128: tl * D + (r + 1) * 128],
                                                ctx.idenb[:, :])
                    ctx.inc("TC", last)

            @ctx.block.vector
            def _(vector, g4=g4):
                for r in range(ND):
                    par = r % 2
                    vector.wait_ge(ctx.sem("TC"), tc_base + g4 * 8 + r + 1)
                    ctx.inc("VC", vector.tensor_copy(
                        ctx.lnT[:, r * T + g4 * 512: r * T + (g4 + 1) * 512], psT[par][:, 0:512]))

        @ctx.block.vector
        def _(vector):
            if getattr(ctx, "debug", False):
                vector.tensor_copy(ctx.dbg[:, 256:272], MU[:, :])
                vector.tensor_copy(ctx.dbg[:, 272:288], S2[:, :])
                vector.tensor_copy(ctx.dbg[:, 288:304], T1[:, :])
                vector.tensor_copy(ctx.dbg[:, 1024:3072], ctx.lnT[:, T:2 * T])
            ctx.inc("VC", vector.drain())
    ctx.lnT_ready = ctx.n("VC")



def emit_moe(ctx, io):
    with ExitStack() as stk:
        wrT = ctx.sb("m_wrT", [128, ND * NE], BF16, stk)
        brbc = ctx.sb("m_brbc", [128, NT * NE], F32, stk)
        bg_raw = ctx.sb("m_bgraw", [128, 128], F32, stk)
        bu_raw = ctx.sb("m_buraw", [128, 128], F32, stk)
        GB7 = ctx.sb("m_gb7", [128, 128], F32, stk)
        BU1 = ctx.sb("m_bu1", [128, 128], F32, stk)
        LG = ctx.sb("m_lg", [128, NT * NE], F32, stk)
        WK = ctx.sb("m_wk", [128, NT * NE], F32, stk)
        EQ = ctx.sb("m_eq", [128, NT * NE], F32, stk)
        KP = ctx.sb("m_kp", [128, NT * NE], F32, stk)
        EX = ctx.sb("m_ex", [128, NT * NE], F32, stk)
        Ma = ctx.sb("m_ma", [128, NT], F32, stk)
        Mt = ctx.sb("m_mt", [128, NT], F32, stk)
        M4 = ctx.sb("m_m4", [128, NT], F32, stk)
        NM1 = ctx.sb("m_nm1", [128, NT], F32, stk)
        SM = ctx.sb("m_sm", [128, NT], F32, stk)
        RC = ctx.sb("m_rc", [128, NT], F32, stk)
        WT = ctx.sb("m_wt", [128, NT * NE], F32, stk)
        wg = [ctx.sb(f"m_wg{i}", [128, ND * 128], BF16, stk) for i in range(2)]
        wu = [ctx.sb(f"m_wu{i}", [128, ND * 128], BF16, stk) for i in range(2)]
        wd = ctx.sb("m_wd", [128, 8 * D], BF16, stk)
        dbf = [ctx.sb(f"m_dbf{i}", [1, D], F32, stk) for i in range(2)]
        dbb = [ctx.sb(f"m_dbb{i}", [1, D], BF16, stk) for i in range(2)]
        onesb = ctx.sb("m_ones", [1, 128], BF16, stk)
        rg = ctx.sb("m_rg", [128, T], F32, stk)
        Ssc = ctx.sb("m_S", [128, T], F32, stk)
        u1 = ctx.sb("m_u1", [128, T], F32, stk)
        act = ctx.sb("m_act", [128, 8 * T], BF16, stk)
        psG = [ctx.ps(f"m_psG{i}", [128, 1024], F32, stk) for i in range(2)]
        psU = [ctx.ps(f"m_psU{i}", [128, 1024], F32, stk) for i in range(2)]

        lnr = ctx.lnT_ready

        @ctx.block.sync
        def _(sync):
            ctx.inc("DMS", sync.dma_start(out=brbc[:, :], in_=io["brbc"][:, :]), 16)
            ctx.inc("DMS", sync.dma_start(out=bg_raw[:, :], in_=io["bg"][:, :]), 16)
            ctx.inc("DMS", sync.dma_start(out=bu_raw[:, :], in_=io["bu"][:, :]), 16)
        ctx.dsem("DMS")
        dw_static = ctx.n("DMS")

        @ctx.block.gpsimd
        def _(gpsimd):
            gpsimd.memset(onesb[:, :], 1.0)
            ctx.inc("GP", gpsimd.drain())
            ctx.inc("DMSG", gpsimd.dma_start(
                out=wrT[:, :].rearrange("p (r e) -> p r e", r=ND),
                in_=io["wrT"].rearrange("(r p) e -> p r e", p=128)), 16)
        ctx.dsem("DMSG")
        gp_ones = ctx.n("GP")
        dg_wrT = ctx.n("DMSG")

        @ctx.block.vector
        def _(vector):
            vector.wait_ge(ctx.sem("DMS"), dw_static)
            vector.tensor_scalar(GB7[:, :], bg_raw[:, :], -1.0, 7.0, ALU.mult, ALU.add)
            vector.tensor_scalar(BU1[:, :], bu_raw[:, :], 1.0, None, ALU.add)
            ctx.inc("VC", vector.drain())
        vc_bias = ctx.n("VC")

        @ctx.block.tensor
        def _(tensor):
            tensor.wait_ge(ctx.sem("VC"), lnr)
            tensor.wait_ge(ctx.sem("DMSG"), dg_wrT)
            last = None
            for t in range(NT):
                for r in range(ND):
                    last = tensor.matmul(psG[0][:, t * 8:(t + 1) * 8],
                                         ctx.lnT[:, r * T + t * 128: r * T + (t + 1) * 128],
                                         wrT[:, r * 8:(r + 1) * 8],
                                         start=(r == 0), stop=(r == ND - 1))
            ctx.inc("TC", last)
        tc_router = ctx.n("TC")

        @ctx.block.vector
        def _(vector):
            vector.wait_ge(ctx.sem("TC"), tc_router)
            vector.wait_ge(ctx.sem("VC"), vc_bias)
            if getattr(ctx, "debug", False):
                vector.tensor_copy(ctx.dbg[:, 0:128], psG[0][:, 0:128])
                vector.drain()
            vector.tensor_add(LG[:, :], psG[0][:, 0:NT * NE], brbc[:, :])
            vector.drain()
            vector.tensor_copy(WK[:, :], LG[:, :])
            for t in range(NT):
                vector.reduce_max(Ma[:, t:t + 1], LG[:, t * 8:(t + 1) * 8], axis=AX.X)
            vector.drain()
            for k in range(3):
                msrc = Ma if k == 0 else Mt
                for t in range(NT):
                    vector.tensor_scalar(EQ[:, t * 8:(t + 1) * 8], WK[:, t * 8:(t + 1) * 8],
                                         msrc[:, t:t + 1], None, ALU.is_equal)
                vector.drain()
                vector.scalar_tensor_tensor(WK[:, :], EQ[:, :], -1e30, WK[:, :], ALU.mult, ALU.add)
                vector.drain()
                mdst = M4 if k == 2 else Mt
                for t in range(NT):
                    vector.reduce_max(mdst[:, t:t + 1], WK[:, t * 8:(t + 1) * 8], axis=AX.X)
                vector.drain()
            for t in range(NT):
                vector.tensor_scalar(KP[:, t * 8:(t + 1) * 8], LG[:, t * 8:(t + 1) * 8],
                                     M4[:, t:t + 1], None, ALU.is_ge)
            vector.tensor_scalar_mul(NM1[:, :], Ma[:, :], -1.0)
            ctx.inc("VC", vector.drain())
        vc_keep = ctx.n("VC")

        @ctx.block.scalar
        def _(scalar):
            scalar.wait_ge(ctx.sem("VC"), vc_keep)
            for t in range(NT):
                scalar.activation(out=EX[:, t * 8:(t + 1) * 8], in_=LG[:, t * 8:(t + 1) * 8],
                                  func=AF.Exp, bias=NM1[:, t:t + 1])
            ctx.inc("SC", scalar.drain())
        sc_exp = ctx.n("SC")

        @ctx.block.vector
        def _(vector):
            vector.wait_ge(ctx.sem("SC"), sc_exp)
            vector.tensor_mul(EX[:, :], EX[:, :], KP[:, :])
            vector.drain()
            for t in range(NT):
                vector.reduce_sum(SM[:, t:t + 1], EX[:, t * 8:(t + 1) * 8], axis=AX.X)
            vector.drain()
            vector.reciprocal(RC[:, :], SM[:, :])
            vector.drain()
            for t in range(NT):
                vector.tensor_scalar(WT[:, t * 8:(t + 1) * 8], EX[:, t * 8:(t + 1) * 8],
                                     RC[:, t:t + 1], WSCALE, ALU.mult, ALU.mult)
            if getattr(ctx, "debug", False):
                vector.tensor_copy(ctx.dbg[:, 128:256], LG[:, :])
                vector.tensor_copy(ctx.dbg[:, 336:464], WT[:, :])
                vector.tensor_copy(ctx.dbg[:, 304:320], M4[:, :])
                vector.tensor_copy(ctx.dbg[:, 320:336], SM[:, :])
            ctx.inc("VC", vector.drain())
        vc_wt = ctx.n("VC")

        slot_par = 0
        wgbuf_free_tc = [0, 0]
        ps_slot_free = [{"sc": 0, "vc": 0}, {"sc": 0, "vc": 0}]
        ps_pool_free_vc = vc_wt
        wd_free_tc = 0
        db_free_tc = [0, 0]
        gelems = io["gate_upT"]
        delems = io["downT"]

        first_slot = True
        for e in range(NE):
            for half in range(2):
                dg_wgwu = []
                for i in range(2):
                    ctx.dsem(f"DWG{i}")
                    ctx.dsem(f"DWU{i}")
                    ctx.dsem(f"DDB{i}")
                ctx.dsem("DWD")

                tc_base = ctx.n("TC")

                @ctx.block.gpsimd
                def _(gpsimd, e=e, half=half, tc_base=tc_base):
                    if half == 0:
                        par = e % 2
                        if db_free_tc[par]:
                            gpsimd.wait_ge(ctx.sem("TC"), db_free_tc[par])
                        ctx.inc(f"DDB{par}", gpsimd.dma_start(out=dbf[par][:, :], in_=io["db"][e:e + 1, :]), 16)
                    for j in range(8):
                        jj = half * 8 + j
                        par = j % 2
                        if j >= 2:
                            gpsimd.wait_ge(ctx.sem("TC"), tc_base + 4 * (j - 2) + 4)
                        elif wgbuf_free_tc[par]:
                            gpsimd.wait_ge(ctx.sem("TC"), wgbuf_free_tc[par])
                        ctx.inc(f"DWG{par}", gpsimd.dma_start(
                            out=wg[par][:, :].rearrange("p (r f) -> p r f", r=ND),
                            in_=gelems[e, :, jj * 128:(jj + 1) * 128]
                                .rearrange("(r p) f -> p r f", p=128)), 16)
                        ctx.inc(f"DWU{par}", gpsimd.dma_start(
                            out=wu[par][:, :].rearrange("p (r f) -> p r f", r=ND),
                            in_=gelems[e, :, 2048 + jj * 128: 2048 + (jj + 1) * 128]
                                .rearrange("(r p) f -> p r f", p=128)), 16)
                        dg_wgwu.append((ctx.n(f"DWG{par}"), ctx.n(f"DWU{par}")))
                    if wd_free_tc:
                        gpsimd.wait_ge(ctx.sem("TC"), wd_free_tc)
                    for j in range(8):
                        jj = half * 8 + j
                        ctx.inc("DWD", gpsimd.dma_start(
                            out=wd[:, j * D:(j + 1) * D],
                            in_=delems[e, jj * 128:(jj + 1) * 128, :]), 16)
                dg_wd = ctx.n("DWD")
                dg_db = ctx.n(f"DDB{e % 2}")

                for j in range(8):
                    slotinfo = []
                    for th in range(2):
                        par = slot_par % 2
                        slot_par += 1
                        slot_pool_wait = ps_pool_free_vc if (j == 0 and th == 0) else 0
                        slot_sc_wait = ps_slot_free[par]["sc"]
                        slot_vc_wait = ps_slot_free[par]["vc"]

                        @ctx.block.tensor
                        def _(tensor, j=j, th=th, par=par, dgw=dg_wgwu[j],
                              spw=slot_pool_wait, ssw=slot_sc_wait, svw=slot_vc_wait):
                            tensor.wait_ge(ctx.sem(f"DWG{j % 2}"), dgw[0])
                            tensor.wait_ge(ctx.sem(f"DWU{j % 2}"), dgw[1])
                            if spw:
                                tensor.wait_ge(ctx.sem("VC"), spw)
                            if ssw:
                                tensor.wait_ge(ctx.sem("SC"), ssw)
                            if svw:
                                tensor.wait_ge(ctx.sem("VC"), svw)
                            last = None
                            for c in range(2):
                                tok0 = th * 1024 + c * 512
                                for r in range(ND):
                                    last = tensor.matmul(
                                        psG[par][:, c * 512:(c + 1) * 512],
                                        wg[j % 2][:, r * 128:(r + 1) * 128],
                                        ctx.lnT[:, r * T + tok0: r * T + tok0 + 512],
                                        start=(r == 0), stop=(r == ND - 1))
                            ctx.inc("TC", last)
                            last = None
                            for c in range(2):
                                tok0 = th * 1024 + c * 512
                                for r in range(ND):
                                    last = tensor.matmul(
                                        psU[par][:, c * 512:(c + 1) * 512],
                                        wu[j % 2][:, r * 128:(r + 1) * 128],
                                        ctx.lnT[:, r * T + tok0: r * T + tok0 + 512],
                                        start=(r == 0), stop=(r == ND - 1))
                            ctx.inc("TC", last)
                        slotinfo.append((par, ctx.n("TC") - 1, ctx.n("TC")))
                        if th == 1:
                            wgbuf_free_tc[j % 2] = ctx.n("TC")

                    jj = half * 8 + j
                    bcol = e * 16 + jj
                    sc_of_th = []

                    @ctx.block.scalar
                    def _(scalar, bcol=bcol, slotinfo=list(slotinfo), fs=first_slot):
                        if fs:
                            scalar.wait_ge(ctx.sem("VC"), vc_bias)
                        for th, (par, tcg, tcu) in enumerate(slotinfo):
                            scalar.wait_ge(ctx.sem("TC"), tcg)
                            scalar.activation(out=rg[:, th * 1024:(th + 1) * 1024],
                                              in_=psG[par][:, :],
                                              func=AF.Relu, scale=cst(ctx, C_M1),
                                              bias=GB7[:, bcol:bcol + 1])
                            scalar.drain()
                            scalar.activation(out=Ssc[:, th * 1024:(th + 1) * 1024],
                                              in_=rg[:, th * 1024:(th + 1) * 1024],
                                              func=AF.Silu, scale=cst(ctx, C_MA),
                                              bias=cst(ctx, C_A7))
                            ctx.inc("SC", scalar.drain())
                            sc_of_th.append(ctx.n("SC"))
                    first_slot = False

                    for th in range(2):
                        par, tcg, tcu = slotinfo[th]
                        ps_slot_free[par]["sc"] = sc_of_th[th]

                        @ctx.block.vector
                        def _(vector, par=par, th=th, j=j, bcol=bcol,
                              tcu=tcu, scs=sc_of_th[th]):
                            vector.wait_ge(ctx.sem("TC"), tcu)
                            vector.tensor_scalar(u1[:, th * 1024:(th + 1) * 1024],
                                                 psU[par][:, :],
                                                 BU1[:, bcol:bcol + 1], 8.0, ALU.add, ALU.min)
                            vector.drain()
                            vector.wait_ge(ctx.sem("SC"), scs)
                            vector.scalar_tensor_tensor(
                                act[:, j * T + th * 1024: j * T + (th + 1) * 1024],
                                u1[:, th * 1024:(th + 1) * 1024], -6.0,
                                Ssc[:, th * 1024:(th + 1) * 1024], ALU.max, ALU.mult)
                            ctx.inc("VC", vector.drain())
                        ps_slot_free[par]["vc"] = ctx.n("VC")

                if half == 0:
                    @ctx.block.vector
                    def _(vector, e=e, dg_db=dg_db):
                        par = e % 2
                        vector.wait_ge(ctx.sem(f"DDB{e % 2}"), dg_db)
                        vector.tensor_scalar_mul(dbf[par][:, :], dbf[par][:, :], SWI_A)
                        vector.drain()
                        vector.tensor_copy(dbb[par][:, :], dbf[par][:, :])
                        ctx.inc("VC", vector.drain())
                vc_db = ctx.n("VC")

                vc_act_done = ctx.n("VC")
                vc_evac_base = ctx.n("VC")
                tc_down = []

                @ctx.block.tensor
                def _(tensor, e=e, half=half, vc_act_done=vc_act_done,
                      dg_wd=dg_wd, vc_db=vc_db, base=vc_evac_base):
                    tensor.wait_ge(ctx.sem("VC"), vc_act_done)
                    tensor.wait_ge(ctx.sem("DWD"), dg_wd)
                    if half == 0:
                        tensor.wait_ge(ctx.sem("VC"), vc_db)
                    for c in range(NT):
                        par = c % 2
                        if c >= 2:
                            tensor.wait_ge(ctx.sem("VC"), base + (c - 2) + 1)
                        psD = psG[par]
                        first = True
                        if half == 0:
                            tensor.matmul(psD[:, 0:512], onesb[0:1, :], dbb[e % 2][0:1, 0:512],
                                          start=True, stop=False)
                            tensor.matmul(psD[:, 512:1024], onesb[0:1, :], dbb[e % 2][0:1, 512:1024],
                                          start=True, stop=False)
                            first = False
                        last = None
                        for j in range(8):
                            a = act[:, j * T + c * 128: j * T + (c + 1) * 128]
                            last = tensor.matmul(psD[:, 0:512], a, wd[:, j * D: j * D + 512],
                                                 start=first, stop=(j == 7))
                            last = tensor.matmul(psD[:, 512:1024], a, wd[:, j * D + 512: (j + 1) * D],
                                                 start=first, stop=(j == 7))
                            first = False
                        ctx.inc("TC", last)
                        tc_down.append(ctx.n("TC"))
                tc_down_end = ctx.n("TC")
                wd_free_tc = tc_down_end
                if half == 0:
                    db_free_tc[e % 2] = tc_down_end

                @ctx.block.vector
                def _(vector, e=e, tc_down=list(tc_down)):
                    vector.drain()
                    for c in range(NT):
                        par = c % 2
                        vector.wait_ge(ctx.sem("TC"), tc_down[c])
                        ctx.inc("VC", vector.scalar_tensor_tensor(
                            ctx.resid[:, c * D:(c + 1) * D],
                            psG[par][:, :], WT[:, c * 8 + e: c * 8 + e + 1],
                            ctx.resid[:, c * D:(c + 1) * D], ALU.mult, ALU.add))
                assert ctx.n("VC") == vc_evac_base + NT
                ps_pool_free_vc = ctx.n("VC")
                ps_slot_free = [{"sc": 0, "vc": 0}, {"sc": 0, "vc": 0}]

        @ctx.block.vector
        def _(vector):
            ctx.inc("VC", vector.drain())



def emit_l1ffn(ctx, io):
    with ExitStack() as stk:
        u1 = ctx.sb("f_u1", [128, ND * 256], BF16, stk)
        v1 = ctx.sb("f_v1", [128, 2 * 4096], BF16, stk)
        u2 = ctx.sb("f_u2", [128, 32 * 256], BF16, stk)
        v2 = ctx.sb("f_v2", [128, 2 * 1024], BF16, stk)
        b1s = ctx.sb("f_b1", [128, 32], F32, stk)
        b2bc = ctx.sb("f_b2bc", [128, 1024], F32, stk)
        Pbf = ctx.sb("f_P", [128, 1024], BF16, stk)
        Qbf = ctx.sb("f_Q", [128, 1024], BF16, stk)
        Hbf = ctx.sb("f_H", [128, 32 * 512], BF16, stk)
        osc = ctx.sb("f_osc", [128, 1024], F32, stk)
        psP = [ctx.ps(f"f_psP{i}", [128, 512], F32, stk) for i in range(2)]
        psH = [ctx.ps(f"f_psH{i}", [128, 512], F32, stk) for i in range(2)]
        psQ = [ctx.ps(f"f_psQ{i}", [128, 512], F32, stk) for i in range(2)]
        psO = ctx.ps("f_psO", [128, 1024], F32, stk)

        lnr = ctx.lnT_ready

        @ctx.block.gpsimd
        def _(gpsimd):
            ctx.inc("DFW", gpsimd.dma_start(
                out=u1[:, :].rearrange("p (rd r) -> p rd r", rd=ND),
                in_=io["u1"].rearrange("(rd p) r -> p rd r", p=128)), 16)
            ctx.inc("DFW", gpsimd.dma_start(
                out=v1[:, :].rearrange("p (rr f) -> p rr f", rr=2),
                in_=io["v1"].rearrange("(rr p) f -> p rr f", p=128)), 16)
            ctx.inc("DFW", gpsimd.dma_start(
                out=u2[:, :].rearrange("p (ft r) -> p ft r", ft=32),
                in_=io["u2"].rearrange("(ft p) r -> p ft r", p=128)), 16)
            ctx.inc("DFW", gpsimd.dma_start(
                out=v2[:, :].rearrange("p (rr d) -> p rr d", rr=2),
                in_=io["v2"].rearrange("(rr p) d -> p rr d", p=128)), 16)
        ctx.dsem("DFW")
        dg_w = ctx.n("DFW")

        @ctx.block.sync
        def _(sync):
            ctx.inc("DFS", sync.dma_start(out=b1s[:, :], in_=io["b1t"][:, :]), 16)
            ctx.inc("DFS", sync.dma_start(out=b2bc[:, :], in_=io["b2bc"][:, :]), 16)
        ctx.dsem("DFS")
        dw_b = ctx.n("DFS")

        for tg in range(4):
            @ctx.block.tensor
            def _(tensor, tg=tg, vce=ctx.n("VC")):
                if tg == 0:
                    tensor.wait_ge(ctx.sem("VC"), lnr)
                    tensor.wait_ge(ctx.sem("DFW"), dg_w)
                else:
                    tensor.wait_ge(ctx.sem("VC"), vce)
                for rr in range(2):
                    last = None
                    for rd in range(ND):
                        last = tensor.matmul(
                            psP[rr][:, :], u1[:, rd * 256 + rr * 128: rd * 256 + (rr + 1) * 128],
                            ctx.lnT[:, rd * T + tg * 512:(rd * T + (tg + 1) * 512)],
                            start=(rd == 0), stop=(rd == ND - 1))
                    ctx.inc("TC", last)
            tc_p = ctx.n("TC")

            @ctx.block.vector
            def _(vector, tc_p=tc_p):
                vector.wait_ge(ctx.sem("TC"), tc_p - 1)
                vector.tensor_copy(Pbf[:, 0:512], psP[0][:, :])
                vector.wait_ge(ctx.sem("TC"), tc_p)
                vector.tensor_copy(Pbf[:, 512:1024], psP[1][:, :])
                ctx.inc("VC", vector.drain())
            vc_p = ctx.n("VC")

            tc_h0 = ctx.n("TC")
            sc_h0 = ctx.n("SC")

            @ctx.block.tensor
            def _(tensor, vc_p=vc_p, tc_h0=tc_h0, sc_h0=sc_h0):
                tensor.wait_ge(ctx.sem("VC"), vc_p)
                for ft in range(32):
                    par = ft % 2
                    if ft >= 2:
                        tensor.wait_ge(ctx.sem("SC"), sc_h0 + ft - 1)
                    last = None
                    for rr in range(2):
                        last = tensor.matmul(psH[par][:, :],
                                             v1[:, rr * 4096 + ft * 128: rr * 4096 + (ft + 1) * 128],
                                             Pbf[:, rr * 512:(rr + 1) * 512],
                                             start=(rr == 0), stop=(rr == 1))
                    ctx.inc("TC", last)

            @ctx.block.scalar
            def _(scalar, tg=tg, tc_h0=tc_h0, dw_b=dw_b):
                if tg == 0:
                    scalar.wait_ge(ctx.sem("DFS"), dw_b)
                for ft in range(32):
                    par = ft % 2
                    scalar.wait_ge(ctx.sem("TC"), tc_h0 + ft + 1)
                    scalar.activation(out=Hbf[:, ft * 512:(ft + 1) * 512], in_=psH[par][:, :],
                                      func=AF.Gelu_apprx_tanh, bias=b1s[:, ft:ft + 1])
                    ctx.inc("SC", scalar.drain())
            sc_h = ctx.n("SC")

            @ctx.block.tensor
            def _(tensor, sc_h=sc_h):
                tensor.wait_ge(ctx.sem("SC"), sc_h)
                for rr in range(2):
                    last = None
                    for ft in range(32):
                        last = tensor.matmul(
                            psQ[rr][:, :], u2[:, ft * 256 + rr * 128: ft * 256 + (rr + 1) * 128],
                            Hbf[:, ft * 512:(ft + 1) * 512],
                            start=(ft == 0), stop=(ft == 31))
                    ctx.inc("TC", last)
            tc_q = ctx.n("TC")

            @ctx.block.vector
            def _(vector, tc_q=tc_q):
                vector.wait_ge(ctx.sem("TC"), tc_q - 1)
                vector.tensor_copy(Qbf[:, 0:512], psQ[0][:, :])
                vector.wait_ge(ctx.sem("TC"), tc_q)
                vector.tensor_copy(Qbf[:, 512:1024], psQ[1][:, :])
                ctx.inc("VC", vector.drain())
            vc_q = ctx.n("VC")

            vc_o0 = ctx.n("VC")
            for tt in range(4):
                @ctx.block.tensor
                def _(tensor, tt=tt, vc_q=vc_q, vc_o0=vc_o0):
                    if tt == 0:
                        tensor.wait_ge(ctx.sem("VC"), vc_q)
                    else:
                        tensor.wait_ge(ctx.sem("VC"), vc_o0 + 2 * tt - 1)
                    last = None
                    for dh in range(2):
                        for rr in range(2):
                            last = tensor.matmul(
                                psO[:, dh * 512:(dh + 1) * 512],
                                Qbf[:, rr * 512 + tt * 128: rr * 512 + (tt + 1) * 128],
                                v2[:, rr * 1024 + dh * 512: rr * 1024 + (dh + 1) * 512],
                                start=(rr == 0), stop=(rr == 1))
                    ctx.inc("TC", last)
                tc_o = ctx.n("TC")

                @ctx.block.vector
                def _(vector, tg=tg, tt=tt, tc_o=tc_o):
                    vector.wait_ge(ctx.sem("TC"), tc_o)
                    ctx.inc("VC", vector.tensor_add(osc[:, :], psO[:, :], b2bc[:, :]))
                    vector.drain()
                    t = tg * 4 + tt
                    vector.scalar_tensor_tensor(
                        ctx.resid[:, t * D:(t + 1) * D], osc[:, :], 0.5,
                        ctx.resid[:, t * D:(t + 1) * D], ALU.mult, ALU.add)
                    ctx.inc("VC", vector.drain())

        @ctx.block.vector
        def _(vector):
            ctx.inc("VC", vector.drain())


def l1ffn_host_prep(inp):
    b1 = inp["L1_ffn_b1"]
    b2 = inp["L1_ffn_b2"]
    return dict(
        u1=np.ascontiguousarray(inp["L1_ffn_U1"]),
        v1=np.ascontiguousarray(inp["L1_ffn_V1"]),
        u2=np.ascontiguousarray(inp["L1_ffn_U2"]),
        v2=np.ascontiguousarray(inp["L1_ffn_V2"]),
        b1t=np.ascontiguousarray(b1.reshape(32, 128).T),
        b2bc=np.ascontiguousarray(np.broadcast_to(b2, (128, 1024))),
    )


def build_l1ffn_program():
    nc = bass.Bass("TRN2", target_bir_lowering=False)
    io = {}
    io["x"] = nc.dram_tensor("x", [T, D], F32, kind="ExternalInput").ap()
    io["u1"] = nc.dram_tensor("u1", [D, 256], F32, kind="ExternalInput").ap()
    io["v1"] = nc.dram_tensor("v1", [256, 4096], F32, kind="ExternalInput").ap()
    io["u2"] = nc.dram_tensor("u2", [4096, 256], F32, kind="ExternalInput").ap()
    io["v2"] = nc.dram_tensor("v2", [256, 1024], F32, kind="ExternalInput").ap()
    io["b1t"] = nc.dram_tensor("b1t", [128, 32], F32, kind="ExternalInput").ap()
    io["b2bc"] = nc.dram_tensor("b2bc", [128, 1024], F32, kind="ExternalInput").ap()
    io["iden"] = nc.dram_tensor("iden", [128, 128], F32, kind="ExternalInput").ap()
    io["y"] = nc.dram_tensor("y", [T, D], F32, kind="ExternalOutput").ap()

    with ExitStack() as stack:
        block = stack.enter_context(nc.Block())
        ctx = Ctx(nc, stack, block)
        ctx.debug = False
        for s in ("PH", "GP", "SC", "VC", "TC", "ST"):
            ctx.sem(s)
        emit_common(ctx, io)
        ctx.lnT = ctx.sb("lnT", [128, ND * T], BF16)
        emit_load_resid(ctx, io, "x")
        barrier(ctx)
        emit_ln(ctx)
        barrier(ctx)
        emit_l1ffn(ctx, io)
        barrier(ctx)
        emit_store_resid(ctx, io, "y")
    return nc



GROUPS = [[0, 1, 2, 3], [4, 5, 6, 7]]


def emit_l0attn(ctx, io):
    with ExitStack() as stk:
        msk = ctx.sb("a_msk", [128, 128], F32, stk)
        lnp = ctx.sb("a_lnp", [64, 4], F32, stk)
        kgbc = ctx.sb("a_kgbc", [128, 64], F32, stk)
        kbbc = ctx.sb("a_kbbc", [128, 64], F32, stk)
        sel_sb = ctx.sb("a_sel", [64, 16], F32, stk)
        wbuf = ctx.sb("a_wbuf", [128, ND * 512], BF16, stk)
        wosb = ctx.sb("a_wosb", [128, 4 * 1024], BF16, stk)
        oTh = ctx.sb("a_oT", [128, 4 * 2048], BF16, stk)
        oS = ctx.sb("a_oS", [64, 2048], BF16, stk)
        qn = ctx.sb("a_qn", [128, 16 * 512], BF16, stk)
        kbf = ctx.sb("a_kbf", [128, 16 * 512], BF16, stk)
        vbf = ctx.sb("a_vbf", [128, 16 * 512], BF16, stk)
        qT_h = ctx.sb("a_qTh", [64, 2048], BF16, stk)
        kT_h = ctx.sb("a_kTh", [64, 2048], BF16, stk)
        ST1 = [ctx.sb(f"a_st{i}", [128, 128], BF16, stk) for i in range(2)]
        Gl = ctx.sb("a_gl", [64, 8 * 64], BF16, stk)
        Pb = [ctx.sb(f"a_pb{i}", [64, 64], BF16, stk) for i in range(2)]
        G_sb = ctx.sb("a_G", [64, 1024], F32, stk)
        gall = ctx.sb("a_gall", [64, 1024], F32, stk)
        Pfx = G_sb
        sqs = ctx.sb("a_sqs", [128, 512], BF16, stk)
        nscr = ctx.sb("a_nscr", [128, 512], F32, stk)
        tscr = ctx.sb("a_tscr", [128, 64], F32, stk)
        QS1 = ctx.sb("a_qs1", [128, 16], F32, stk)
        QS2 = ctx.sb("a_qs2", [128, 16], F32, stk)
        QMU = ctx.sb("a_qmu", [128, 16], F32, stk)
        QT1 = ctx.sb("a_qt1", [128, 16], F32, stk)
        QNM = ctx.sb("a_qnm", [128, 16], F32, stk)
        prj = [ctx.ps(f"a_prj{i}", [128, 512], F32, stk) for i in range(2)]
        trp = [ctx.ps(f"a_trp{i}", [128, 1024], BF16, stk) for i in range(2)]
        scp = [ctx.ps(f"a_scp{i}", [128, 512], F32, stk) for i in range(2)]
        opp = [ctx.ps(f"a_opp{i}", [128, 512], F32, stk) for i in range(2)]

        lnr = ctx.lnT_ready

        @ctx.block.sync
        def _(sync):
            ctx.inc("DAS", sync.dma_start(out=msk[:, :], in_=io["msk_lin"][:, :]), 16)
            ctx.inc("DAS", sync.dma_start(out=lnp[:, :], in_=io["qklnp"][:, :]), 16)
            ctx.inc("DAS", sync.dma_start(out=kgbc[:, :], in_=io["kgbc"][:, :]), 16)
            ctx.inc("DAS", sync.dma_start(out=kbbc[:, :], in_=io["kbbc"][:, :]), 16)
            ctx.inc("DAS", sync.dma_start(out=sel_sb[:, :], in_=io["sel"][:, :]), 16)
        ctx.dsem("DAS")
        dw_as = ctx.n("DAS")

        ctx.dsem("DQW")
        ctx.dsem("DAW")
        ctx.dsem("DGX")
        ctx.dsem("DGA")
        ctx.dsem("DOT")

        prj_war = [(0, 0), (0, 0)]
        prj_evac_free = [0, 0]
        nscr_free_vc = 0
        wo_free_tc = 0
        oS_free_dot = 0

        for half in range(2):
            hoff = half * 512
            proj_t0 = ctx.n("TC")

            for wi, wname in ((0, "wqT"), (1, "wkT"), (2, "wvT")):
                war_tc = ctx.wbuf_free_tc

                @ctx.block.gpsimd
                def _(gpsimd, half=half, wname=wname, war_tc=war_tc):
                    if war_tc:
                        gpsimd.wait_ge(ctx.sem("TC"), war_tc)
                    ctx.inc("DQW", gpsimd.dma_start(
                        out=wbuf[:, :].rearrange("p (rd f) -> p rd f", rd=ND),
                        in_=io[wname][:, half * 512:(half + 1) * 512]
                            .rearrange("(rd p) f -> p rd f", p=128)), 16)
                dg_w = ctx.n("DQW")

                for tt in range(NT):
                    par = (wi * NT + tt) % 2

                    @ctx.block.tensor
                    def _(tensor, tt=tt, wi=wi, par=par, dg_w=dg_w, lnr=lnr,
                          pw=tuple(prj_war[par])):
                        if tt == 0:
                            if wi == 0 and half == 0:
                                tensor.wait_ge(ctx.sem("VC"), lnr)
                            tensor.wait_ge(ctx.sem("DQW"), dg_w)
                        if pw[0]:
                            tensor.wait_ge(ctx.sem("SC"), pw[0])
                        if pw[1]:
                            tensor.wait_ge(ctx.sem("VC"), pw[1])
                        last = None
                        for rd in range(ND):
                            last = tensor.matmul(
                                prj[par][:, :],
                                ctx.lnT[:, rd * T + tt * 128: rd * T + (tt + 1) * 128],
                                wbuf[:, rd * 512:(rd + 1) * 512],
                                start=(rd == 0), stop=(rd == ND - 1))
                        ctx.inc("TC", last)
                    tc_prj = ctx.n("TC")

                    if wi == 2:
                        @ctx.block.vector
                        def _(vector, tt=tt, par=par, tc_prj=tc_prj):
                            vector.wait_ge(ctx.sem("TC"), tc_prj)
                            vector.tensor_copy(vbf[:, tt * 512:(tt + 1) * 512], prj[par][:, :])
                            ctx.inc("VC", vector.drain())
                        prj_war[par] = (0, ctx.n("VC"))
                    else:
                        c0 = par * 8

                        @ctx.block.scalar
                        def _(scalar, par=par, c0=c0, tc_prj=tc_prj,
                              first=(half == 0 and wi == 0 and tt == 0)):
                            if first:
                                scalar.wait_ge(ctx.sem("DAS"), dw_as)
                            scalar.wait_ge(ctx.sem("TC"), tc_prj)
                            for hh in range(8):
                                scalar.activation(out=sqs[:, hh * 64:(hh + 1) * 64],
                                                  in_=prj[par][:, hh * 64:(hh + 1) * 64],
                                                  func=AF.Square,
                                                  accum_out=QS2[:, c0 + hh: c0 + hh + 1])
                            ctx.inc("SC", scalar.drain())
                        sc_st = ctx.n("SC")

                        @ctx.block.vector
                        def _(vector, par=par, c0=c0, tc_prj=tc_prj, sc_st=sc_st):
                            vector.wait_ge(ctx.sem("TC"), tc_prj)
                            for hh in range(8):
                                vector.reduce_sum(QS1[:, c0 + hh: c0 + hh + 1],
                                                  prj[par][:, hh * 64:(hh + 1) * 64], axis=AX.X)
                            vector.drain()
                            vector.tensor_scalar_mul(QMU[:, c0:c0 + 8], QS1[:, c0:c0 + 8], 1.0 / 64)
                            vector.wait_ge(ctx.sem("SC"), sc_st)
                            vector.tensor_scalar_mul(QS2[:, c0:c0 + 8], QS2[:, c0:c0 + 8], 1.0 / 64)
                            vector.drain()
                            vector.tensor_mul(QT1[:, c0:c0 + 8], QMU[:, c0:c0 + 8], QMU[:, c0:c0 + 8])
                            vector.drain()
                            vector.tensor_sub(QS2[:, c0:c0 + 8], QS2[:, c0:c0 + 8], QT1[:, c0:c0 + 8])
                            ctx.inc("VC", vector.drain())
                        vc_var = ctx.n("VC")

                        sc_sd = ctx.n("SC") + 1
                        vc_rs = ctx.n("VC") + 1

                        @ctx.block.scalar
                        def _(scalar, tt=tt, wi=wi, par=par, c0=c0, vc_var=vc_var,
                              vc_rs=vc_rs, nfw=nscr_free_vc):
                            scalar.wait_ge(ctx.sem("VC"), vc_var)
                            scalar.activation(out=QT1[:, c0:c0 + 8], in_=QS2[:, c0:c0 + 8],
                                              func=AF.Sqrt, bias=cst(ctx, EPS))
                            ctx.inc("SC", scalar.drain())
                            scalar.wait_ge(ctx.sem("VC"), vc_rs)
                            if wi == 1 and nfw:
                                scalar.wait_ge(ctx.sem("VC"), nfw)
                            for hh in range(8):
                                if wi == 0:
                                    out = qn[:, tt * 512 + hh * 64: tt * 512 + (hh + 1) * 64]
                                else:
                                    out = nscr[:, hh * 64:(hh + 1) * 64]
                                scalar.activation(out=out, in_=prj[par][:, hh * 64:(hh + 1) * 64],
                                                  func=AF.Identity,
                                                  scale=QS1[:, c0 + hh: c0 + hh + 1],
                                                  bias=QNM[:, c0 + hh: c0 + hh + 1])
                            ctx.inc("SC", scalar.drain())
                        assert ctx.n("SC") == sc_sd + 1

                        @ctx.block.vector
                        def _(vector, c0=c0, sc_sd=sc_sd):
                            vector.wait_ge(ctx.sem("SC"), sc_sd)
                            vector.reciprocal(QS1[:, c0:c0 + 8], QT1[:, c0:c0 + 8])
                            vector.drain()
                            vector.scalar_tensor_tensor(QNM[:, c0:c0 + 8], QMU[:, c0:c0 + 8], -1.0,
                                                        QS1[:, c0:c0 + 8], ALU.mult, ALU.mult)
                            ctx.inc("VC", vector.drain())
                        assert ctx.n("VC") == vc_rs
                        sc_nm = ctx.n("SC")
                        prj_evac_free[par] = sc_nm
                        prj_war[par] = (sc_nm, vc_var)

                        if wi == 1:
                            @ctx.block.vector
                            def _(vector, tt=tt, sc_nm=sc_nm):
                                vector.wait_ge(ctx.sem("SC"), sc_nm)
                                for hh in range(8):
                                    vector.tensor_mul(tscr[:, :], nscr[:, hh * 64:(hh + 1) * 64], kgbc[:, :])
                                    vector.drain()
                                    vector.tensor_add(kbf[:, tt * 512 + hh * 64: tt * 512 + (hh + 1) * 64],
                                                      tscr[:, :], kbbc[:, :])
                                    vector.drain()
                                ctx.inc("VC", vector.drain())
                            nscr_free_vc = ctx.n("VC")
                ctx.wbuf_free_tc = ctx.n("TC")

            vc_proj = ctx.n("VC")
            sc_proj = ctx.n("SC")

            @ctx.block.gpsimd
            def _(gpsimd, half=half, wtc=wo_free_tc):
                if wtc:
                    gpsimd.wait_ge(ctx.sem("TC"), wtc)
                ctx.inc("DAW", gpsimd.dma_start(
                    out=wosb[:, :].rearrange("p (m d) -> p m d", m=4),
                    in_=io["woT"][half * 512:(half + 1) * 512, :]
                        .rearrange("(m p) d -> p m d", p=128)), 16)
            dg_wo = ctx.n("DAW")

            tc_g0 = ctx.n("TC")
            vc_g0 = ctx.n("VC")

            @ctx.block.tensor
            def _(tensor, vc_proj=vc_proj, sc_proj=sc_proj, tc_g0=tc_g0, vc_g0=vc_g0):
                tensor.wait_ge(ctx.sem("VC"), vc_proj)
                tensor.wait_ge(ctx.sem("SC"), sc_proj)
                for i in range(16):
                    hh, blk = i // 2, i % 2
                    if i >= 1:
                        tensor.wait_ge(ctx.sem("VC"), vc_g0 + i)
                    last = None
                    for cl in range(8):
                        col = (blk * 8 + cl) * 512 + hh * 64
                        last = tensor.matmul(opp[0][0:64, 0:64], kbf[:, col:col + 64],
                                             vbf[:, col:col + 64],
                                             start=(cl == 0), stop=(cl == 7))
                    ctx.inc("TC", last)

            @ctx.block.vector
            def _(vector, tc_g0=tc_g0):
                for i in range(16):
                    vector.wait_ge(ctx.sem("TC"), tc_g0 + i + 1)
                    ctx.inc("VC", vector.tensor_copy(G_sb[:, i * 64:(i + 1) * 64], opp[0][0:64, 0:64]))
            assert ctx.n("VC") == vc_g0 + 16
            vc_g = ctx.n("VC")

            @ctx.block.sync
            def _(sync, vc_g=vc_g):
                sync.wait_ge(ctx.sem("VC"), vc_g)
                ctx.inc("DGX", sync.dma_start(out=io["g_loc"][:, :], in_=G_sb[:, :]), 16)
            dgx = ctx.n("DGX")

            @ctx.block.gpsimd
            def _(gpsimd, dgx=dgx):
                gpsimd.wait_ge(ctx.sem("DGX"), dgx)
                ctx.inc("CC", gpsimd.collective_compute(
                    "AllGather", ALU.bypass, replica_groups=GROUPS,
                    ins=[io["g_loc"].opt()], outs=[io["g_all"].opt()]))
            cc_n = ctx.n("CC")

            @ctx.block.gpsimd
            def _(gpsimd, pfw=getattr(ctx, "pfx_free_vc", 0), dgx=dgx):
                if pfw:
                    gpsimd.wait_ge(ctx.sem("VC"), pfw)
                gpsimd.wait_ge(ctx.sem("DGX"), dgx)
                gpsimd.memset(Pfx[:, :], 0.0)
                ctx.inc("GP", gpsimd.drain())
            gp_pfx = ctx.n("GP")

            vc_pfx_base = ctx.n("VC")
            for r in range(4):
                @ctx.block.sync
                def _(sync, r=r, cc_n=cc_n, base=vc_pfx_base):
                    if r == 0:
                        sync.wait_ge(ctx.sem("CC"), cc_n)
                    else:
                        sync.wait_ge(ctx.sem("VC"), base + r)
                    ctx.inc("DGA", sync.dma_start(out=gall[:, :],
                                                  in_=io["g_all"][r * 64:(r + 1) * 64, :]), 16)
                dga = ctx.n("DGA")

                @ctx.block.vector
                def _(vector, r=r, dga=dga, gp_pfx=gp_pfx):
                    vector.wait_ge(ctx.sem("DGA"), dga)
                    if r == 0:
                        vector.wait_ge(ctx.sem("GP"), gp_pfx)
                    for sub in range(2):
                        sgb = r if sub == 0 else 7 - r
                        for hh in range(8):
                            for blk in range(2):
                                pcol = (hh * 2 + blk) * 64
                                vector.scalar_tensor_tensor(
                                    Pfx[:, pcol:pcol + 64],
                                    gall[:, hh * 128 + sub * 64: hh * 128 + (sub + 1) * 64],
                                    sel_sb[:, sgb * 2 + blk: sgb * 2 + blk + 1],
                                    Pfx[:, pcol:pcol + 64], ALU.mult, ALU.add)
                        vector.drain()
                    ctx.inc("VC", vector.drain())
            vc_pfx = ctx.n("VC")
            ctx.pfx_free_vc = 0

            prev_head_t3 = getattr(ctx, "l0_prev_head_t3", 0)
            t3_hist = getattr(ctx, "l0_t3_hist", {})
            ctx.l0_t3_hist = t3_hist
            for hh in range(8):
                h = half * 8 + hh
                mloc = hh // 2
                odd = hh % 2

                head_t0 = ctx.n("TC")

                head_sc0 = ctx.n("SC")

                @ctx.block.tensor
                def _(tensor, hh=hh, sc_prev=ctx.n("SC"), head_sc0=head_sc0,
                      wvp=(vc_proj if hh == 0 else 0)):
                    tensor.wait_ge(ctx.sem("SC"), sc_prev)
                    if wvp:
                        tensor.wait_ge(ctx.sem("VC"), wvp)
                    i = 0
                    for si, srcb in ((0, qn), (1, kbf)):
                        for q4 in range(4):
                            tpar = i % 2
                            if i >= 2:
                                tensor.wait_ge(ctx.sem("SC"), head_sc0 + i - 1)
                            last = None
                            for tl in range(4):
                                tt = q4 * 4 + tl
                                last = tensor.transpose(
                                    trp[tpar][0:64, tl * 128:(tl + 1) * 128],
                                    srcb[:, tt * 512 + hh * 64: tt * 512 + (hh + 1) * 64],
                                    ctx.idenb[:, :])
                            ctx.inc("TC", last)
                            i += 1

                @ctx.block.scalar
                def _(scalar, hh=hh, head_t0=head_t0):
                    i = 0
                    for si, dst in ((0, qT_h), (1, kT_h)):
                        for q4 in range(4):
                            tpar = i % 2
                            i += 1
                            scalar.wait_ge(ctx.sem("TC"), head_t0 + i)
                            if si == 0:
                                scalar.activation(out=dst[:, q4 * 512:(q4 + 1) * 512],
                                                  in_=trp[tpar][0:64, 0:512], func=AF.Identity,
                                                  scale=lnp[:, 0:1], bias=lnp[:, 1:2])
                            else:
                                scalar.activation(out=dst[:, q4 * 512:(q4 + 1) * 512],
                                                  in_=trp[tpar][0:64, 0:512], func=AF.Identity)
                            ctx.inc("SC", scalar.drain())
                sc_tr = ctx.n("SC")

                sT_ready_vc = {}
                t3_tc = {}
                oev_sc = {}
                gl_vc = {}

                for c in range(16):
                    blk, cl = c // 8, c % 8
                    cpar = c % 2

                    if cl == 0:
                        @ctx.block.vector
                        def _(vector, hh=hh, blk=blk, c=c, sc_tr=sc_tr,
                              w1=t3_tc.get(c - 1, prev_head_t3), vc_pfx=vc_pfx):
                            if blk == 0:
                                vector.wait_ge(ctx.sem("SC"), sc_tr)
                                if hh == 0:
                                    vector.wait_ge(ctx.sem("VC"), vc_pfx)
                            if w1:
                                vector.wait_ge(ctx.sem("TC"), w1)
                            pcol = (hh * 2 + blk) * 64
                            vector.tensor_copy(Pb[blk][:, :], Pfx[:, pcol:pcol + 64])
                            ctx.inc("VC", vector.drain())

                    vc_sT = ctx.n("VC") + 1
                    sT_ready_vc[c] = vc_sT
                    tc_0 = ctx.n("TC")
                    t1 = tc_0 + 1
                    t2 = tc_0 + 2 if cl != 7 else None
                    t3 = tc_0 + (3 if cl != 7 else 2)

                    @ctx.block.tensor
                    def _(tensor, hh=hh, c=c, cl=cl, blk=blk, cpar=cpar, sc_tr=sc_tr,
                          wsc=sT_ready_vc.get(c - 2, 0),
                          wgl=gl_vc.get(c - 2, 0),
                          wop=oev_sc.get(c - 2, 0),
                          glv=gl_vc.get(c - 1, 0) if cl >= 1 else 0,
                          vc_sT=vc_sT):
                        col = c * 512 + hh * 64
                        if c == 0:
                            tensor.wait_ge(ctx.sem("SC"), sc_tr)
                        if wsc:
                            tensor.wait_ge(ctx.sem("VC"), wsc)
                        ctx.inc("TC", tensor.matmul(scp[cpar][0:128, 0:128],
                                                    kT_h[:, c * 128:(c + 1) * 128],
                                                    qT_h[:, c * 128:(c + 1) * 128],
                                                    start=True, stop=True))
                        if cl != 7:
                            if wgl:
                                tensor.wait_ge(ctx.sem("VC"), wgl)
                            ctx.inc("TC", tensor.matmul(prj[cpar][0:64, 0:64],
                                                        kbf[:, col:col + 64], vbf[:, col:col + 64],
                                                        start=True, stop=True))
                        tensor.wait_ge(ctx.sem("VC"), vc_sT)
                        if glv:
                            tensor.wait_ge(ctx.sem("VC"), glv)
                        if wop:
                            tensor.wait_ge(ctx.sem("VC"), wop)
                        tensor.matmul(opp[cpar][0:64, 0:128], vbf[:, col:col + 64],
                                      ST1[cpar][:, :], start=True, stop=False)
                        mm = tensor.matmul(opp[cpar][0:64, 0:128], Pb[blk][:, :],
                                           qT_h[:, c * 128:(c + 1) * 128],
                                           start=False, stop=(cl == 0))
                        for j in range(cl):
                            mm = tensor.matmul(opp[cpar][0:64, 0:128], Gl[:, j * 64:(j + 1) * 64],
                                               qT_h[:, c * 128:(c + 1) * 128],
                                               start=False, stop=(j == cl - 1))
                        ctx.inc("TC", mm)
                    assert ctx.n("TC") == t3
                    t3_tc[c] = t3

                    @ctx.block.vector
                    def _(vector, hh=hh, c=c, cpar=cpar, t1=t1, t2=t2,
                          t3=t3, odd=odd, mloc=mloc, osf=oS_free_dot,
                          wtw=t3_tc.get(c - 2, 0), cl=cl):
                        vector.wait_ge(ctx.sem("TC"), t1)
                        if wtw:
                            vector.wait_ge(ctx.sem("TC"), wtw)
                        vector.tensor_mul(ST1[cpar][:, :], scp[cpar][0:128, 0:128], msk[:, :])
                        ctx.inc("VC", vector.drain())
                        if cl != 7:
                            vector.wait_ge(ctx.sem("TC"), t2)
                            vector.tensor_copy(Gl[:, cl * 64:(cl + 1) * 64],
                                               prj[cpar][0:64, 0:64])
                            ctx.inc("VC", vector.drain())
                        vector.wait_ge(ctx.sem("TC"), t3)
                        if odd and c == 0 and osf:
                            vector.wait_ge(ctx.sem("DOT"), osf)
                        if odd:
                            out = oS[:, c * 128:(c + 1) * 128]
                        else:
                            out = oTh[0:64, mloc * 2048 + c * 128: mloc * 2048 + (c + 1) * 128]
                        vector.tensor_copy(out, opp[cpar][0:64, 0:128])
                        ctx.inc("VC", vector.drain())
                    if cl != 7:
                        gl_vc[c] = vc_sT + 1
                    oev_sc[c] = ctx.n("VC")

                prev_head_t3 = ctx.n("TC")
                ctx.l0_prev_head_t3 = prev_head_t3
                t3_hist[half * 8 + hh] = prev_head_t3

                if odd:
                    @ctx.block.sync
                    def _(sync, mloc=mloc, vc_last=ctx.n("VC")):
                        sync.wait_ge(ctx.sem("VC"), vc_last)
                        ctx.inc("DOT", sync.dma_start(
                            out=oTh[64:128, mloc * 2048:(mloc + 1) * 2048], in_=oS[:, :]), 16)
                    oS_free_dot = ctx.n("DOT")

            dot_n = ctx.n("DOT")
            vc_chunks = ctx.n("VC")
            sc_chunks = ctx.n("SC")
            wo_t0 = ctx.n("TC")
            vc_wo0 = ctx.n("VC")

            @ctx.block.tensor
            def _(tensor, dg_wo=dg_wo, dot_n=dot_n, sc_chunks=sc_chunks,
                  vc_wo0=vc_wo0, vc_chunks=vc_chunks):
                tensor.wait_ge(ctx.sem("DAW"), dg_wo)
                tensor.wait_ge(ctx.sem("DOT"), dot_n)
                tensor.wait_ge(ctx.sem("SC"), sc_chunks)
                tensor.wait_ge(ctx.sem("VC"), vc_chunks)
                for tt in range(NT):
                    for dh in range(2):
                        if tt >= 1:
                            tensor.wait_ge(ctx.sem("VC"), vc_wo0 + (tt - 1) * 2 + dh + 1)
                        last = None
                        for mm in range(4):
                            last = tensor.matmul(
                                prj[dh][:, :],
                                oTh[:, mm * 2048 + tt * 128: mm * 2048 + (tt + 1) * 128],
                                wosb[:, mm * 1024 + dh * 512: mm * 1024 + (dh + 1) * 512],
                                start=(mm == 0), stop=(mm == 3))
                        ctx.inc("TC", last)

            @ctx.block.vector
            def _(vector, wo_t0=wo_t0):
                vector.drain()
                for tt in range(NT):
                    for dh in range(2):
                        vector.wait_ge(ctx.sem("TC"), wo_t0 + tt * 2 + dh + 1)
                        ctx.inc("VC", vector.tensor_add(
                            ctx.resid[:, tt * D + dh * 512: tt * D + (dh + 1) * 512],
                            ctx.resid[:, tt * D + dh * 512: tt * D + (dh + 1) * 512],
                            prj[dh][:, :]))
            vc_wo = ctx.n("VC")
            assert vc_wo == vc_wo0 + 32
            wo_free_tc = ctx.n("TC")
            ctx.pfx_free_vc = vc_chunks
            ctx.wbuf_free_tc = ctx.n("TC")
            prj_war = [(0, vc_wo), (0, vc_wo)]

        @ctx.block.vector
        def _(vector):
            ctx.inc("VC", vector.drain())


def l0attn_host_prep(inp, core):
    b, g = core // 4, core % 4
    sel = np.zeros((16,), np.float32)
    for blk, gb in ((0, g), (1, 7 - g)):
        for sgb in range(8):
            if sgb < gb:
                sel[sgb * 2 + blk] = 1.0
    mask = np.tril(np.ones((128, 128), np.float32)).T
    return dict(
        sel=np.ascontiguousarray(np.broadcast_to(sel, (64, 16))),
        msk_lin=np.ascontiguousarray(mask),
    )


def l0attn_host_prep_shared(inp):
    qg = inp["L0_attn_q_gamma"]; qb = inp["L0_attn_q_beta"]
    kg = inp["L0_attn_k_gamma"]; kb = inp["L0_attn_k_beta"]
    return dict(
        wqT=np.ascontiguousarray(inp["L0_attn_Wq"].T),
        wkT=np.ascontiguousarray(inp["L0_attn_Wk"].T),
        wvT=np.ascontiguousarray(inp["L0_attn_Wv"].T),
        woT=np.ascontiguousarray(inp["L0_attn_Wo"].T),
        qklnp=np.ascontiguousarray(np.stack([qg, qb, kg, kb], axis=1)),
        kgbc=np.ascontiguousarray(np.broadcast_to(kg, (128, 64))),
        kbbc=np.ascontiguousarray(np.broadcast_to(kb, (128, 64))),
    )


def build_l0attn_program():
    nc = bass.Bass("TRN2", target_bir_lowering=False)
    io = {}
    io["x"] = nc.dram_tensor("x", [T, D], F32, kind="ExternalInput").ap()
    for nm in ("wqT", "wkT", "wvT", "woT"):
        io[nm] = nc.dram_tensor(nm, [D, D], F32, kind="ExternalInput").ap()
    io["qklnp"] = nc.dram_tensor("qklnp", [64, 4], F32, kind="ExternalInput").ap()
    io["kgbc"] = nc.dram_tensor("kgbc", [128, 64], F32, kind="ExternalInput").ap()
    io["kbbc"] = nc.dram_tensor("kbbc", [128, 64], F32, kind="ExternalInput").ap()
    io["msk_lin"] = nc.dram_tensor("msk_lin", [128, 128], F32, kind="ExternalInput").ap()
    io["sel"] = nc.dram_tensor("sel", [64, 16], F32, kind="ExternalInput").ap()
    io["iden"] = nc.dram_tensor("iden", [128, 128], F32, kind="ExternalInput").ap()
    io["g_loc"] = nc.dram_tensor("g_loc", [64, 1024], F32).ap()
    io["g_all"] = nc.dram_tensor("g_all", [256, 1024], F32).ap()
    io["y"] = nc.dram_tensor("y", [T, D], F32, kind="ExternalOutput").ap()

    with ExitStack() as stack:
        block = stack.enter_context(nc.Block())
        ctx = Ctx(nc, stack, block)
        ctx.debug = False
        ctx.wbuf_free_tc = 0
        for s in ("PH", "GP", "SC", "VC", "TC", "ST", "CC"):
            ctx.sem(s)
        emit_common(ctx, io)
        ctx.lnT = ctx.sb("lnT", [128, ND * T], BF16)
        emit_load_resid(ctx, io, "x")
        barrier(ctx)
        emit_ln(ctx)
        barrier(ctx)
        emit_l0attn(ctx, io)
        barrier(ctx)
        emit_store_resid(ctx, io, "y")
    return nc



KVW = 16384 + 16 * 1040
KVWH = 8192 + 8 * 1040


def emit_l1attn(ctx, io, ln_stk):
    with ExitStack() as stk:
        bqk = ctx.sb("t_bqk", [128, 16], F32, stk)
        bvbc = ctx.sb("t_bvbc", [128, 1024], F32, stk)
        qT = ctx.sb("t_qT", [128, 8 * 2048], BF16, stk)
        sel128 = ctx.sb("t_sel", [128, 2], BF16, stk)
        ones1 = ctx.sb("t_ones1", [1, 128], BF16, stk)
        ones1f = ctx.sb("t_ones1f", [8, 128], F32, stk)
        KNt = ctx.sb("t_knt", [2, 32], F32, stk)
        QNt = ctx.sb("t_qnt", [2, 32], F32, stk)
        KN8 = ctx.sb("t_kn8", [2, 8], F32, stk)
        KN1 = ctx.sb("t_kn1", [1, 16], F32, stk)
        QN1 = ctx.sb("t_qn1", [1, 64], F32, stk)
        KN4 = ctx.sb("t_kn4", [1, 64], F32, stk)
        MN = ctx.sb("t_mn", [1, 64], F32, stk)
        MB = ctx.sb("t_mb", [128, 64], F32, stk)
        ZB = ctx.sb("t_zb", [128, 16], F32, stk)
        MBZ = ctx.sb("t_mbz", [128, 512], F32, stk)
        prj = [ctx.ps(f"t_prj{i}", [128, 512], F32, stk) for i in range(2)]
        prjO = [ctx.ps(f"t_prjO{i}", [128, 512], F32, stk) for i in range(2)]
        accE = [ctx.ps(f"t_accE{i}", [128, 512], F32, stk) for i in range(2)]
        accO = [ctx.ps(f"t_accO{i}", [128, 512], F32, stk) for i in range(2)]
        nrm = prjO[0]
        rbp = prjO[1]

        lnr = ctx.lnT_ready
        kvw = [io["kv_loc0"], io["kv_loc1"]]
        kvg = [io["kv_all0"], io["kv_all1"]]

        @ctx.block.sync
        def _(sync):
            ctx.inc("DTS", sync.dma_start(out=bqk[:, :], in_=io["bqk"][:, :]), 16)
            ctx.inc("DTS", sync.dma_start(out=bvbc[:, :], in_=io["bv_bc"][:, :]), 16)
            ctx.inc("DTS", sync.dma_start(out=ZB[:, :], in_=io["zb"][:, :]), 16)
        ctx.dsem("DTS")
        dts = ctx.n("DTS")

        @ctx.block.gpsimd
        def _(gpsimd):
            gpsimd.memset(ones1[:, :], 1.0)
            gpsimd.memset(ones1f[:, :], 1.0)
            gpsimd.memset(sel128[:, :], 0.0)
            gpsimd.drain()
            gpsimd.memset(sel128[0:64, 0:1], 1.0)
            gpsimd.memset(sel128[64:128, 1:2], 1.0)
            ctx.inc("GP", gpsimd.drain())
        gp_c = ctx.n("GP")

        ctx.dsem("DTW")
        ctx.dsem("DKV0")
        ctx.dsem("DKV1")
        ctx.dsem("DNB")

        with ExitStack() as pstk:
            wbuf = ctx.sb("t_wbuf", [128, ND * 1024], BF16, pstk)
            ksc = ctx.sb("t_ksc", [128, 512], BF16, pstk)
            vsc = ctx.sb("t_vsc", [128, 512], F32, pstk)
            vaH = ctx.sb("t_vaH", [128, 8 * 1040], BF16, pstk)
            sq2 = ctx.sb("t_sq2", [128, 2048], BF16, pstk)

            @ctx.block.gpsimd
            def _(gpsimd):
                gpsimd.memset(vaH[:, :], 1.0)
                ctx.inc("GP", gpsimd.drain())
            gp_va = ctx.n("GP")

            prj_free = [(0, 0), (0, 0)]
            ppar = 0
            ksc_free_dkv = 0
            va_free_dkv = 0
            knt_col = 0
            nrm_free_vc = 0
            wbuf_free = 0

            for wi, wname in ((0, "wq1"), (1, "wk1")):
                @ctx.block.gpsimd
                def _(gpsimd, wname=wname, war_tc=wbuf_free):
                    if war_tc:
                        gpsimd.wait_ge(ctx.sem("TC"), war_tc)
                    ctx.inc("DTW", gpsimd.dma_start(
                        out=wbuf[:, :].rearrange("p (rd f) -> p rd f", rd=ND),
                        in_=io[wname].rearrange("(rd p) f -> p rd f", p=128)), 16)
                dg_w = ctx.n("DTW")

                if wi < 2:
                    for m in range(8):
                        for t4 in range(4):
                            par = ppar % 2
                            ppar += 1

                            @ctx.block.tensor
                            def _(tensor, wi=wi, m=m, t4=t4, par=par, dg_w=dg_w,
                                  pf=tuple(prj_free[par])):
                                if m == 0 and t4 == 0:
                                    if wi == 0:
                                        tensor.wait_ge(ctx.sem("VC"), lnr)
                                    tensor.wait_ge(ctx.sem("DTW"), dg_w)
                                if pf[0]:
                                    tensor.wait_ge(ctx.sem("SC"), pf[0])
                                if pf[1]:
                                    tensor.wait_ge(ctx.sem("VC"), pf[1])
                                last = None
                                for rd in range(ND):
                                    last = tensor.matmul(
                                        prj[par][:, :],
                                        wbuf[:, rd * 1024 + m * 128: rd * 1024 + (m + 1) * 128],
                                        ctx.lnT[:, rd * T + t4 * 512: rd * T + (t4 + 1) * 512],
                                        start=(rd == 0), stop=(rd == ND - 1))
                                ctx.inc("TC", last)
                            tc_p = ctx.n("TC")

                            if wi == 0:
                                @ctx.block.scalar
                                def _(scalar, m=m, t4=t4, par=par, tc_p=tc_p,
                                      first=(m == 0 and t4 == 0)):
                                    if first:
                                        scalar.wait_ge(ctx.sem("DTS"), dts)
                                    scalar.wait_ge(ctx.sem("TC"), tc_p)
                                    scalar.activation(
                                        out=qT[:, m * 2048 + t4 * 512: m * 2048 + (t4 + 1) * 512],
                                        in_=prj[par][:, :], func=AF.Identity,
                                        bias=bqk[:, m:m + 1])
                                    ctx.inc("SC", scalar.drain())
                                prj_free[par] = (ctx.n("SC"), 0)
                            else:
                                @ctx.block.scalar
                                def _(scalar, m=m, par=par, tc_p=tc_p, kdkv=ksc_free_dkv,
                                      sqw=getattr(ctx, "t_sq2_free_tc", 0)):
                                    scalar.wait_ge(ctx.sem("TC"), tc_p)
                                    if kdkv:
                                        scalar.wait_ge(ctx.sem(kdkv[0]), kdkv[1])
                                    if sqw:
                                        scalar.wait_ge(ctx.sem("TC"), sqw)
                                    scalar.activation(out=ksc[:, :], in_=prj[par][:, :],
                                                      func=AF.Identity, bias=bqk[:, 8 + m:9 + m])
                                    scalar.drain()
                                    scalar.activation(out=sq2[:, 0:512], in_=ksc[:, :], func=AF.Square)
                                    ctx.inc("SC", scalar.drain())
                                sc_k = ctx.n("SC")
                                prj_free[par] = (sc_k, 0)

                                @ctx.block.sync
                                def _(sync, m=m, t4=t4, sc_k=sc_k):
                                    sync.wait_ge(ctx.sem("SC"), sc_k)
                                    kh, lm = m // 4, m % 4
                                    ctx.inc("DKV%d" % kh, sync.dma_start(
                                        out=kvw[kh][:, lm * 2048 + t4 * 512: lm * 2048 + (t4 + 1) * 512],
                                        in_=ksc[:, :]), 16)
                                ksc_free_dkv = ("DKV%d" % (m // 4), ctx.n("DKV%d" % (m // 4)))

                                @ctx.block.tensor
                                def _(tensor, sc_k=sc_k, wn=nrm_free_vc, gp_c=gp_c):
                                    tensor.wait_ge(ctx.sem("GP"), gp_c)
                                    tensor.wait_ge(ctx.sem("SC"), sc_k)
                                    if wn:
                                        tensor.wait_ge(ctx.sem("VC"), wn)
                                    ctx.inc("TC", tensor.matmul(nrm[0:2, 0:512], sel128[:, :],
                                                                sq2[:, 0:512], start=True, stop=True))
                                tc_n = ctx.n("TC")

                                @ctx.block.vector
                                def _(vector, tc_n=tc_n, col=knt_col):
                                    vector.wait_ge(ctx.sem("TC"), tc_n)
                                    vector.reduce_max(KNt[:, col:col + 1], nrm[0:2, 0:512], axis=AX.X)
                                    ctx.inc("VC", vector.drain())
                                nrm_free_vc = ctx.n("VC")
                                knt_col += 1
                                ctx.t_sq2_free_tc = tc_n
                wbuf_free = ctx.n("TC")

            sc_qsq0 = ctx.n("SC")

            tc_qn0 = ctx.n("TC")

            @ctx.block.scalar
            def _(scalar, tc_qn0=tc_qn0, kw=getattr(ctx, "t_sq2_free_tc", 0)):
                for m in range(8):
                    if m == 0:
                        if kw:
                            scalar.wait_ge(ctx.sem("TC"), kw)
                    else:
                        scalar.wait_ge(ctx.sem("TC"), tc_qn0 + 4 * m)
                    scalar.activation(out=sq2[:, :], in_=qT[:, m * 2048:(m + 1) * 2048],
                                      func=AF.Square)
                    ctx.inc("SC", scalar.drain())

            vc_qn0 = ctx.n("VC")

            @ctx.block.tensor
            def _(tensor, sc_qsq0=sc_qsq0, tc_qn0=tc_qn0, vc_qn0=vc_qn0, wn=nrm_free_vc):
                if wn:
                    tensor.wait_ge(ctx.sem("VC"), wn)
                i = 0
                for m in range(8):
                    tensor.wait_ge(ctx.sem("SC"), sc_qsq0 + m + 1)
                    for qgg in range(4):
                        if i >= 1:
                            tensor.wait_ge(ctx.sem("VC"), vc_qn0 + i)
                        ctx.inc("TC", tensor.matmul(nrm[0:2, 0:512], sel128[:, :],
                                                    sq2[:, qgg * 512:(qgg + 1) * 512],
                                                    start=True, stop=True))
                        i += 1

            @ctx.block.vector
            def _(vector, tc_qn0=tc_qn0):
                for i in range(32):
                    vector.wait_ge(ctx.sem("TC"), tc_qn0 + i + 1)
                    vector.reduce_max(QNt[:, i:i + 1], nrm[0:2, 0:512], axis=AX.X)
                    ctx.inc("VC", vector.drain())
                vector.reduce_max(KN8[:, :], KNt[:, :].rearrange("p (m c) -> p m c", m=8),
                                  axis=AX.X)
                ctx.inc("VC", vector.drain())
            vc_qn = ctx.n("VC")

            @ctx.block.sync
            def _(sync, vc_qn=vc_qn):
                sync.wait_ge(ctx.sem("VC"), vc_qn)
                ctx.inc("DNB", sync.dma_start(out=io["knloc"][:, :], in_=KN8[:, :]), 16)
                ctx.inc("DNB", sync.dma_start(out=io["nscr"][0:2, 0:32], in_=QNt[:, :]), 16)
            dnb = ctx.n("DNB")

            @ctx.block.gpsimd
            def _(gpsimd, dnb=dnb):
                gpsimd.wait_ge(ctx.sem("DNB"), dnb)
                ctx.inc("CC", gpsimd.collective_compute(
                    "AllReduce", ALU.max, replica_groups=GROUPS,
                    ins=[io["knloc"].opt()], outs=[io["nmax"].opt()]))
            cc_kn = ctx.n("CC")

            @ctx.block.sync
            def _(sync, cc_kn=cc_kn):
                sync.wait_ge(ctx.sem("CC"), cc_kn)
                ctx.inc("DNB", sync.dma_start(
                    out=KN1[:, :], in_=io["nmax"].rearrange("a b -> () (a b)")), 16)
                ctx.inc("DNB", sync.dma_start(
                    out=QN1[:, :].rearrange("p (m a q) -> p m a q", m=8, a=2),
                    in_=io["nscr"][0:2, 0:32].rearrange("a (m q) -> () m a q", m=8)), 16)
            dnb2 = ctx.n("DNB")

            @ctx.block.vector
            def _(vector, dnb2=dnb2):
                vector.wait_ge(ctx.sem("DNB"), dnb2)
                for qg in range(4):
                    vector.tensor_copy(
                        KN4[:, :].rearrange("p (h q) -> p h q", q=4)[:, :, qg:qg + 1],
                        KN1[:, :].rearrange("p h -> p h ()"))
                vector.drain()
                vector.tensor_mul(MN[:, :], QN1[:, :], KN4[:, :])
                ctx.inc("VC", vector.drain())
            vc_mn = ctx.n("VC")

            @ctx.block.scalar
            def _(scalar, vc_mn=vc_mn):
                scalar.wait_ge(ctx.sem("VC"), vc_mn)
                scalar.activation(out=MN[:, :], in_=MN[:, :], func=AF.Sqrt)
                ctx.inc("SC", scalar.drain())
            sc_mn = ctx.n("SC")

            @ctx.block.vector
            def _(vector, sc_mn=sc_mn):
                vector.wait_ge(ctx.sem("SC"), sc_mn)
                vector.tensor_scalar(MN[:, :], MN[:, :], 0.125, 1.0, ALU.mult, ALU.add)
                ctx.inc("VC", vector.drain())
            vc_mnf = ctx.n("VC")

            @ctx.block.tensor
            def _(tensor, vc_mnf=vc_mnf):
                tensor.wait_ge(ctx.sem("VC"), vc_mnf)
                ctx.inc("TC", tensor.matmul(rbp[:, 0:64], ones1f[0:1, :], MN[:, :],
                                            start=True, stop=True))
            tc_mb = ctx.n("TC")

            @ctx.block.vector
            def _(vector, tc_mb=tc_mb, dts=dts):
                vector.wait_ge(ctx.sem("TC"), tc_mb)
                vector.wait_ge(ctx.sem("DTS"), dts)
                vector.tensor_scalar_mul(MB[:, :], rbp[:, 0:64], -1.0)
                vector.drain()
                for gb in range(8):
                    for qb in range(2):
                        vector.tensor_scalar(
                            MBZ[:, gb * 64: (gb + 1) * 64]
                                .rearrange("p (h q) -> p h q", q=4)[:, :, 2 * qb: 2 * qb + 2],
                            MB[:, :].rearrange("p (h q) -> p h q", q=4)[:, :, 2 * qb: 2 * qb + 2],
                            ZB[:, qb * 8 + gb: qb * 8 + gb + 1], None, ALU.add)
                ctx.inc("VC", vector.drain())
            vc_mbz = ctx.n("VC")

            @ctx.block.gpsimd
            def _(gpsimd, war_tc=wbuf_free):
                if war_tc:
                    gpsimd.wait_ge(ctx.sem("TC"), war_tc)
                ctx.inc("DTW", gpsimd.dma_start(
                    out=wbuf[:, :].rearrange("p (rd f) -> p rd f", rd=ND),
                    in_=io["wv1"].rearrange("(rd p) f -> p rd f", p=128)), 16)
            dg_w = ctx.n("DTW")

            for fh in range(2):
                for tt in range(NT):
                    par = ppar % 2
                    ppar += 1

                    @ctx.block.tensor
                    def _(tensor, tt=tt, fh=fh, par=par, dg_w=dg_w,
                          pf=tuple(prj_free[par])):
                        if tt == 0 and fh == 0:
                            tensor.wait_ge(ctx.sem("DTW"), dg_w)
                        if pf[0]:
                            tensor.wait_ge(ctx.sem("SC"), pf[0])
                        if pf[1]:
                            tensor.wait_ge(ctx.sem("VC"), pf[1])
                        last = None
                        for rd in range(ND):
                            last = tensor.matmul(
                                prj[par][:, :],
                                ctx.lnT[:, rd * T + tt * 128: rd * T + (tt + 1) * 128],
                                wbuf[:, rd * 1024 + fh * 512: rd * 1024 + (fh + 1) * 512],
                                start=(rd == 0), stop=(rd == ND - 1))
                        ctx.inc("TC", last)
                    tc_p = ctx.n("TC")

                    @ctx.block.vector
                    def _(vector, tt=tt, fh=fh, par=par, tc_p=tc_p, wdkv=va_free_dkv,
                          gp_va=gp_va):
                        vector.wait_ge(ctx.sem("TC"), tc_p)
                        if tt == 0 and fh == 0:
                            vector.wait_ge(ctx.sem("GP"), gp_va)
                        if tt == 0 and wdkv:
                            vector.wait_ge(ctx.sem(wdkv[0]), wdkv[1])
                        vector.tensor_add(vsc[:, :], prj[par][:, :],
                                          bvbc[:, fh * 512:(fh + 1) * 512])
                        vector.drain()
                        for hh in range(8):
                            vector.tensor_copy(vaH[:, hh * 1040 + tt * 65: hh * 1040 + tt * 65 + 64],
                                               vsc[:, hh * 64:(hh + 1) * 64])
                        ctx.inc("VC", vector.drain())
                    vc_va = ctx.n("VC")
                    prj_free[par] = (0, vc_va)

                vc_vall = ctx.n("VC")

                @ctx.block.sync
                def _(sync, fh=fh, vc_vall=vc_vall):
                    sync.wait_ge(ctx.sem("VC"), vc_vall)
                    for hh in range(8):
                        ctx.inc("DKV%d" % fh, sync.dma_start(
                            out=kvw[fh][:, 8192 + hh * 1040: 8192 + (hh + 1) * 1040],
                            in_=vaH[:, hh * 1040:(hh + 1) * 1040]), 16)
                va_free_dkv = ("DKV%d" % fh, ctx.n("DKV%d" % fh))

                dkvh_all = ctx.n("DKV%d" % fh)

                @ctx.block.gpsimd
                def _(gpsimd, fh=fh, dkvh_all=dkvh_all):
                    gpsimd.wait_ge(ctx.sem("DKV%d" % fh), dkvh_all)
                    ctx.inc("CC", gpsimd.collective_compute(
                        "AllGather", ALU.bypass, replica_groups=GROUPS,
                        ins=[kvw[fh].opt()], outs=[kvg[fh].opt()]))
                if fh == 0:
                    cc_kv0 = ctx.n("CC")
                    dkv0_all = dkvh_all
                else:
                    cc_kv1 = ctx.n("CC")
                    dkv1_all = dkvh_all
            wbuf_free = ctx.n("TC")
            proj_end_tc = ctx.n("TC")

        ln_stk.close()

        with ExitStack() as fstk:
            mskf = ctx.sb("t_mskf", [128, 4 * 512], BF16, fstk)
            kT2 = ctx.sb("t_kT2", [128, 10 * 1024], BF16, fstk)
            vE = ctx.sb("t_vE", [128, 10 * 520], BF16, fstk)
            vO = ctx.sb("t_vO", [128, 10 * 520], BF16, fstk)
            pTE = [ctx.sb(f"t_pTE{i}", [128, 512], BF16, fstk) for i in range(2)]
            pTO = [ctx.sb(f"t_pTO{i}", [128, 512], BF16, fstk) for i in range(2)]
            rrow = ctx.sb("t_rrow", [1, 4096], F32, fstk)
            oT = ctx.sb("t_oT", [128, 8 * 2048], BF16, fstk)
            oS = ctx.sb("t_oS", [64, 2048], BF16, fstk)

            @ctx.block.gpsimd
            def _(gpsimd, proj_end_tc=proj_end_tc, vc_mbz=vc_mbz):
                gpsimd.wait_ge(ctx.sem("TC"), proj_end_tc)
                gpsimd.wait_ge(ctx.sem("VC"), vc_mbz)
                ctx.inc("DTW", gpsimd.dma_start(out=mskf[:, :], in_=io["mskf"][:, :]), 16)
            dts2 = ctx.n("DTW")

            ctx.dsem("DFH")
            ctx.dsem("DOT")
            pair_free_tc = 0
            oS_free_dot = 0
            exp_sc = [0, 0]
            acc_free = [0, 0]
            av_done = [0, 0]
            rb_free_vc = 0

            for m in range(8):
                hE, hO = 2 * m, 2 * m + 1

                @ctx.block.sync
                def _(sync, m=m, hE=hE, hO=hO, wtc=pair_free_tc, dts2=dts2,
                      cc_kv0=cc_kv0, cc_kv1=cc_kv1,
                      dkv0_all=dkv0_all, dkv1_all=dkv1_all):
                    kh, lm = m // 4, m % 4
                    if m == 0:
                        sync.wait_ge(ctx.sem("DKV0"), dkv0_all)
                        sync.wait_ge(ctx.sem("DTW"), dts2)
                    if m == 4:
                        sync.wait_ge(ctx.sem("DKV1"), dkv1_all)
                    if wtc:
                        sync.wait_ge(ctx.sem("TC"), wtc)
                    for qb in range(2):
                        ctx.inc("DFD", sync.dma_start(
                            out=kT2[:, (8 + qb) * 1024:(9 + qb) * 1024],
                            in_=kvw[kh][:, lm * 2048 + qb * 1024: lm * 2048 + (qb + 1) * 1024]), 16)
                        for hh, vt in ((hE, vE), (hO, vO)):
                            lh = hh % 8
                            ctx.inc("DFD", sync.dma_start(
                                out=vt[:, (8 + qb) * 520:(9 + qb) * 520],
                                in_=kvw[kh][:, 8192 + lh * 1040 + qb * 520: 8192 + lh * 1040 + (qb + 1) * 520]), 16)
                    if m == 0:
                        sync.wait_ge(ctx.sem("CC"), cc_kv0)
                    if m == 4:
                        sync.wait_ge(ctx.sem("CC"), cc_kv1)
                    for r in range(4):
                        for sub in range(2):
                            slot = r if sub == 0 else 7 - r
                            ctx.inc("DFH", sync.dma_start(
                                out=kT2[:, slot * 1024:(slot + 1) * 1024],
                                in_=kvg[kh][r * 128:(r + 1) * 128,
                                            lm * 2048 + sub * 1024: lm * 2048 + (sub + 1) * 1024]), 16)
                            for hh, vt in ((hE, vE), (hO, vO)):
                                lh = hh % 8
                                ctx.inc("DFH", sync.dma_start(
                                    out=vt[:, slot * 520:(slot + 1) * 520],
                                    in_=kvg[kh][r * 128:(r + 1) * 128,
                                                8192 + lh * 1040 + sub * 520: 8192 + lh * 1040 + (sub + 1) * 520]), 16)
                dfh = ctx.n("DFH")
                dfd = ctx.n("DFD")

                pend = [None]
                av_rec = {}

                def flush_av(tensor):
                    if pend[0] is None:
                        return
                    (par, vslot, c2i, aq, first, lastu, wsem, wval, uidx) = pend[0]
                    tensor.wait_ge(ctx.sem(wsem), wval)
                    ctx.inc("TC", tensor.matmul(
                        accE[aq][0:65, 0:512],
                        vE[:, vslot * 520 + c2i * 65: vslot * 520 + (c2i + 1) * 65],
                        pTE[par][:, :], start=first, stop=lastu))
                    ctx.inc("TC", tensor.matmul(
                        accO[aq][0:65, 0:512],
                        vO[:, vslot * 520 + c2i * 65: vslot * 520 + (c2i + 1) * 65],
                        pTO[par][:, :], start=first, stop=lastu))
                    av_done[par] = ctx.n("TC")
                    av_rec[uidx] = ctx.n("TC")
                    pend[0] = None

                ppi = 0
                for qblk in range(2):
                    units = []
                    for qg in range(2):
                        if qg == 0:
                            for c2 in range(4):
                                units.append((8 + qblk, qg, c2, "mask"))
                        else:
                            for c2 in range(4):
                                units.append((8 + qblk, qg, c2, "full"))
                            for c2 in range(4, 8):
                                units.append((8 + qblk, qg, c2, "mask"))
                    ngb = 4 if qblk == 0 else 8
                    for gb in range(ngb):
                        for qg in range(2):
                            for c2 in range(8):
                                units.append((gb, qg, c2, "plain"))
                    seen = {0: False, 1: False}
                    lastidx = {0: None, 1: None}
                    for idx, (gb, qg, c2, cls) in enumerate(units):
                        lastidx[qg] = idx

                    sc_base = ctx.n("SC")
                    vc_base = ctx.n("VC")
                    ppi0 = ppi
                    prev_av = {0: av_done[0], 1: av_done[1]}
                    nmask_before = []
                    nm = 0
                    for (gb, qg, c2, cls) in units:
                        nmask_before.append(nm)
                        if cls == "mask":
                            nm += 1
                    tc_scores = []
                    exp_meta = []

                    for idx, (gb, qg, c2, cls) in enumerate(units):
                        par = ppi % 2
                        ppi += 1
                        qgg = qblk * 2 + qg
                        qlo = m * 2048 + qblk * 1024 + qg * 512
                        first = not seen[qg]
                        seen[qg] = True
                        lastu = (idx == lastidx[qg])
                        if cls == "plain":
                            biasE = MBZ[:, gb * 64 + hE * 4 + qgg: gb * 64 + hE * 4 + qgg + 1]
                            biasO = MBZ[:, gb * 64 + hO * 4 + qgg: gb * 64 + hO * 4 + qgg + 1]
                        else:
                            biasE = MB[:, hE * 4 + qgg: hE * 4 + qgg + 1]
                            biasO = MB[:, hO * 4 + qgg: hO * 4 + qgg + 1]
                        if cls == "mask":
                            wsem, wval = "VC", vc_base + nmask_before[idx] + 1
                        else:
                            wsem, wval = "SC", sc_base + idx + 1
                        if idx >= 2:
                            w_prj = ("SC", sc_base + idx - 1)
                        else:
                            w_prj = ("SC", exp_sc[par]) if exp_sc[par] else None

                        @ctx.block.tensor
                        def _(tensor, gb=gb, c2=c2, par=par, dfh=dfh, dfd=dfd,
                              qlo=qlo, w_prj=w_prj, wfirst=(m == 0 and ppi == 1),
                              vc_mbz=vc_mbz, af=acc_free[qg] if first else 0,
                              rbf=rb_free_vc if (qblk == 0 and idx < 2) else 0,
                              first=first, isdiag=(gb >= 8)):
                            if wfirst:
                                tensor.wait_ge(ctx.sem("VC"), vc_mbz)
                            if rbf:
                                tensor.wait_ge(ctx.sem("VC"), rbf)
                            if first and af:
                                tensor.wait_ge(ctx.sem("VC"), af)
                            if isdiag:
                                tensor.wait_ge(ctx.sem("DFD"), dfd)
                            else:
                                tensor.wait_ge(ctx.sem("DFH"), dfh)
                            if w_prj:
                                tensor.wait_ge(ctx.sem(w_prj[0]), w_prj[1])
                            ctx.inc("TC", tensor.matmul(
                                prj[par][0:128, 0:512],
                                kT2[0:64, gb * 1024 + c2 * 128: gb * 1024 + (c2 + 1) * 128],
                                qT[0:64, qlo: qlo + 512],
                                start=True, stop=True))
                            ctx.inc("TC", tensor.matmul(
                                prjO[par][0:128, 0:512],
                                kT2[64:128, gb * 1024 + c2 * 128: gb * 1024 + (c2 + 1) * 128],
                                qT[64:128, qlo: qlo + 512],
                                start=True, stop=True))
                            tc_scores.append(ctx.n("TC"))
                            flush_av(tensor)
                        pend[0] = (par, gb, c2, qg, first, lastu, wsem, wval, idx)
                        exp_meta.append((par, biasE, biasO, idx))
                        exp_sc[par] = sc_base + idx + 1

                    @ctx.block.tensor
                    def _(tensor):
                        flush_av(tensor)
                    tc_last = ctx.n("TC")

                    @ctx.block.scalar
                    def _(scalar, exp_meta=list(exp_meta), tc_scores=list(tc_scores),
                          av_rec=av_rec, prev_av=dict(prev_av)):
                        for (par, biasE, biasO, idx) in exp_meta:
                            scalar.wait_ge(ctx.sem("TC"), tc_scores[idx])
                            if idx >= 2:
                                scalar.wait_ge(ctx.sem("TC"), av_rec[idx - 2])
                            elif prev_av.get(par):
                                scalar.wait_ge(ctx.sem("TC"), prev_av[par])
                            scalar.activation(out=pTE[par][:, :], in_=prj[par][0:128, 0:512],
                                              func=AF.Exp, scale=0.125, bias=biasE)
                            scalar.activation(out=pTO[par][:, :], in_=prjO[par][0:128, 0:512],
                                              func=AF.Exp, scale=0.125, bias=biasO)
                            ctx.inc("SC", scalar.drain())
                    assert ctx.n("SC") == sc_base + len(units)

                    masked = [(i, u) for i, u in enumerate(units) if u[3] == "mask"]
                    if masked:
                        @ctx.block.vector
                        def _(vector, masked=list(masked), sc_base=sc_base, ppi0=ppi0):
                            for (idx, (gb, qg, c2, cls)) in masked:
                                d = c2 if qg == 0 else c2 - 4
                                p = (ppi0 + idx) % 2
                                vector.wait_ge(ctx.sem("SC"), sc_base + idx + 1)
                                vector.tensor_mul(pTE[p][:, :], pTE[p][:, :],
                                                  mskf[:, d * 512:(d + 1) * 512])
                                vector.tensor_mul(pTO[p][:, :], pTO[p][:, :],
                                                  mskf[:, d * 512:(d + 1) * 512])
                                ctx.inc("VC", vector.drain())
                        assert ctx.n("VC") == vc_base + len(masked)

                    for (hh, accT, hp) in ((hE, accE, 0), (hO, accO, 1)):
                        @ctx.block.vector
                        def _(vector, accT=accT, hp=hp, m=m, qblk=qblk,
                              tc_last=tc_last, osf=oS_free_dot, isodd=(hp == 1)):
                            vector.wait_ge(ctx.sem("TC"), tc_last)
                            if isodd and qblk == 0 and osf:
                                vector.wait_ge(ctx.sem("DOT"), osf)
                            for qg in range(2):
                                col = qblk * 1024 + qg * 512
                                r = hp * 4 + qblk * 2 + qg
                                vector.reciprocal(
                                    rrow[0:1, r * 512:(r + 1) * 512],
                                    accT[qg][64:65, 0:512])
                                if not isodd:
                                    out = oT[0:64, m * 2048 + col: m * 2048 + col + 512]
                                else:
                                    out = oS[:, col:col + 512]
                                vector.tensor_copy(out, accT[qg][0:64, 0:512])
                                ctx.inc("VC", vector.drain())
                        if hp == 1:
                            acc_free[0] = ctx.n("VC") - 1
                            acc_free[1] = ctx.n("VC")

                vc_rr = ctx.n("VC")

                rbps = [prj[0], prj[1], prjO[0], prjO[1]]
                div_vc = {}
                for u in range(8):
                    hp, qc = u // 4, u % 4
                    pb = rbps[u % 4]

                    @ctx.block.tensor
                    def _(tensor, u=u, pb=pb, vc_rr=vc_rr,
                          wmul=div_vc.get(u - 4, 0)):
                        if u == 0:
                            tensor.wait_ge(ctx.sem("VC"), vc_rr)
                        if wmul:
                            tensor.wait_ge(ctx.sem("VC"), wmul)
                        ctx.inc("TC", tensor.matmul(
                            pb[0:64, 0:512], ones1f[0:1, 0:64],
                            rrow[0:1, u * 512:(u + 1) * 512],
                            start=True, stop=True))
                    tc_rb_u = ctx.n("TC")

                    @ctx.block.vector
                    def _(vector, hp=hp, qc=qc, pb=pb, tc=tc_rb_u, m=m):
                        vector.wait_ge(ctx.sem("TC"), tc)
                        col = qc * 512
                        if hp == 0:
                            out = oT[0:64, m * 2048 + col: m * 2048 + col + 512]
                        else:
                            out = oS[:, col:col + 512]
                        vector.tensor_mul(out, out, pb[0:64, 0:512])
                        ctx.inc("VC", vector.drain())
                    div_vc[u] = ctx.n("VC")
                rb_free_vc = ctx.n("VC")

                pair_free_tc = ctx.n("TC")

                @ctx.block.sync
                def _(sync, m=m, vc_e=ctx.n("VC")):
                    sync.wait_ge(ctx.sem("VC"), vc_e)
                    ctx.inc("DOT", sync.dma_start(
                        out=oT[64:128, m * 2048:(m + 1) * 2048], in_=oS[:, :]), 16)
                oS_free_dot = ctx.n("DOT")

            flash_end_tc = ctx.n("TC")
            flash_end_vc = ctx.n("VC")
            dot_all = ctx.n("DOT")

        with ExitStack() as wstk:
            wosb = ctx.sb("t_wosb", [128, 8 * 1024], BF16, wstk)
            bobc = ctx.sb("t_bobc", [128, 1024], F32, wstk)
            gmbc = ctx.sb("t_gmbc", [128, 1024], F32, wstk)
            btbc = ctx.sb("t_btbc", [128, 1024], F32, wstk)
            ysc = ctx.sb("t_ysc", [128, 1024], F32, wstk)
            yh = ctx.sb("t_yh", [128, 1024], F32, wstk)
            ysq = ctx.sb("t_ysq", [128, 1024], F32, wstk)
            yst = ctx.sb("t_yst", [128, 8], F32, wstk)

            @ctx.block.sync
            def _(sync, flash_end_tc=flash_end_tc, flash_end_vc=flash_end_vc):
                sync.wait_ge(ctx.sem("TC"), flash_end_tc)
                sync.wait_ge(ctx.sem("VC"), flash_end_vc)
                ctx.inc("DTS", sync.dma_start(out=bobc[:, :], in_=io["bo_bc"][:, :]), 16)
                ctx.inc("DTS", sync.dma_start(out=gmbc[:, :], in_=io["gm_bc"][:, :]), 16)
                ctx.inc("DTS", sync.dma_start(out=btbc[:, :], in_=io["bt_bc"][:, :]), 16)
            dts3 = ctx.n("DTS")

            @ctx.block.gpsimd
            def _(gpsimd, flash_end_tc=flash_end_tc):
                gpsimd.wait_ge(ctx.sem("TC"), flash_end_tc)
                ctx.inc("DTW", gpsimd.dma_start(
                    out=wosb[:, :].rearrange("p (m d) -> p m d", m=8),
                    in_=io["wo1"].rearrange("(m p) d -> p m d", p=128)), 16)
            dg_wo = ctx.n("DTW")

            prev_evac_vc = 0
            sc_nm_prev = 0
            for tt in range(NT):
                @ctx.block.tensor
                def _(tensor, tt=tt, dg_wo=dg_wo, dot_all=dot_all, pv=prev_evac_vc,
                      fv=flash_end_vc):
                    if tt == 0:
                        tensor.wait_ge(ctx.sem("DTW"), dg_wo)
                        tensor.wait_ge(ctx.sem("DOT"), dot_all)
                        tensor.wait_ge(ctx.sem("VC"), fv)
                    if pv:
                        tensor.wait_ge(ctx.sem("VC"), pv)
                    for dh in range(2):
                        last = None
                        for mm in range(8):
                            last = tensor.matmul(
                                accE[dh][:, :],
                                oT[:, mm * 2048 + tt * 128: mm * 2048 + (tt + 1) * 128],
                                wosb[:, mm * 1024 + dh * 512: mm * 1024 + (dh + 1) * 512],
                                start=(mm == 0), stop=(mm == 7))
                        ctx.inc("TC", last)
                tc_wo = ctx.n("TC")

                @ctx.block.vector
                def _(vector, tt=tt, tc_wo=tc_wo, dts3=dts3, snp=sc_nm_prev):
                    if tt == 0:
                        vector.wait_ge(ctx.sem("DTS"), dts3)
                    if snp:
                        vector.wait_ge(ctx.sem("SC"), snp)
                    vector.wait_ge(ctx.sem("TC"), tc_wo - 1)
                    vector.tensor_add(ysc[:, 0:512], accE[0][:, :], bobc[:, 0:512])
                    vector.wait_ge(ctx.sem("TC"), tc_wo)
                    vector.tensor_add(ysc[:, 512:1024], accE[1][:, :], bobc[:, 512:1024])
                    vector.drain()
                    vector.reduce_sum(yst[:, 0:1], ysc[:, :], axis=AX.X)
                    ctx.inc("VC", vector.drain())
                vc_y = ctx.n("VC")

                @ctx.block.scalar
                def _(scalar, vc_y=vc_y):
                    scalar.wait_ge(ctx.sem("VC"), vc_y)
                    scalar.activation(out=ysq[:, :], in_=ysc[:, :], func=AF.Square,
                                      accum_out=yst[:, 1:2])
                    ctx.inc("SC", scalar.drain())
                sc_ys = ctx.n("SC")

                @ctx.block.vector
                def _(vector, sc_ys=sc_ys):
                    vector.wait_ge(ctx.sem("SC"), sc_ys)
                    vector.tensor_scalar_mul(yst[:, 2:3], yst[:, 0:1], 1.0 / D)
                    vector.tensor_scalar_mul(yst[:, 3:4], yst[:, 1:2], 1.0 / D)
                    vector.drain()
                    vector.tensor_mul(yst[:, 4:5], yst[:, 2:3], yst[:, 2:3])
                    vector.drain()
                    vector.tensor_sub(yst[:, 3:4], yst[:, 3:4], yst[:, 4:5])
                    ctx.inc("VC", vector.drain())
                vc_v = ctx.n("VC")

                @ctx.block.scalar
                def _(scalar, vc_v=vc_v):
                    scalar.wait_ge(ctx.sem("VC"), vc_v)
                    scalar.activation(out=yst[:, 5:6], in_=yst[:, 3:4], func=AF.Sqrt,
                                      bias=cst(ctx, EPS))
                    ctx.inc("SC", scalar.drain())
                sc_sd = ctx.n("SC")

                @ctx.block.vector
                def _(vector, sc_sd=sc_sd):
                    vector.wait_ge(ctx.sem("SC"), sc_sd)
                    vector.reciprocal(yst[:, 6:7], yst[:, 5:6])
                    vector.drain()
                    vector.scalar_tensor_tensor(yst[:, 7:8], yst[:, 2:3], -1.0,
                                                yst[:, 6:7], ALU.mult, ALU.mult)
                    ctx.inc("VC", vector.drain())
                vc_rs = ctx.n("VC")

                @ctx.block.scalar
                def _(scalar, vc_rs=vc_rs):
                    scalar.wait_ge(ctx.sem("VC"), vc_rs)
                    scalar.activation(out=yh[:, :], in_=ysc[:, :], func=AF.Identity,
                                      scale=yst[:, 6:7], bias=yst[:, 7:8])
                    ctx.inc("SC", scalar.drain())
                sc_nm = ctx.n("SC")
                sc_nm_prev = sc_nm

                @ctx.block.vector
                def _(vector, tt=tt, sc_nm=sc_nm):
                    vector.wait_ge(ctx.sem("SC"), sc_nm)
                    vector.tensor_mul(yh[:, :], yh[:, :], gmbc[:, :])
                    vector.drain()
                    vector.tensor_add(yh[:, :], yh[:, :], btbc[:, :])
                    vector.drain()
                    vector.tensor_add(ctx.resid[:, tt * D:(tt + 1) * D],
                                      ctx.resid[:, tt * D:(tt + 1) * D], yh[:, :])
                    ctx.inc("VC", vector.drain())
                prev_evac_vc = ctx.n("VC")

        @ctx.block.vector
        def _(vector):
            ctx.inc("VC", vector.drain())


def l1attn_host_prep(inp, core):
    g = core % 4
    zb = np.zeros((2, 8), np.float32)
    for qb, gbq in ((0, g), (1, 7 - g)):
        for gb in range(8):
            if gb >= gbq:
                zb[qb, gb] = -1e30
    mskf = np.zeros((4, 128, 512), np.float32)
    for d in range(4):
        for j in range(128):
            mskf[d, j, d * 128 + j:] = 1.0
    return dict(
        zb=np.ascontiguousarray(np.broadcast_to(zb.reshape(16), (128, 16))),
        mskf=np.ascontiguousarray(mskf.transpose(1, 0, 2).reshape(128, 2048)),
    )


def l1attn_host_prep_shared(inp):
    bq = inp["L1_attn_bq"]; bk = inp["L1_attn_bk"]
    bqk = np.concatenate([bq.reshape(8, 128).T, bk.reshape(8, 128).T], axis=1)
    return dict(
        wq1=np.ascontiguousarray(inp["L1_attn_Wq"]),
        wk1=np.ascontiguousarray(inp["L1_attn_Wk"]),
        wv1=np.ascontiguousarray(inp["L1_attn_Wv"]),
        wo1=np.ascontiguousarray(inp["L1_attn_Wo"]),
        bqk=np.ascontiguousarray(bqk),
        bv_bc=np.ascontiguousarray(np.broadcast_to(inp["L1_attn_bv"], (128, 1024))),
        bo_bc=np.ascontiguousarray(np.broadcast_to(inp["L1_attn_bo"], (128, 1024))),
        gm_bc=np.ascontiguousarray(np.broadcast_to(inp["L1_attn_gamma"], (128, 1024))),
        bt_bc=np.ascontiguousarray(np.broadcast_to(inp["L1_attn_beta"], (128, 1024))),
    )


def build_l1attn_program():
    nc = bass.Bass("TRN2", target_bir_lowering=False)
    io = {}
    io["x"] = nc.dram_tensor("x", [T, D], F32, kind="ExternalInput").ap()
    for nm in ("wq1", "wk1", "wv1", "wo1"):
        io[nm] = nc.dram_tensor(nm, [D, D], F32, kind="ExternalInput").ap()
    io["bqk"] = nc.dram_tensor("bqk", [128, 16], F32, kind="ExternalInput").ap()
    for nm in ("bv_bc", "bo_bc", "gm_bc", "bt_bc"):
        io[nm] = nc.dram_tensor(nm, [128, 1024], F32, kind="ExternalInput").ap()
    io["zb"] = nc.dram_tensor("zb", [128, 16], F32, kind="ExternalInput").ap()
    io["mskf"] = nc.dram_tensor("mskf", [128, 2048], F32, kind="ExternalInput").ap()
    io["iden"] = nc.dram_tensor("iden", [128, 128], F32, kind="ExternalInput").ap()
    io["kv_loc0"] = nc.dram_tensor("kv_loc0", [128, KVWH], BF16).ap()
    io["kv_loc1"] = nc.dram_tensor("kv_loc1", [128, KVWH], BF16).ap()
    io["kv_all0"] = nc.dram_tensor("kv_all0", [512, KVWH], BF16).ap()
    io["kv_all1"] = nc.dram_tensor("kv_all1", [512, KVWH], BF16).ap()
    io["nscr"] = nc.dram_tensor("nscr", [2, 32], F32).ap()
    io["knloc"] = nc.dram_tensor("knloc", [2, 8], F32).ap()
    io["nmax"] = nc.dram_tensor("nmax", [2, 8], F32).ap()
    io["y"] = nc.dram_tensor("y", [T, D], F32, kind="ExternalOutput").ap()

    with ExitStack() as stack:
        block = stack.enter_context(nc.Block())
        ctx = Ctx(nc, stack, block)
        ctx.debug = False
        for s in ("PH", "GP", "SC", "VC", "TC", "ST", "CC"):
            ctx.sem(s)
        emit_common(ctx, io)
        ln_stk = ExitStack()
        ctx.lnT = ctx.sb("lnT", [128, ND * T], BF16, ln_stk, side="right")
        emit_load_resid(ctx, io, "x")
        barrier(ctx)
        emit_ln(ctx)
        barrier(ctx)
        emit_l1attn(ctx, io, ln_stk)
        barrier(ctx)
        emit_store_resid(ctx, io, "y")
    return nc



def _shard_tokens(x):
    out = []
    for core in range(NCORES):
        b, g = core // 4, core % 4
        a = x[b, g * 1024:(g + 1) * 1024]
        c = x[b, (7 - g) * 1024:(8 - g) * 1024]
        out.append(np.ascontiguousarray(np.concatenate([a, c], axis=0)))
    return out


def _unshard_tokens(parts):
    d = parts[0].shape[-1]
    x = np.empty((2, 8192, d), dtype=parts[0].dtype)
    for core in range(NCORES):
        b, g = core // 4, core % 4
        x[b, g * 1024:(g + 1) * 1024] = parts[core][:1024]
        x[b, (7 - g) * 1024:(8 - g) * 1024] = parts[core][1024:]
    return x


def moe_host_prep(inp):
    gate_upT = np.ascontiguousarray(inp["L0_ffn_gate_up_proj"].transpose(0, 2, 1))
    downT = np.ascontiguousarray(inp["L0_ffn_down_proj"].transpose(0, 2, 1))
    wrT = np.ascontiguousarray(inp["L0_ffn_router_weight"].T)
    br = inp["L0_ffn_router_bias"]
    brbc = np.ascontiguousarray(np.broadcast_to(np.tile(br, NT), (128, NT * NE)))
    gub = inp["L0_ffn_gate_up_proj_bias"]
    bg = np.ascontiguousarray(gub[:, :2048].reshape(NE, 16, 128).transpose(2, 0, 1).reshape(128, 128))
    bu = np.ascontiguousarray(gub[:, 2048:].reshape(NE, 16, 128).transpose(2, 0, 1).reshape(128, 128))
    db = np.ascontiguousarray(inp["L0_ffn_down_proj_bias"])
    return dict(gate_upT=gate_upT, downT=downT, wrT=wrT, brbc=brbc, bg=bg, bu=bu, db=db,
                iden=np.eye(128, dtype=np.float32))



def build_moe_program(debug=False):
    nc = bass.Bass("TRN2", target_bir_lowering=False)
    io = {}
    io["x"] = nc.dram_tensor("x", [T, D], F32, kind="ExternalInput").ap()
    io["gate_upT"] = nc.dram_tensor("gate_upT", [NE, D, 4096], F32, kind="ExternalInput").ap()
    io["downT"] = nc.dram_tensor("downT", [NE, 2048, D], F32, kind="ExternalInput").ap()
    io["wrT"] = nc.dram_tensor("wrT", [D, NE], F32, kind="ExternalInput").ap()
    io["brbc"] = nc.dram_tensor("brbc", [128, NT * NE], F32, kind="ExternalInput").ap()
    io["bg"] = nc.dram_tensor("bg", [128, 128], F32, kind="ExternalInput").ap()
    io["bu"] = nc.dram_tensor("bu", [128, 128], F32, kind="ExternalInput").ap()
    io["db"] = nc.dram_tensor("db", [NE, D], F32, kind="ExternalInput").ap()
    io["iden"] = nc.dram_tensor("iden", [128, 128], F32, kind="ExternalInput").ap()
    io["y"] = nc.dram_tensor("y", [T, D], F32, kind="ExternalOutput").ap()
    if debug:
        io["dbg"] = nc.dram_tensor("dbg", [128, 3072], F32, kind="ExternalOutput").ap()

    with ExitStack() as stack:
        block = stack.enter_context(nc.Block())
        ctx = Ctx(nc, stack, block)
        ctx.debug = debug
        for s in ("PH", "GP", "SC", "VC", "TC", "ST"):
            ctx.sem(s)
        emit_common(ctx, io)
        ctx.lnT = ctx.sb("lnT", [128, ND * T], BF16)
        if debug:
            ctx.dbg = ctx.sb("dbgt", [128, 3072], F32)

            @ctx.block.gpsimd
            def _(gpsimd):
                gpsimd.memset(ctx.dbg[:, :], 0.0)
                ctx.inc("GP", gpsimd.drain())
        emit_load_resid(ctx, io, "x")
        barrier(ctx)
        emit_ln(ctx)
        barrier(ctx)
        emit_moe(ctx, io)
        barrier(ctx)
        emit_store_resid(ctx, io, "y")
        if debug:
            @ctx.block.sync
            def _(sync):
                ctx.inc("ST", sync.dma_start(out=io["dbg"][:, :], in_=ctx.dbg[:, :]), 16)
                sync.wait_ge(ctx.sem("ST"), ctx.n("ST"))
    return nc



def build_fused_program():
    nc = bass.Bass("TRN2", target_bir_lowering=False)
    io = {}
    io["x"] = nc.dram_tensor("x", [T, D], F32, kind="ExternalInput").ap()
    io["gate_upT"] = nc.dram_tensor("gate_upT", [NE, D, 4096], F32, kind="ExternalInput").ap()
    io["downT"] = nc.dram_tensor("downT", [NE, 2048, D], F32, kind="ExternalInput").ap()
    io["wrT"] = nc.dram_tensor("wrT", [D, NE], F32, kind="ExternalInput").ap()
    io["brbc"] = nc.dram_tensor("brbc", [128, NT * NE], F32, kind="ExternalInput").ap()
    io["bg"] = nc.dram_tensor("bg", [128, 128], F32, kind="ExternalInput").ap()
    io["bu"] = nc.dram_tensor("bu", [128, 128], F32, kind="ExternalInput").ap()
    io["db"] = nc.dram_tensor("db", [NE, D], F32, kind="ExternalInput").ap()
    for nm in ("wqT", "wkT", "wvT", "woT"):
        io[nm] = nc.dram_tensor(nm, [D, D], F32, kind="ExternalInput").ap()
    io["qklnp"] = nc.dram_tensor("qklnp", [64, 4], F32, kind="ExternalInput").ap()
    io["kgbc"] = nc.dram_tensor("kgbc", [128, 64], F32, kind="ExternalInput").ap()
    io["kbbc"] = nc.dram_tensor("kbbc", [128, 64], F32, kind="ExternalInput").ap()
    io["msk_lin"] = nc.dram_tensor("msk_lin", [128, 128], F32, kind="ExternalInput").ap()
    io["sel"] = nc.dram_tensor("sel", [64, 16], F32, kind="ExternalInput").ap()
    io["g_loc"] = nc.dram_tensor("g_loc", [64, 1024], F32).ap()
    io["g_all"] = nc.dram_tensor("g_all", [256, 1024], F32).ap()
    io["u1"] = nc.dram_tensor("u1", [D, 256], F32, kind="ExternalInput").ap()
    io["v1"] = nc.dram_tensor("v1", [256, 4096], F32, kind="ExternalInput").ap()
    io["u2"] = nc.dram_tensor("u2", [4096, 256], F32, kind="ExternalInput").ap()
    io["v2"] = nc.dram_tensor("v2", [256, 1024], F32, kind="ExternalInput").ap()
    io["b1t"] = nc.dram_tensor("b1t", [128, 32], F32, kind="ExternalInput").ap()
    io["b2bc"] = nc.dram_tensor("b2bc", [128, 1024], F32, kind="ExternalInput").ap()
    for nm in ("wq1", "wk1", "wv1", "wo1"):
        io[nm] = nc.dram_tensor(nm, [D, D], F32, kind="ExternalInput").ap()
    io["bqk"] = nc.dram_tensor("bqk", [128, 16], F32, kind="ExternalInput").ap()
    for nm in ("bv_bc", "bo_bc", "gm_bc", "bt_bc"):
        io[nm] = nc.dram_tensor(nm, [128, 1024], F32, kind="ExternalInput").ap()
    io["zb"] = nc.dram_tensor("zb", [128, 16], F32, kind="ExternalInput").ap()
    io["mskf"] = nc.dram_tensor("mskf", [128, 2048], F32, kind="ExternalInput").ap()
    io["kv_loc0"] = nc.dram_tensor("kv_loc0", [128, KVWH], BF16).ap()
    io["kv_loc1"] = nc.dram_tensor("kv_loc1", [128, KVWH], BF16).ap()
    io["kv_all0"] = nc.dram_tensor("kv_all0", [512, KVWH], BF16).ap()
    io["kv_all1"] = nc.dram_tensor("kv_all1", [512, KVWH], BF16).ap()
    io["nscr"] = nc.dram_tensor("nscr", [2, 32], F32).ap()
    io["knloc"] = nc.dram_tensor("knloc", [2, 8], F32).ap()
    io["nmax"] = nc.dram_tensor("nmax", [2, 8], F32).ap()
    io["iden"] = nc.dram_tensor("iden", [128, 128], F32, kind="ExternalInput").ap()
    io["y"] = nc.dram_tensor("y", [T, D], F32, kind="ExternalOutput").ap()

    with ExitStack() as stack:
        block = stack.enter_context(nc.Block())
        ctx = Ctx(nc, stack, block)
        ctx.debug = False
        ctx.wbuf_free_tc = 0
        for s in ("PH", "GP", "SC", "VC", "TC", "ST", "CC"):
            ctx.sem(s)
        emit_common(ctx, io)
        emit_load_resid(ctx, io, "x")
        barrier(ctx)

        def with_ln(i, fn, needs_scope=False):
            ln_stk = ExitStack()
            ctx.prefix = f"s{i}_"
            ctx.lnT = ctx.sb("lnT", [128, ND * T], BF16, ln_stk, side="right")
            emit_ln(ctx)
            barrier(ctx)
            if needs_scope:
                fn(ln_stk)
            else:
                fn()
                ln_stk.close()
            barrier(ctx)

        with_ln(0, lambda: emit_moe(ctx, io))
        with_ln(1, lambda: emit_l0attn(ctx, io))
        with_ln(2, lambda: emit_moe(ctx, io))
        with_ln(3, lambda: emit_l1ffn(ctx, io))
        with_ln(4, lambda ln_stk: emit_l1attn(ctx, io, ln_stk), needs_scope=True)
        with_ln(5, lambda: emit_l1ffn(ctx, io))
        ctx.prefix = ""
        emit_store_resid(ctx, io, "y")
    return nc


_PROGRAMS = {}


def _get_programs():
    if not _PROGRAMS:
        _PROGRAMS["fused"] = build_fused_program()
    return _PROGRAMS


def kernel(**inputs):
    progs = _get_programs()
    iden = np.eye(128, dtype=np.float32)

    shared = {}
    shared.update(moe_host_prep(inputs))
    shared.update(l0attn_host_prep_shared(inputs))
    shared.update(l1ffn_host_prep(inputs))
    shared.update(l1attn_host_prep_shared(inputs))
    shared["iden"] = iden
    a0_pc = [l0attn_host_prep(inputs, c) for c in range(NCORES)]
    a1_pc = [l1attn_host_prep(inputs, c) for c in range(NCORES)]

    xs = _shard_tokens(np.ascontiguousarray(inputs["L0_attn_x"]))
    in_maps = [dict(shared, x=xs[c], **a0_pc[c], **a1_pc[c]) for c in range(NCORES)]
    res = run_bass_kernel_spmd(progs["fused"], in_maps, core_ids=list(range(NCORES)))
    ys = [res.results[c]["y"] for c in range(NCORES)]
    return _unshard_tokens(ys).astype(np.float32)


# version 13
# speedup vs baseline: 1.0412x; 1.0003x over previous
import numpy as np
import ml_dtypes
import concourse.bass as bass
import concourse.mybir as mybir
from concourse.bass_utils import run_bass_kernel_spmd
from contextlib import ExitStack

F32 = mybir.dt.float32
BF16 = mybir.dt.bfloat16
AF = mybir.ActivationFunctionType
ALU = mybir.AluOpType
AX = mybir.AxisListType
BFNP = ml_dtypes.bfloat16

NCORES = 8
T = 2048
NT = T // 128
D = 1024
ND = D // 128
NE = 8
EPS = 1e-5
SWI_A = 1.702
WSCALE = 0.5 / SWI_A
C_M1 = -1.0
C_MA = -SWI_A
C_A7 = SWI_A * 7.0

CONST_VALS = [C_M1, C_MA, C_A7, EPS]



class Ctx:
    def __init__(self, nc, stack, block):
        self.nc = nc
        self.stack = stack
        self.block = block
        self.semh = {}
        self.semn = {}

    def sem(self, name):
        if name not in self.semh:
            self.semh[name] = self.stack.enter_context(self.nc.semaphore(name))
            self.semn[name] = 0
        return self.semh[name]

    def dsem(self, name):
        if not hasattr(self, "dma_sems"):
            self.dma_sems = set()
        self.dma_sems.add(name)
        return self.sem(name)

    def inc(self, name, instr, k=1):
        instr.then_inc(self.sem(name), k)
        self.semn[name] += k
        return self.semn[name]

    def n(self, name):
        self.sem(name)
        return self.semn[name]

    prefix = ""

    def sb(self, name, shape, dt, stk=None, side=None):
        name = self.prefix + name
        if side:
            return (stk or self.stack).enter_context(self.nc.sbuf_tensor(name, shape, dt, side=side))
        return (stk or self.stack).enter_context(self.nc.sbuf_tensor(name, shape, dt))

    def ps(self, name, shape, dt=F32, stk=None):
        return (stk or self.stack).enter_context(self.nc.psum_tensor(self.prefix + name, shape, dt))


ENG_NAMES = ("sync", "gpsimd", "scalar", "vector", "tensor")


def barrier(ctx):
    target = ctx.n("PH") + len(ENG_NAMES)
    for en in ENG_NAMES:
        sec = getattr(ctx.block, en)

        def mk(en):
            def _(eng):
                if en == "sync":
                    for s in sorted(getattr(ctx, "dma_sems", ())):
                        eng.wait_ge(ctx.sem(s), ctx.n(s))
                ctx.inc("PH", eng.drain())
                eng.wait_ge(ctx.sem("PH"), target)
            return _
        sec(mk(en))
    ctx.semn["PH"] = target


def cst(ctx, v):
    return ctx.ctile[:, ctx.cidx[v]:ctx.cidx[v] + 1]


def emit_common(ctx, io):
    ctx.cidx = {v: i for i, v in enumerate(CONST_VALS)}
    ctx.ctile = ctx.sb("ctile", [128, len(CONST_VALS)], F32)
    ctx.iden = ctx.sb("sb_iden", [128, 128], F32)
    ctx.idenb = ctx.sb("sb_idenb", [128, 128], BF16)

    @ctx.block.gpsimd
    def _(gpsimd):
        for v, i in ctx.cidx.items():
            gpsimd.memset(ctx.ctile[:, i:i + 1], float(v))
        ctx.inc("GP", gpsimd.drain())
    ctx.gp_consts = ctx.n("GP")

    @ctx.block.sync
    def _(sync):
        ctx.inc("DCM", sync.dma_start(out=ctx.iden[:, :], in_=io["iden"][:, :]), 16)
    ctx.dsem("DCM")

    @ctx.block.gpsimd
    def _(gpsimd):
        ctx.inc("DCMG", gpsimd.dma_start(out=ctx.idenb[:, :], in_=io["iden"][:, :]), 16)
    ctx.dsem("DCMG")


def emit_load_resid(ctx, io, name="x"):
    ctx.resid = ctx.sb("resid", [128, NT * D], F32)
    src = io[name].rearrange("(t p) d -> p t d", p=128)
    dst = ctx.resid[:, :].rearrange("p (t d) -> p t d", t=NT)

    @ctx.block.sync
    def _(sync):
        ctx.inc("DRES", sync.dma_start(out=dst, in_=src), 16)
    ctx.dsem("DRES")


def emit_store_resid(ctx, io, name="y"):
    dst = io[name].rearrange("(t p) d -> p t d", p=128)
    src = ctx.resid[:, :].rearrange("p (t d) -> p t d", t=NT)

    @ctx.block.sync
    def _(sync):
        sync.wait_ge(ctx.sem("PH"), ctx.n("PH"))
        ctx.inc("ST", sync.dma_start(out=dst, in_=src), 16)
        sync.wait_ge(ctx.sem("ST"), ctx.n("ST"))



def emit_ln(ctx):
    with ExitStack() as stk:
        sq = ctx.sb("ln_sq", [128, D], F32, stk)
        S1 = ctx.sb("ln_s1", [128, NT], F32, stk)
        S2 = ctx.sb("ln_s2", [128, NT], F32, stk)
        MU = ctx.sb("ln_mu", [128, NT], F32, stk)
        T1 = ctx.sb("ln_t1", [128, NT], F32, stk)
        RS = ctx.sb("ln_rs", [128, NT], F32, stk)
        NMU = ctx.sb("ln_nmu", [128, NT], F32, stk)
        lnmaj = ctx.sb("ln_maj", [128, 4 * D], BF16, stk)
        psT = [ctx.ps(f"ln_psT{i}", [128, 1024], BF16, stk) for i in range(2)]

        @ctx.block.scalar
        def _(scalar):
            scalar.wait_ge(ctx.sem("GP"), ctx.gp_consts)
            for t in range(NT):
                scalar.activation(out=sq[:, :], in_=ctx.resid[:, t * D:(t + 1) * D],
                                  func=AF.Square, accum_out=S2[:, t:t + 1])
                scalar.drain()
            ctx.inc("SC", scalar.drain())
        sc_stats = ctx.n("SC")

        @ctx.block.vector
        def _(vector):
            for t in range(NT):
                vector.reduce_sum(S1[:, t:t + 1], ctx.resid[:, t * D:(t + 1) * D], axis=AX.X)
            vector.drain()
            vector.tensor_scalar_mul(MU[:, :], S1[:, :], 1.0 / D)
            vector.wait_ge(ctx.sem("SC"), sc_stats)
            vector.tensor_scalar_mul(S2[:, :], S2[:, :], 1.0 / D)
            vector.drain()
            vector.tensor_mul(T1[:, :], MU[:, :], MU[:, :])
            vector.drain()
            vector.tensor_sub(S2[:, :], S2[:, :], T1[:, :])
            ctx.inc("VC", vector.drain())
        vc_var = ctx.n("VC")

        @ctx.block.scalar
        def _(scalar):
            scalar.wait_ge(ctx.sem("VC"), vc_var)
            scalar.activation(out=RS[:, :], in_=S2[:, :], func=AF.Sqrt, bias=cst(ctx, EPS))
            ctx.inc("SC", scalar.drain())
        sc_std = ctx.n("SC")

        @ctx.block.vector
        def _(vector):
            vector.wait_ge(ctx.sem("SC"), sc_std)
            vector.reciprocal(T1[:, :], RS[:, :])
            vector.drain()
            vector.scalar_tensor_tensor(NMU[:, :], MU[:, :], -1.0, T1[:, :], ALU.mult, ALU.mult)
            ctx.inc("VC", vector.drain())
        vc_rstd = ctx.n("VC")

        sc_base = ctx.n("SC")
        tc_base = ctx.n("TC")
        vc_base = ctx.n("VC")
        for g4 in range(4):
            @ctx.block.scalar
            def _(scalar, g4=g4):
                if g4 == 0:
                    scalar.wait_ge(ctx.sem("VC"), vc_rstd)
                if g4 >= 1:
                    scalar.wait_ge(ctx.sem("TC"), tc_base + g4 * 8)
                for tl in range(4):
                    t = g4 * 4 + tl
                    scalar.activation(out=lnmaj[:, tl * D:(tl + 1) * D],
                                      in_=ctx.resid[:, t * D:(t + 1) * D],
                                      func=AF.Identity, scale=T1[:, t:t + 1], bias=NMU[:, t:t + 1])
                ctx.inc("SC", scalar.drain())

            @ctx.block.tensor
            def _(tensor, g4=g4):
                tensor.wait_ge(ctx.sem("SC"), sc_base + g4 + 1)
                for r in range(ND):
                    par = r % 2
                    if g4 * 8 + r >= 2:
                        tensor.wait_ge(ctx.sem("VC"), vc_base + g4 * 8 + r - 1)
                    last = None
                    for tl in range(4):
                        last = tensor.transpose(psT[par][:, tl * 128:(tl + 1) * 128],
                                                lnmaj[:, tl * D + r * 128: tl * D + (r + 1) * 128],
                                                ctx.idenb[:, :])
                    ctx.inc("TC", last)

            @ctx.block.vector
            def _(vector, g4=g4):
                for r in range(ND):
                    par = r % 2
                    vector.wait_ge(ctx.sem("TC"), tc_base + g4 * 8 + r + 1)
                    ctx.inc("VC", vector.tensor_copy(
                        ctx.lnT[:, r * T + g4 * 512: r * T + (g4 + 1) * 512], psT[par][:, 0:512]))

        @ctx.block.vector
        def _(vector):
            if getattr(ctx, "debug", False):
                vector.tensor_copy(ctx.dbg[:, 256:272], MU[:, :])
                vector.tensor_copy(ctx.dbg[:, 272:288], S2[:, :])
                vector.tensor_copy(ctx.dbg[:, 288:304], T1[:, :])
                vector.tensor_copy(ctx.dbg[:, 1024:3072], ctx.lnT[:, T:2 * T])
            ctx.inc("VC", vector.drain())
    ctx.lnT_ready = ctx.n("VC")



def emit_moe(ctx, io):
    with ExitStack() as stk:
        wrT = ctx.sb("m_wrT", [128, ND * NE], BF16, stk)
        brbc = ctx.sb("m_brbc", [128, NT * NE], F32, stk)
        bg_raw = ctx.sb("m_bgraw", [128, 128], F32, stk)
        bu_raw = ctx.sb("m_buraw", [128, 128], F32, stk)
        GB7 = ctx.sb("m_gb7", [128, 128], F32, stk)
        BU1 = ctx.sb("m_bu1", [128, 128], F32, stk)
        LG = ctx.sb("m_lg", [128, NT * NE], F32, stk)
        WK = ctx.sb("m_wk", [128, NT * NE], F32, stk)
        EQ = ctx.sb("m_eq", [128, NT * NE], F32, stk)
        KP = ctx.sb("m_kp", [128, NT * NE], F32, stk)
        EX = ctx.sb("m_ex", [128, NT * NE], F32, stk)
        Ma = ctx.sb("m_ma", [128, NT], F32, stk)
        Mt = ctx.sb("m_mt", [128, NT], F32, stk)
        M4 = ctx.sb("m_m4", [128, NT], F32, stk)
        NM1 = ctx.sb("m_nm1", [128, NT], F32, stk)
        SM = ctx.sb("m_sm", [128, NT], F32, stk)
        RC = ctx.sb("m_rc", [128, NT], F32, stk)
        WT = ctx.sb("m_wt", [128, NT * NE], F32, stk)
        wg = [ctx.sb(f"m_wg{i}", [128, ND * 128], BF16, stk) for i in range(2)]
        wu = [ctx.sb(f"m_wu{i}", [128, ND * 128], BF16, stk) for i in range(2)]
        wd = ctx.sb("m_wd", [128, 8 * D], BF16, stk)
        dbf = [ctx.sb(f"m_dbf{i}", [1, D], F32, stk) for i in range(2)]
        dbb = [ctx.sb(f"m_dbb{i}", [1, D], BF16, stk) for i in range(2)]
        onesb = ctx.sb("m_ones", [1, 128], BF16, stk)
        rg = ctx.sb("m_rg", [128, T], F32, stk)
        Ssc = ctx.sb("m_S", [128, T], F32, stk)
        u1 = ctx.sb("m_u1", [128, T], F32, stk)
        act = ctx.sb("m_act", [128, 8 * T], BF16, stk)
        psG = [ctx.ps(f"m_psG{i}", [128, 1024], F32, stk) for i in range(2)]
        psU = [ctx.ps(f"m_psU{i}", [128, 1024], F32, stk) for i in range(2)]

        lnr = ctx.lnT_ready

        @ctx.block.sync
        def _(sync):
            ctx.inc("DMS", sync.dma_start(out=brbc[:, :], in_=io["brbc"][:, :]), 16)
            ctx.inc("DMS", sync.dma_start(out=bg_raw[:, :], in_=io["bg"][:, :]), 16)
            ctx.inc("DMS", sync.dma_start(out=bu_raw[:, :], in_=io["bu"][:, :]), 16)
        ctx.dsem("DMS")
        dw_static = ctx.n("DMS")

        @ctx.block.gpsimd
        def _(gpsimd):
            gpsimd.memset(onesb[:, :], 1.0)
            ctx.inc("GP", gpsimd.drain())
            ctx.inc("DMSG", gpsimd.dma_start(
                out=wrT[:, :].rearrange("p (r e) -> p r e", r=ND),
                in_=io["wrT"].rearrange("(r p) e -> p r e", p=128)), 16)
        ctx.dsem("DMSG")
        gp_ones = ctx.n("GP")
        dg_wrT = ctx.n("DMSG")

        @ctx.block.vector
        def _(vector):
            vector.wait_ge(ctx.sem("DMS"), dw_static)
            vector.tensor_scalar(GB7[:, :], bg_raw[:, :], -1.0, 7.0, ALU.mult, ALU.add)
            vector.tensor_scalar(BU1[:, :], bu_raw[:, :], 1.0, None, ALU.add)
            ctx.inc("VC", vector.drain())
        vc_bias = ctx.n("VC")

        @ctx.block.tensor
        def _(tensor):
            tensor.wait_ge(ctx.sem("VC"), lnr)
            tensor.wait_ge(ctx.sem("DMSG"), dg_wrT)
            last = None
            for t in range(NT):
                for r in range(ND):
                    last = tensor.matmul(psG[0][:, t * 8:(t + 1) * 8],
                                         ctx.lnT[:, r * T + t * 128: r * T + (t + 1) * 128],
                                         wrT[:, r * 8:(r + 1) * 8],
                                         start=(r == 0), stop=(r == ND - 1))
            ctx.inc("TC", last)
        tc_router = ctx.n("TC")

        @ctx.block.vector
        def _(vector):
            vector.wait_ge(ctx.sem("TC"), tc_router)
            vector.wait_ge(ctx.sem("VC"), vc_bias)
            if getattr(ctx, "debug", False):
                vector.tensor_copy(ctx.dbg[:, 0:128], psG[0][:, 0:128])
                vector.drain()
            vector.tensor_add(LG[:, :], psG[0][:, 0:NT * NE], brbc[:, :])
            vector.drain()
            vector.tensor_copy(WK[:, :], LG[:, :])
            for t in range(NT):
                vector.reduce_max(Ma[:, t:t + 1], LG[:, t * 8:(t + 1) * 8], axis=AX.X)
            vector.drain()
            for k in range(3):
                msrc = Ma if k == 0 else Mt
                for t in range(NT):
                    vector.tensor_scalar(EQ[:, t * 8:(t + 1) * 8], WK[:, t * 8:(t + 1) * 8],
                                         msrc[:, t:t + 1], None, ALU.is_equal)
                vector.drain()
                vector.scalar_tensor_tensor(WK[:, :], EQ[:, :], -1e30, WK[:, :], ALU.mult, ALU.add)
                vector.drain()
                mdst = M4 if k == 2 else Mt
                for t in range(NT):
                    vector.reduce_max(mdst[:, t:t + 1], WK[:, t * 8:(t + 1) * 8], axis=AX.X)
                vector.drain()
            for t in range(NT):
                vector.tensor_scalar(KP[:, t * 8:(t + 1) * 8], LG[:, t * 8:(t + 1) * 8],
                                     M4[:, t:t + 1], None, ALU.is_ge)
            vector.tensor_scalar_mul(NM1[:, :], Ma[:, :], -1.0)
            ctx.inc("VC", vector.drain())
        vc_keep = ctx.n("VC")

        @ctx.block.scalar
        def _(scalar):
            scalar.wait_ge(ctx.sem("VC"), vc_keep)
            for t in range(NT):
                scalar.activation(out=EX[:, t * 8:(t + 1) * 8], in_=LG[:, t * 8:(t + 1) * 8],
                                  func=AF.Exp, bias=NM1[:, t:t + 1])
            ctx.inc("SC", scalar.drain())
        sc_exp = ctx.n("SC")

        @ctx.block.vector
        def _(vector):
            vector.wait_ge(ctx.sem("SC"), sc_exp)
            vector.tensor_mul(EX[:, :], EX[:, :], KP[:, :])
            vector.drain()
            for t in range(NT):
                vector.reduce_sum(SM[:, t:t + 1], EX[:, t * 8:(t + 1) * 8], axis=AX.X)
            vector.drain()
            vector.reciprocal(RC[:, :], SM[:, :])
            vector.drain()
            for t in range(NT):
                vector.tensor_scalar(WT[:, t * 8:(t + 1) * 8], EX[:, t * 8:(t + 1) * 8],
                                     RC[:, t:t + 1], WSCALE, ALU.mult, ALU.mult)
            if getattr(ctx, "debug", False):
                vector.tensor_copy(ctx.dbg[:, 128:256], LG[:, :])
                vector.tensor_copy(ctx.dbg[:, 336:464], WT[:, :])
                vector.tensor_copy(ctx.dbg[:, 304:320], M4[:, :])
                vector.tensor_copy(ctx.dbg[:, 320:336], SM[:, :])
            ctx.inc("VC", vector.drain())
        vc_wt = ctx.n("VC")

        slot_par = 0
        wgbuf_free_tc = [0, 0]
        ps_slot_free = [{"sc": 0, "vc": 0}, {"sc": 0, "vc": 0}]
        ps_pool_free_vc = vc_wt
        wd_free_tc = 0
        db_free_tc = [0, 0]
        gelems = io["gate_upT"]
        delems = io["downT"]

        first_slot = True
        for e in range(NE):
            for half in range(2):
                dg_wgwu = []
                for i in range(2):
                    ctx.dsem(f"DWG{i}")
                    ctx.dsem(f"DWU{i}")
                    ctx.dsem(f"DDB{i}")
                ctx.dsem("DWD")

                tc_base = ctx.n("TC")

                @ctx.block.gpsimd
                def _(gpsimd, e=e, half=half, tc_base=tc_base):
                    if half == 0:
                        par = e % 2
                        if db_free_tc[par]:
                            gpsimd.wait_ge(ctx.sem("TC"), db_free_tc[par])
                        ctx.inc(f"DDB{par}", gpsimd.dma_start(out=dbf[par][:, :], in_=io["db"][e:e + 1, :]), 16)
                    for j in range(8):
                        jj = half * 8 + j
                        par = j % 2
                        if j >= 2:
                            gpsimd.wait_ge(ctx.sem("TC"), tc_base + 4 * (j - 2) + 4)
                        elif wgbuf_free_tc[par]:
                            gpsimd.wait_ge(ctx.sem("TC"), wgbuf_free_tc[par])
                        ctx.inc(f"DWG{par}", gpsimd.dma_start(
                            out=wg[par][:, :].rearrange("p (r f) -> p r f", r=ND),
                            in_=gelems[e, :, jj * 128:(jj + 1) * 128]
                                .rearrange("(r p) f -> p r f", p=128)), 16)
                        ctx.inc(f"DWU{par}", gpsimd.dma_start(
                            out=wu[par][:, :].rearrange("p (r f) -> p r f", r=ND),
                            in_=gelems[e, :, 2048 + jj * 128: 2048 + (jj + 1) * 128]
                                .rearrange("(r p) f -> p r f", p=128)), 16)
                        dg_wgwu.append((ctx.n(f"DWG{par}"), ctx.n(f"DWU{par}")))
                    if wd_free_tc:
                        gpsimd.wait_ge(ctx.sem("TC"), wd_free_tc)
                    for j in range(8):
                        jj = half * 8 + j
                        ctx.inc("DWD", gpsimd.dma_start(
                            out=wd[:, j * D:(j + 1) * D],
                            in_=delems[e, jj * 128:(jj + 1) * 128, :]), 16)
                dg_wd = ctx.n("DWD")
                dg_db = ctx.n(f"DDB{e % 2}")

                for j in range(8):
                    slotinfo = []
                    for th in range(2):
                        par = slot_par % 2
                        slot_par += 1
                        slot_pool_wait = ps_pool_free_vc if (j == 0 and th == 0) else 0
                        slot_sc_wait = ps_slot_free[par]["sc"]
                        slot_vc_wait = ps_slot_free[par]["vc"]

                        @ctx.block.tensor
                        def _(tensor, j=j, th=th, par=par, dgw=dg_wgwu[j],
                              spw=slot_pool_wait, ssw=slot_sc_wait, svw=slot_vc_wait):
                            tensor.wait_ge(ctx.sem(f"DWG{j % 2}"), dgw[0])
                            tensor.wait_ge(ctx.sem(f"DWU{j % 2}"), dgw[1])
                            if spw:
                                tensor.wait_ge(ctx.sem("VC"), spw)
                            if ssw:
                                tensor.wait_ge(ctx.sem("SC"), ssw)
                            if svw:
                                tensor.wait_ge(ctx.sem("VC"), svw)
                            last = None
                            for c in range(2):
                                tok0 = th * 1024 + c * 512
                                for r in range(ND):
                                    last = tensor.matmul(
                                        psG[par][:, c * 512:(c + 1) * 512],
                                        wg[j % 2][:, r * 128:(r + 1) * 128],
                                        ctx.lnT[:, r * T + tok0: r * T + tok0 + 512],
                                        start=(r == 0), stop=(r == ND - 1))
                            ctx.inc("TC", last)
                            last = None
                            for c in range(2):
                                tok0 = th * 1024 + c * 512
                                for r in range(ND):
                                    last = tensor.matmul(
                                        psU[par][:, c * 512:(c + 1) * 512],
                                        wu[j % 2][:, r * 128:(r + 1) * 128],
                                        ctx.lnT[:, r * T + tok0: r * T + tok0 + 512],
                                        start=(r == 0), stop=(r == ND - 1))
                            ctx.inc("TC", last)
                        slotinfo.append((par, ctx.n("TC") - 1, ctx.n("TC")))
                        if th == 1:
                            wgbuf_free_tc[j % 2] = ctx.n("TC")

                    jj = half * 8 + j
                    bcol = e * 16 + jj
                    sc_of_th = []

                    @ctx.block.scalar
                    def _(scalar, bcol=bcol, slotinfo=list(slotinfo), fs=first_slot):
                        if fs:
                            scalar.wait_ge(ctx.sem("VC"), vc_bias)
                        for th, (par, tcg, tcu) in enumerate(slotinfo):
                            scalar.wait_ge(ctx.sem("TC"), tcg)
                            scalar.activation(out=rg[:, th * 1024:(th + 1) * 1024],
                                              in_=psG[par][:, :],
                                              func=AF.Relu, scale=cst(ctx, C_M1),
                                              bias=GB7[:, bcol:bcol + 1])
                            scalar.drain()
                            scalar.activation(out=Ssc[:, th * 1024:(th + 1) * 1024],
                                              in_=rg[:, th * 1024:(th + 1) * 1024],
                                              func=AF.Silu, scale=cst(ctx, C_MA),
                                              bias=cst(ctx, C_A7))
                            ctx.inc("SC", scalar.drain())
                            sc_of_th.append(ctx.n("SC"))
                    first_slot = False

                    for th in range(2):
                        par, tcg, tcu = slotinfo[th]
                        ps_slot_free[par]["sc"] = sc_of_th[th]

                        @ctx.block.vector
                        def _(vector, par=par, th=th, j=j, bcol=bcol,
                              tcu=tcu, scs=sc_of_th[th]):
                            vector.wait_ge(ctx.sem("TC"), tcu)
                            vector.tensor_scalar(u1[:, th * 1024:(th + 1) * 1024],
                                                 psU[par][:, :],
                                                 BU1[:, bcol:bcol + 1], 8.0, ALU.add, ALU.min)
                            vector.drain()
                            vector.wait_ge(ctx.sem("SC"), scs)
                            vector.scalar_tensor_tensor(
                                act[:, j * T + th * 1024: j * T + (th + 1) * 1024],
                                u1[:, th * 1024:(th + 1) * 1024], -6.0,
                                Ssc[:, th * 1024:(th + 1) * 1024], ALU.max, ALU.mult)
                            ctx.inc("VC", vector.drain())
                        ps_slot_free[par]["vc"] = ctx.n("VC")

                if half == 0:
                    @ctx.block.vector
                    def _(vector, e=e, dg_db=dg_db):
                        par = e % 2
                        vector.wait_ge(ctx.sem(f"DDB{e % 2}"), dg_db)
                        vector.tensor_scalar_mul(dbf[par][:, :], dbf[par][:, :], SWI_A)
                        vector.drain()
                        vector.tensor_copy(dbb[par][:, :], dbf[par][:, :])
                        ctx.inc("VC", vector.drain())
                vc_db = ctx.n("VC")

                vc_act_done = ctx.n("VC")
                vc_evac_base = ctx.n("VC")
                tc_down = []

                @ctx.block.tensor
                def _(tensor, e=e, half=half, vc_act_done=vc_act_done,
                      dg_wd=dg_wd, vc_db=vc_db, base=vc_evac_base):
                    tensor.wait_ge(ctx.sem("VC"), vc_act_done)
                    tensor.wait_ge(ctx.sem("DWD"), dg_wd)
                    if half == 0:
                        tensor.wait_ge(ctx.sem("VC"), vc_db)
                    for c in range(NT):
                        par = c % 2
                        if c >= 2:
                            tensor.wait_ge(ctx.sem("VC"), base + (c - 2) + 1)
                        psD = psG[par]
                        first = True
                        if half == 0:
                            tensor.matmul(psD[:, 0:512], onesb[0:1, :], dbb[e % 2][0:1, 0:512],
                                          start=True, stop=False)
                            tensor.matmul(psD[:, 512:1024], onesb[0:1, :], dbb[e % 2][0:1, 512:1024],
                                          start=True, stop=False)
                            first = False
                        last = None
                        for j in range(8):
                            a = act[:, j * T + c * 128: j * T + (c + 1) * 128]
                            last = tensor.matmul(psD[:, 0:512], a, wd[:, j * D: j * D + 512],
                                                 start=first, stop=(j == 7))
                            last = tensor.matmul(psD[:, 512:1024], a, wd[:, j * D + 512: (j + 1) * D],
                                                 start=first, stop=(j == 7))
                            first = False
                        ctx.inc("TC", last)
                        tc_down.append(ctx.n("TC"))
                tc_down_end = ctx.n("TC")
                wd_free_tc = tc_down_end
                if half == 0:
                    db_free_tc[e % 2] = tc_down_end

                @ctx.block.vector
                def _(vector, e=e, tc_down=list(tc_down)):
                    vector.drain()
                    for c in range(NT):
                        par = c % 2
                        vector.wait_ge(ctx.sem("TC"), tc_down[c])
                        ctx.inc("VC", vector.scalar_tensor_tensor(
                            ctx.resid[:, c * D:(c + 1) * D],
                            psG[par][:, :], WT[:, c * 8 + e: c * 8 + e + 1],
                            ctx.resid[:, c * D:(c + 1) * D], ALU.mult, ALU.add))
                assert ctx.n("VC") == vc_evac_base + NT
                ps_pool_free_vc = ctx.n("VC")
                ps_slot_free = [{"sc": 0, "vc": 0}, {"sc": 0, "vc": 0}]

        @ctx.block.vector
        def _(vector):
            ctx.inc("VC", vector.drain())



def emit_l1ffn(ctx, io):
    with ExitStack() as stk:
        u1 = ctx.sb("f_u1", [128, ND * 256], BF16, stk)
        v1 = ctx.sb("f_v1", [128, 2 * 4096], BF16, stk)
        u2 = ctx.sb("f_u2", [128, 32 * 256], BF16, stk)
        v2 = ctx.sb("f_v2", [128, 2 * 1024], BF16, stk)
        b1s = ctx.sb("f_b1", [128, 32], F32, stk)
        b2bc = ctx.sb("f_b2bc", [128, 1024], F32, stk)
        Pbf = ctx.sb("f_P", [128, 1024], BF16, stk)
        Qbf = ctx.sb("f_Q", [128, 1024], BF16, stk)
        Hbf = ctx.sb("f_H", [128, 32 * 512], BF16, stk)
        osc = ctx.sb("f_osc", [128, 1024], F32, stk)
        psP = [ctx.ps(f"f_psP{i}", [128, 512], F32, stk) for i in range(2)]
        psH = [ctx.ps(f"f_psH{i}", [128, 512], F32, stk) for i in range(2)]
        psQ = [ctx.ps(f"f_psQ{i}", [128, 512], F32, stk) for i in range(2)]
        psO = ctx.ps("f_psO", [128, 1024], F32, stk)

        lnr = ctx.lnT_ready

        @ctx.block.gpsimd
        def _(gpsimd):
            ctx.inc("DFW", gpsimd.dma_start(
                out=u1[:, :].rearrange("p (rd r) -> p rd r", rd=ND),
                in_=io["u1"].rearrange("(rd p) r -> p rd r", p=128)), 16)
            ctx.inc("DFW", gpsimd.dma_start(
                out=v1[:, :].rearrange("p (rr f) -> p rr f", rr=2),
                in_=io["v1"].rearrange("(rr p) f -> p rr f", p=128)), 16)
            ctx.inc("DFW", gpsimd.dma_start(
                out=u2[:, :].rearrange("p (ft r) -> p ft r", ft=32),
                in_=io["u2"].rearrange("(ft p) r -> p ft r", p=128)), 16)
            ctx.inc("DFW", gpsimd.dma_start(
                out=v2[:, :].rearrange("p (rr d) -> p rr d", rr=2),
                in_=io["v2"].rearrange("(rr p) d -> p rr d", p=128)), 16)
        ctx.dsem("DFW")
        dg_w = ctx.n("DFW")

        @ctx.block.sync
        def _(sync):
            ctx.inc("DFS", sync.dma_start(out=b1s[:, :], in_=io["b1t"][:, :]), 16)
            ctx.inc("DFS", sync.dma_start(out=b2bc[:, :], in_=io["b2bc"][:, :]), 16)
        ctx.dsem("DFS")
        dw_b = ctx.n("DFS")

        for tg in range(4):
            @ctx.block.tensor
            def _(tensor, tg=tg, vce=ctx.n("VC")):
                if tg == 0:
                    tensor.wait_ge(ctx.sem("VC"), lnr)
                    tensor.wait_ge(ctx.sem("DFW"), dg_w)
                else:
                    tensor.wait_ge(ctx.sem("VC"), vce)
                for rr in range(2):
                    last = None
                    for rd in range(ND):
                        last = tensor.matmul(
                            psP[rr][:, :], u1[:, rd * 256 + rr * 128: rd * 256 + (rr + 1) * 128],
                            ctx.lnT[:, rd * T + tg * 512:(rd * T + (tg + 1) * 512)],
                            start=(rd == 0), stop=(rd == ND - 1))
                    ctx.inc("TC", last)
            tc_p = ctx.n("TC")

            @ctx.block.vector
            def _(vector, tc_p=tc_p):
                vector.wait_ge(ctx.sem("TC"), tc_p - 1)
                vector.tensor_copy(Pbf[:, 0:512], psP[0][:, :])
                vector.wait_ge(ctx.sem("TC"), tc_p)
                vector.tensor_copy(Pbf[:, 512:1024], psP[1][:, :])
                ctx.inc("VC", vector.drain())
            vc_p = ctx.n("VC")

            tc_h0 = ctx.n("TC")
            sc_h0 = ctx.n("SC")

            @ctx.block.tensor
            def _(tensor, vc_p=vc_p, tc_h0=tc_h0, sc_h0=sc_h0):
                tensor.wait_ge(ctx.sem("VC"), vc_p)
                for ft in range(32):
                    par = ft % 2
                    if ft >= 2:
                        tensor.wait_ge(ctx.sem("SC"), sc_h0 + ft - 1)
                    last = None
                    for rr in range(2):
                        last = tensor.matmul(psH[par][:, :],
                                             v1[:, rr * 4096 + ft * 128: rr * 4096 + (ft + 1) * 128],
                                             Pbf[:, rr * 512:(rr + 1) * 512],
                                             start=(rr == 0), stop=(rr == 1))
                    ctx.inc("TC", last)

            @ctx.block.scalar
            def _(scalar, tg=tg, tc_h0=tc_h0, dw_b=dw_b):
                if tg == 0:
                    scalar.wait_ge(ctx.sem("DFS"), dw_b)
                for ft in range(32):
                    par = ft % 2
                    scalar.wait_ge(ctx.sem("TC"), tc_h0 + ft + 1)
                    scalar.activation(out=Hbf[:, ft * 512:(ft + 1) * 512], in_=psH[par][:, :],
                                      func=AF.Gelu_apprx_tanh, bias=b1s[:, ft:ft + 1])
                    ctx.inc("SC", scalar.drain())
            sc_h = ctx.n("SC")

            @ctx.block.tensor
            def _(tensor, sc_h=sc_h):
                tensor.wait_ge(ctx.sem("SC"), sc_h)
                for rr in range(2):
                    last = None
                    for ft in range(32):
                        last = tensor.matmul(
                            psQ[rr][:, :], u2[:, ft * 256 + rr * 128: ft * 256 + (rr + 1) * 128],
                            Hbf[:, ft * 512:(ft + 1) * 512],
                            start=(ft == 0), stop=(ft == 31))
                    ctx.inc("TC", last)
            tc_q = ctx.n("TC")

            @ctx.block.vector
            def _(vector, tc_q=tc_q):
                vector.wait_ge(ctx.sem("TC"), tc_q - 1)
                vector.tensor_copy(Qbf[:, 0:512], psQ[0][:, :])
                vector.wait_ge(ctx.sem("TC"), tc_q)
                vector.tensor_copy(Qbf[:, 512:1024], psQ[1][:, :])
                ctx.inc("VC", vector.drain())
            vc_q = ctx.n("VC")

            vc_o0 = ctx.n("VC")
            for tt in range(4):
                @ctx.block.tensor
                def _(tensor, tt=tt, vc_q=vc_q, vc_o0=vc_o0):
                    if tt == 0:
                        tensor.wait_ge(ctx.sem("VC"), vc_q)
                    else:
                        tensor.wait_ge(ctx.sem("VC"), vc_o0 + 2 * tt - 1)
                    last = None
                    for dh in range(2):
                        for rr in range(2):
                            last = tensor.matmul(
                                psO[:, dh * 512:(dh + 1) * 512],
                                Qbf[:, rr * 512 + tt * 128: rr * 512 + (tt + 1) * 128],
                                v2[:, rr * 1024 + dh * 512: rr * 1024 + (dh + 1) * 512],
                                start=(rr == 0), stop=(rr == 1))
                    ctx.inc("TC", last)
                tc_o = ctx.n("TC")

                @ctx.block.vector
                def _(vector, tg=tg, tt=tt, tc_o=tc_o):
                    vector.wait_ge(ctx.sem("TC"), tc_o)
                    ctx.inc("VC", vector.tensor_add(osc[:, :], psO[:, :], b2bc[:, :]))
                    vector.drain()
                    t = tg * 4 + tt
                    vector.scalar_tensor_tensor(
                        ctx.resid[:, t * D:(t + 1) * D], osc[:, :], 0.5,
                        ctx.resid[:, t * D:(t + 1) * D], ALU.mult, ALU.add)
                    ctx.inc("VC", vector.drain())

        @ctx.block.vector
        def _(vector):
            ctx.inc("VC", vector.drain())


def l1ffn_host_prep(inp):
    b1 = inp["L1_ffn_b1"]
    b2 = inp["L1_ffn_b2"]
    return dict(
        u1=np.ascontiguousarray(inp["L1_ffn_U1"]),
        v1=np.ascontiguousarray(inp["L1_ffn_V1"]),
        u2=np.ascontiguousarray(inp["L1_ffn_U2"]),
        v2=np.ascontiguousarray(inp["L1_ffn_V2"]),
        b1t=np.ascontiguousarray(b1.reshape(32, 128).T),
        b2bc=np.ascontiguousarray(np.broadcast_to(b2, (128, 1024))),
    )


def build_l1ffn_program():
    nc = bass.Bass("TRN2", target_bir_lowering=False)
    io = {}
    io["x"] = nc.dram_tensor("x", [T, D], F32, kind="ExternalInput").ap()
    io["u1"] = nc.dram_tensor("u1", [D, 256], F32, kind="ExternalInput").ap()
    io["v1"] = nc.dram_tensor("v1", [256, 4096], F32, kind="ExternalInput").ap()
    io["u2"] = nc.dram_tensor("u2", [4096, 256], F32, kind="ExternalInput").ap()
    io["v2"] = nc.dram_tensor("v2", [256, 1024], F32, kind="ExternalInput").ap()
    io["b1t"] = nc.dram_tensor("b1t", [128, 32], F32, kind="ExternalInput").ap()
    io["b2bc"] = nc.dram_tensor("b2bc", [128, 1024], F32, kind="ExternalInput").ap()
    io["iden"] = nc.dram_tensor("iden", [128, 128], F32, kind="ExternalInput").ap()
    io["y"] = nc.dram_tensor("y", [T, D], F32, kind="ExternalOutput").ap()

    with ExitStack() as stack:
        block = stack.enter_context(nc.Block())
        ctx = Ctx(nc, stack, block)
        ctx.debug = False
        for s in ("PH", "GP", "SC", "VC", "TC", "ST"):
            ctx.sem(s)
        emit_common(ctx, io)
        ctx.lnT = ctx.sb("lnT", [128, ND * T], BF16)
        emit_load_resid(ctx, io, "x")
        barrier(ctx)
        emit_ln(ctx)
        barrier(ctx)
        emit_l1ffn(ctx, io)
        barrier(ctx)
        emit_store_resid(ctx, io, "y")
    return nc



GROUPS = [[0, 1, 2, 3], [4, 5, 6, 7]]


def emit_l0attn(ctx, io):
    with ExitStack() as stk:
        msk = ctx.sb("a_msk", [128, 128], F32, stk)
        lnp = ctx.sb("a_lnp", [64, 4], F32, stk)
        kgbc = ctx.sb("a_kgbc", [128, 64], F32, stk)
        kbbc = ctx.sb("a_kbbc", [128, 64], F32, stk)
        sel_sb = ctx.sb("a_sel", [64, 16], F32, stk)
        wbuf = ctx.sb("a_wbuf", [128, ND * 512], BF16, stk)
        wosb = ctx.sb("a_wosb", [128, 4 * 1024], BF16, stk)
        oTh = ctx.sb("a_oT", [128, 4 * 2048], BF16, stk)
        oS = ctx.sb("a_oS", [64, 2048], BF16, stk)
        qn = ctx.sb("a_qn", [128, 16 * 512], BF16, stk)
        kbf = ctx.sb("a_kbf", [128, 16 * 512], BF16, stk)
        vbf = ctx.sb("a_vbf", [128, 16 * 512], BF16, stk)
        qT_h = ctx.sb("a_qTh", [64, 2048], BF16, stk)
        kT_h = ctx.sb("a_kTh", [64, 2048], BF16, stk)
        ST1 = [ctx.sb(f"a_st{i}", [128, 128], BF16, stk) for i in range(2)]
        Gl = ctx.sb("a_gl", [64, 8 * 64], BF16, stk)
        Pb = [ctx.sb(f"a_pb{i}", [64, 64], BF16, stk) for i in range(2)]
        G_sb = ctx.sb("a_G", [64, 1024], F32, stk)
        gall = ctx.sb("a_gall", [64, 1024], F32, stk)
        Pfx = G_sb
        sqs = ctx.sb("a_sqs", [128, 512], BF16, stk)
        nscr = ctx.sb("a_nscr", [128, 512], F32, stk)
        tscr = ctx.sb("a_tscr", [128, 64], F32, stk)
        QS1 = ctx.sb("a_qs1", [128, 16], F32, stk)
        QS2 = ctx.sb("a_qs2", [128, 16], F32, stk)
        QMU = ctx.sb("a_qmu", [128, 16], F32, stk)
        QT1 = ctx.sb("a_qt1", [128, 16], F32, stk)
        QNM = ctx.sb("a_qnm", [128, 16], F32, stk)
        prj = [ctx.ps(f"a_prj{i}", [128, 512], F32, stk) for i in range(2)]
        trp = [ctx.ps(f"a_trp{i}", [128, 1024], BF16, stk) for i in range(2)]
        scp = [ctx.ps(f"a_scp{i}", [128, 512], F32, stk) for i in range(2)]
        opp = [ctx.ps(f"a_opp{i}", [128, 512], F32, stk) for i in range(2)]

        lnr = ctx.lnT_ready

        @ctx.block.sync
        def _(sync):
            ctx.inc("DAS", sync.dma_start(out=msk[:, :], in_=io["msk_lin"][:, :]), 16)
            ctx.inc("DAS", sync.dma_start(out=lnp[:, :], in_=io["qklnp"][:, :]), 16)
            ctx.inc("DAS", sync.dma_start(out=kgbc[:, :], in_=io["kgbc"][:, :]), 16)
            ctx.inc("DAS", sync.dma_start(out=kbbc[:, :], in_=io["kbbc"][:, :]), 16)
            ctx.inc("DAS", sync.dma_start(out=sel_sb[:, :], in_=io["sel"][:, :]), 16)
        ctx.dsem("DAS")
        dw_as = ctx.n("DAS")

        ctx.dsem("DQW")
        ctx.dsem("DAW")
        ctx.dsem("DGX")
        ctx.dsem("DGA")
        ctx.dsem("DOT")

        prj_war = [(0, 0), (0, 0)]
        prj_evac_free = [0, 0]
        nscr_free_vc = 0
        wo_free_tc = 0
        oS_free_dot = 0

        for half in range(2):
            hoff = half * 512
            proj_t0 = ctx.n("TC")

            for wi, wname in ((0, "wqT"), (1, "wkT"), (2, "wvT")):
                war_tc = ctx.wbuf_free_tc

                @ctx.block.gpsimd
                def _(gpsimd, half=half, wname=wname, war_tc=war_tc):
                    if war_tc:
                        gpsimd.wait_ge(ctx.sem("TC"), war_tc)
                    ctx.inc("DQW", gpsimd.dma_start(
                        out=wbuf[:, :].rearrange("p (rd f) -> p rd f", rd=ND),
                        in_=io[wname][:, half * 512:(half + 1) * 512]
                            .rearrange("(rd p) f -> p rd f", p=128)), 16)
                dg_w = ctx.n("DQW")

                for tt in range(NT):
                    par = (wi * NT + tt) % 2

                    @ctx.block.tensor
                    def _(tensor, tt=tt, wi=wi, par=par, dg_w=dg_w, lnr=lnr,
                          pw=tuple(prj_war[par])):
                        if tt == 0:
                            if wi == 0 and half == 0:
                                tensor.wait_ge(ctx.sem("VC"), lnr)
                            tensor.wait_ge(ctx.sem("DQW"), dg_w)
                        if pw[0]:
                            tensor.wait_ge(ctx.sem("SC"), pw[0])
                        if pw[1]:
                            tensor.wait_ge(ctx.sem("VC"), pw[1])
                        last = None
                        for rd in range(ND):
                            last = tensor.matmul(
                                prj[par][:, :],
                                ctx.lnT[:, rd * T + tt * 128: rd * T + (tt + 1) * 128],
                                wbuf[:, rd * 512:(rd + 1) * 512],
                                start=(rd == 0), stop=(rd == ND - 1))
                        ctx.inc("TC", last)
                    tc_prj = ctx.n("TC")

                    if wi == 2:
                        @ctx.block.vector
                        def _(vector, tt=tt, par=par, tc_prj=tc_prj):
                            vector.wait_ge(ctx.sem("TC"), tc_prj)
                            vector.tensor_copy(vbf[:, tt * 512:(tt + 1) * 512], prj[par][:, :])
                            ctx.inc("VC", vector.drain())
                        prj_war[par] = (0, ctx.n("VC"))
                    else:
                        c0 = par * 8

                        @ctx.block.scalar
                        def _(scalar, par=par, c0=c0, tc_prj=tc_prj,
                              first=(half == 0 and wi == 0 and tt == 0)):
                            if first:
                                scalar.wait_ge(ctx.sem("DAS"), dw_as)
                            scalar.wait_ge(ctx.sem("TC"), tc_prj)
                            for hh in range(8):
                                scalar.activation(out=sqs[:, hh * 64:(hh + 1) * 64],
                                                  in_=prj[par][:, hh * 64:(hh + 1) * 64],
                                                  func=AF.Square,
                                                  accum_out=QS2[:, c0 + hh: c0 + hh + 1])
                            ctx.inc("SC", scalar.drain())
                        sc_st = ctx.n("SC")

                        @ctx.block.vector
                        def _(vector, par=par, c0=c0, tc_prj=tc_prj, sc_st=sc_st):
                            vector.wait_ge(ctx.sem("TC"), tc_prj)
                            for hh in range(8):
                                vector.reduce_sum(QS1[:, c0 + hh: c0 + hh + 1],
                                                  prj[par][:, hh * 64:(hh + 1) * 64], axis=AX.X)
                            vector.drain()
                            vector.tensor_scalar_mul(QMU[:, c0:c0 + 8], QS1[:, c0:c0 + 8], 1.0 / 64)
                            vector.wait_ge(ctx.sem("SC"), sc_st)
                            vector.tensor_scalar_mul(QS2[:, c0:c0 + 8], QS2[:, c0:c0 + 8], 1.0 / 64)
                            vector.drain()
                            vector.tensor_mul(QT1[:, c0:c0 + 8], QMU[:, c0:c0 + 8], QMU[:, c0:c0 + 8])
                            vector.drain()
                            vector.tensor_sub(QS2[:, c0:c0 + 8], QS2[:, c0:c0 + 8], QT1[:, c0:c0 + 8])
                            ctx.inc("VC", vector.drain())
                        vc_var = ctx.n("VC")

                        sc_sd = ctx.n("SC") + 1
                        vc_rs = ctx.n("VC") + 1

                        @ctx.block.scalar
                        def _(scalar, tt=tt, wi=wi, par=par, c0=c0, vc_var=vc_var,
                              vc_rs=vc_rs, nfw=nscr_free_vc):
                            scalar.wait_ge(ctx.sem("VC"), vc_var)
                            scalar.activation(out=QT1[:, c0:c0 + 8], in_=QS2[:, c0:c0 + 8],
                                              func=AF.Sqrt, bias=cst(ctx, EPS))
                            ctx.inc("SC", scalar.drain())
                            scalar.wait_ge(ctx.sem("VC"), vc_rs)
                            if wi == 1 and nfw:
                                scalar.wait_ge(ctx.sem("VC"), nfw)
                            for hh in range(8):
                                if wi == 0:
                                    out = qn[:, tt * 512 + hh * 64: tt * 512 + (hh + 1) * 64]
                                else:
                                    out = nscr[:, hh * 64:(hh + 1) * 64]
                                scalar.activation(out=out, in_=prj[par][:, hh * 64:(hh + 1) * 64],
                                                  func=AF.Identity,
                                                  scale=QS1[:, c0 + hh: c0 + hh + 1],
                                                  bias=QNM[:, c0 + hh: c0 + hh + 1])
                            ctx.inc("SC", scalar.drain())
                        assert ctx.n("SC") == sc_sd + 1

                        @ctx.block.vector
                        def _(vector, c0=c0, sc_sd=sc_sd):
                            vector.wait_ge(ctx.sem("SC"), sc_sd)
                            vector.reciprocal(QS1[:, c0:c0 + 8], QT1[:, c0:c0 + 8])
                            vector.drain()
                            vector.scalar_tensor_tensor(QNM[:, c0:c0 + 8], QMU[:, c0:c0 + 8], -1.0,
                                                        QS1[:, c0:c0 + 8], ALU.mult, ALU.mult)
                            ctx.inc("VC", vector.drain())
                        assert ctx.n("VC") == vc_rs
                        sc_nm = ctx.n("SC")
                        prj_evac_free[par] = sc_nm
                        prj_war[par] = (sc_nm, vc_var)

                        if wi == 1:
                            @ctx.block.vector
                            def _(vector, tt=tt, sc_nm=sc_nm):
                                vector.wait_ge(ctx.sem("SC"), sc_nm)
                                for hh in range(8):
                                    vector.tensor_mul(tscr[:, :], nscr[:, hh * 64:(hh + 1) * 64], kgbc[:, :])
                                    vector.drain()
                                    vector.tensor_add(kbf[:, tt * 512 + hh * 64: tt * 512 + (hh + 1) * 64],
                                                      tscr[:, :], kbbc[:, :])
                                    vector.drain()
                                ctx.inc("VC", vector.drain())
                            nscr_free_vc = ctx.n("VC")
                ctx.wbuf_free_tc = ctx.n("TC")

            vc_proj = ctx.n("VC")
            sc_proj = ctx.n("SC")

            @ctx.block.gpsimd
            def _(gpsimd, half=half, wtc=wo_free_tc):
                if wtc:
                    gpsimd.wait_ge(ctx.sem("TC"), wtc)
                ctx.inc("DAW", gpsimd.dma_start(
                    out=wosb[:, :].rearrange("p (m d) -> p m d", m=4),
                    in_=io["woT"][half * 512:(half + 1) * 512, :]
                        .rearrange("(m p) d -> p m d", p=128)), 16)
            dg_wo = ctx.n("DAW")

            tc_g0 = ctx.n("TC")
            vc_g0 = ctx.n("VC")

            @ctx.block.tensor
            def _(tensor, vc_proj=vc_proj, sc_proj=sc_proj, tc_g0=tc_g0, vc_g0=vc_g0):
                tensor.wait_ge(ctx.sem("VC"), vc_proj)
                tensor.wait_ge(ctx.sem("SC"), sc_proj)
                for i in range(16):
                    hh, blk = i // 2, i % 2
                    if i >= 1:
                        tensor.wait_ge(ctx.sem("VC"), vc_g0 + i)
                    last = None
                    for cl in range(8):
                        col = (blk * 8 + cl) * 512 + hh * 64
                        last = tensor.matmul(opp[0][0:64, 0:64], kbf[:, col:col + 64],
                                             vbf[:, col:col + 64],
                                             start=(cl == 0), stop=(cl == 7))
                    ctx.inc("TC", last)

            @ctx.block.vector
            def _(vector, tc_g0=tc_g0):
                for i in range(16):
                    vector.wait_ge(ctx.sem("TC"), tc_g0 + i + 1)
                    ctx.inc("VC", vector.tensor_copy(G_sb[:, i * 64:(i + 1) * 64], opp[0][0:64, 0:64]))
            assert ctx.n("VC") == vc_g0 + 16
            vc_g = ctx.n("VC")

            @ctx.block.sync
            def _(sync, vc_g=vc_g):
                sync.wait_ge(ctx.sem("VC"), vc_g)
                ctx.inc("DGX", sync.dma_start(out=io["g_loc"][:, :], in_=G_sb[:, :]), 16)
            dgx = ctx.n("DGX")

            @ctx.block.gpsimd
            def _(gpsimd, dgx=dgx):
                gpsimd.wait_ge(ctx.sem("DGX"), dgx)
                ctx.inc("CC", gpsimd.collective_compute(
                    "AllGather", ALU.bypass, replica_groups=GROUPS,
                    ins=[io["g_loc"].opt()], outs=[io["g_all"].opt()]))
            cc_n = ctx.n("CC")

            @ctx.block.gpsimd
            def _(gpsimd, pfw=getattr(ctx, "pfx_free_vc", 0), dgx=dgx):
                if pfw:
                    gpsimd.wait_ge(ctx.sem("VC"), pfw)
                gpsimd.wait_ge(ctx.sem("DGX"), dgx)
                gpsimd.memset(Pfx[:, :], 0.0)
                ctx.inc("GP", gpsimd.drain())
            gp_pfx = ctx.n("GP")

            vc_pfx_base = ctx.n("VC")
            for r in range(4):
                @ctx.block.sync
                def _(sync, r=r, cc_n=cc_n, base=vc_pfx_base):
                    if r == 0:
                        sync.wait_ge(ctx.sem("CC"), cc_n)
                    else:
                        sync.wait_ge(ctx.sem("VC"), base + r)
                    ctx.inc("DGA", sync.dma_start(out=gall[:, :],
                                                  in_=io["g_all"][r * 64:(r + 1) * 64, :]), 16)
                dga = ctx.n("DGA")

                @ctx.block.vector
                def _(vector, r=r, dga=dga, gp_pfx=gp_pfx):
                    vector.wait_ge(ctx.sem("DGA"), dga)
                    if r == 0:
                        vector.wait_ge(ctx.sem("GP"), gp_pfx)
                    for sub in range(2):
                        sgb = r if sub == 0 else 7 - r
                        for hh in range(8):
                            for blk in range(2):
                                pcol = (hh * 2 + blk) * 64
                                vector.scalar_tensor_tensor(
                                    Pfx[:, pcol:pcol + 64],
                                    gall[:, hh * 128 + sub * 64: hh * 128 + (sub + 1) * 64],
                                    sel_sb[:, sgb * 2 + blk: sgb * 2 + blk + 1],
                                    Pfx[:, pcol:pcol + 64], ALU.mult, ALU.add)
                        vector.drain()
                    ctx.inc("VC", vector.drain())
            vc_pfx = ctx.n("VC")
            ctx.pfx_free_vc = 0

            prev_head_t3 = getattr(ctx, "l0_prev_head_t3", 0)
            t3_hist = getattr(ctx, "l0_t3_hist", {})
            ctx.l0_t3_hist = t3_hist
            for hh in range(8):
                h = half * 8 + hh
                mloc = hh // 2
                odd = hh % 2

                head_t0 = ctx.n("TC")

                head_sc0 = ctx.n("SC")

                @ctx.block.tensor
                def _(tensor, hh=hh, sc_prev=ctx.n("SC"), head_sc0=head_sc0,
                      wvp=(vc_proj if hh == 0 else 0)):
                    tensor.wait_ge(ctx.sem("SC"), sc_prev)
                    if wvp:
                        tensor.wait_ge(ctx.sem("VC"), wvp)
                    i = 0
                    for si, srcb in ((0, qn), (1, kbf)):
                        for q4 in range(4):
                            tpar = i % 2
                            if i >= 2:
                                tensor.wait_ge(ctx.sem("SC"), head_sc0 + i - 1)
                            last = None
                            for tl in range(4):
                                tt = q4 * 4 + tl
                                last = tensor.transpose(
                                    trp[tpar][0:64, tl * 128:(tl + 1) * 128],
                                    srcb[:, tt * 512 + hh * 64: tt * 512 + (hh + 1) * 64],
                                    ctx.idenb[:, :])
                            ctx.inc("TC", last)
                            i += 1

                @ctx.block.scalar
                def _(scalar, hh=hh, head_t0=head_t0):
                    i = 0
                    for si, dst in ((0, qT_h), (1, kT_h)):
                        for q4 in range(4):
                            tpar = i % 2
                            i += 1
                            scalar.wait_ge(ctx.sem("TC"), head_t0 + i)
                            if si == 0:
                                scalar.activation(out=dst[:, q4 * 512:(q4 + 1) * 512],
                                                  in_=trp[tpar][0:64, 0:512], func=AF.Identity,
                                                  scale=lnp[:, 0:1], bias=lnp[:, 1:2])
                            else:
                                scalar.activation(out=dst[:, q4 * 512:(q4 + 1) * 512],
                                                  in_=trp[tpar][0:64, 0:512], func=AF.Identity)
                            ctx.inc("SC", scalar.drain())
                sc_tr = ctx.n("SC")

                sT_ready_vc = {}
                t3_tc = {}
                oev_sc = {}
                gl_vc = {}

                for c in range(16):
                    blk, cl = c // 8, c % 8
                    cpar = c % 2

                    if cl == 0:
                        @ctx.block.vector
                        def _(vector, hh=hh, blk=blk, c=c, sc_tr=sc_tr,
                              w1=t3_tc.get(c - 1, prev_head_t3), vc_pfx=vc_pfx):
                            if blk == 0:
                                vector.wait_ge(ctx.sem("SC"), sc_tr)
                                if hh == 0:
                                    vector.wait_ge(ctx.sem("VC"), vc_pfx)
                            if w1:
                                vector.wait_ge(ctx.sem("TC"), w1)
                            pcol = (hh * 2 + blk) * 64
                            vector.tensor_copy(Pb[blk][:, :], Pfx[:, pcol:pcol + 64])
                            ctx.inc("VC", vector.drain())

                    vc_sT = ctx.n("VC") + 1
                    sT_ready_vc[c] = vc_sT
                    tc_0 = ctx.n("TC")
                    t1 = tc_0 + 1
                    t2 = tc_0 + 2 if cl != 7 else None
                    t3 = tc_0 + (3 if cl != 7 else 2)

                    @ctx.block.tensor
                    def _(tensor, hh=hh, c=c, cl=cl, blk=blk, cpar=cpar, sc_tr=sc_tr,
                          wsc=sT_ready_vc.get(c - 2, 0),
                          wgl=gl_vc.get(c - 2, 0),
                          wop=oev_sc.get(c - 2, 0),
                          glv=gl_vc.get(c - 1, 0) if cl >= 1 else 0,
                          vc_sT=vc_sT):
                        col = c * 512 + hh * 64
                        if c == 0:
                            tensor.wait_ge(ctx.sem("SC"), sc_tr)
                        if wsc:
                            tensor.wait_ge(ctx.sem("VC"), wsc)
                        ctx.inc("TC", tensor.matmul(scp[cpar][0:128, 0:128],
                                                    kT_h[:, c * 128:(c + 1) * 128],
                                                    qT_h[:, c * 128:(c + 1) * 128],
                                                    start=True, stop=True))
                        if cl != 7:
                            if wgl:
                                tensor.wait_ge(ctx.sem("VC"), wgl)
                            ctx.inc("TC", tensor.matmul(prj[cpar][0:64, 0:64],
                                                        kbf[:, col:col + 64], vbf[:, col:col + 64],
                                                        start=True, stop=True))
                        tensor.wait_ge(ctx.sem("VC"), vc_sT)
                        if glv:
                            tensor.wait_ge(ctx.sem("VC"), glv)
                        if wop:
                            tensor.wait_ge(ctx.sem("VC"), wop)
                        tensor.matmul(opp[cpar][0:64, 0:128], vbf[:, col:col + 64],
                                      ST1[cpar][:, :], start=True, stop=False)
                        mm = tensor.matmul(opp[cpar][0:64, 0:128], Pb[blk][:, :],
                                           qT_h[:, c * 128:(c + 1) * 128],
                                           start=False, stop=(cl == 0))
                        for j in range(cl):
                            mm = tensor.matmul(opp[cpar][0:64, 0:128], Gl[:, j * 64:(j + 1) * 64],
                                               qT_h[:, c * 128:(c + 1) * 128],
                                               start=False, stop=(j == cl - 1))
                        ctx.inc("TC", mm)
                    assert ctx.n("TC") == t3
                    t3_tc[c] = t3

                    @ctx.block.vector
                    def _(vector, hh=hh, c=c, cpar=cpar, t1=t1, t2=t2,
                          t3=t3, odd=odd, mloc=mloc, osf=oS_free_dot,
                          wtw=t3_tc.get(c - 2, 0), cl=cl):
                        vector.wait_ge(ctx.sem("TC"), t1)
                        if wtw:
                            vector.wait_ge(ctx.sem("TC"), wtw)
                        vector.tensor_mul(ST1[cpar][:, :], scp[cpar][0:128, 0:128], msk[:, :])
                        ctx.inc("VC", vector.drain())
                        if cl != 7:
                            vector.wait_ge(ctx.sem("TC"), t2)
                            vector.tensor_copy(Gl[:, cl * 64:(cl + 1) * 64],
                                               prj[cpar][0:64, 0:64])
                            ctx.inc("VC", vector.drain())
                        vector.wait_ge(ctx.sem("TC"), t3)
                        if odd and c == 0 and osf:
                            vector.wait_ge(ctx.sem("DOT"), osf)
                        if odd:
                            out = oS[:, c * 128:(c + 1) * 128]
                        else:
                            out = oTh[0:64, mloc * 2048 + c * 128: mloc * 2048 + (c + 1) * 128]
                        vector.tensor_copy(out, opp[cpar][0:64, 0:128])
                        ctx.inc("VC", vector.drain())
                    if cl != 7:
                        gl_vc[c] = vc_sT + 1
                    oev_sc[c] = ctx.n("VC")

                prev_head_t3 = ctx.n("TC")
                ctx.l0_prev_head_t3 = prev_head_t3
                t3_hist[half * 8 + hh] = prev_head_t3

                if odd:
                    @ctx.block.sync
                    def _(sync, mloc=mloc, vc_last=ctx.n("VC")):
                        sync.wait_ge(ctx.sem("VC"), vc_last)
                        ctx.inc("DOT", sync.dma_start(
                            out=oTh[64:128, mloc * 2048:(mloc + 1) * 2048], in_=oS[:, :]), 16)
                    oS_free_dot = ctx.n("DOT")

            dot_n = ctx.n("DOT")
            vc_chunks = ctx.n("VC")
            sc_chunks = ctx.n("SC")
            wo_t0 = ctx.n("TC")
            vc_wo0 = ctx.n("VC")

            @ctx.block.tensor
            def _(tensor, dg_wo=dg_wo, dot_n=dot_n, sc_chunks=sc_chunks,
                  vc_wo0=vc_wo0, vc_chunks=vc_chunks):
                tensor.wait_ge(ctx.sem("DAW"), dg_wo)
                tensor.wait_ge(ctx.sem("DOT"), dot_n)
                tensor.wait_ge(ctx.sem("SC"), sc_chunks)
                tensor.wait_ge(ctx.sem("VC"), vc_chunks)
                for tt in range(NT):
                    for dh in range(2):
                        if tt >= 1:
                            tensor.wait_ge(ctx.sem("VC"), vc_wo0 + (tt - 1) * 2 + dh + 1)
                        last = None
                        for mm in range(4):
                            last = tensor.matmul(
                                prj[dh][:, :],
                                oTh[:, mm * 2048 + tt * 128: mm * 2048 + (tt + 1) * 128],
                                wosb[:, mm * 1024 + dh * 512: mm * 1024 + (dh + 1) * 512],
                                start=(mm == 0), stop=(mm == 3))
                        ctx.inc("TC", last)

            @ctx.block.vector
            def _(vector, wo_t0=wo_t0):
                vector.drain()
                for tt in range(NT):
                    for dh in range(2):
                        vector.wait_ge(ctx.sem("TC"), wo_t0 + tt * 2 + dh + 1)
                        ctx.inc("VC", vector.tensor_add(
                            ctx.resid[:, tt * D + dh * 512: tt * D + (dh + 1) * 512],
                            ctx.resid[:, tt * D + dh * 512: tt * D + (dh + 1) * 512],
                            prj[dh][:, :]))
            vc_wo = ctx.n("VC")
            assert vc_wo == vc_wo0 + 32
            wo_free_tc = ctx.n("TC")
            ctx.pfx_free_vc = vc_chunks
            ctx.wbuf_free_tc = ctx.n("TC")
            prj_war = [(0, vc_wo), (0, vc_wo)]

        @ctx.block.vector
        def _(vector):
            ctx.inc("VC", vector.drain())


def l0attn_host_prep(inp, core):
    b, g = core // 4, core % 4
    sel = np.zeros((16,), np.float32)
    for blk, gb in ((0, g), (1, 7 - g)):
        for sgb in range(8):
            if sgb < gb:
                sel[sgb * 2 + blk] = 1.0
    mask = np.tril(np.ones((128, 128), np.float32)).T
    return dict(
        sel=np.ascontiguousarray(np.broadcast_to(sel, (64, 16))),
        msk_lin=np.ascontiguousarray(mask),
    )


def l0attn_host_prep_shared(inp):
    qg = inp["L0_attn_q_gamma"]; qb = inp["L0_attn_q_beta"]
    kg = inp["L0_attn_k_gamma"]; kb = inp["L0_attn_k_beta"]
    return dict(
        wqT=np.ascontiguousarray(inp["L0_attn_Wq"].T),
        wkT=np.ascontiguousarray(inp["L0_attn_Wk"].T),
        wvT=np.ascontiguousarray(inp["L0_attn_Wv"].T),
        woT=np.ascontiguousarray(inp["L0_attn_Wo"].T),
        qklnp=np.ascontiguousarray(np.stack([qg, qb, kg, kb], axis=1)),
        kgbc=np.ascontiguousarray(np.broadcast_to(kg, (128, 64))),
        kbbc=np.ascontiguousarray(np.broadcast_to(kb, (128, 64))),
    )


def build_l0attn_program():
    nc = bass.Bass("TRN2", target_bir_lowering=False)
    io = {}
    io["x"] = nc.dram_tensor("x", [T, D], F32, kind="ExternalInput").ap()
    for nm in ("wqT", "wkT", "wvT", "woT"):
        io[nm] = nc.dram_tensor(nm, [D, D], F32, kind="ExternalInput").ap()
    io["qklnp"] = nc.dram_tensor("qklnp", [64, 4], F32, kind="ExternalInput").ap()
    io["kgbc"] = nc.dram_tensor("kgbc", [128, 64], F32, kind="ExternalInput").ap()
    io["kbbc"] = nc.dram_tensor("kbbc", [128, 64], F32, kind="ExternalInput").ap()
    io["msk_lin"] = nc.dram_tensor("msk_lin", [128, 128], F32, kind="ExternalInput").ap()
    io["sel"] = nc.dram_tensor("sel", [64, 16], F32, kind="ExternalInput").ap()
    io["iden"] = nc.dram_tensor("iden", [128, 128], F32, kind="ExternalInput").ap()
    io["g_loc"] = nc.dram_tensor("g_loc", [64, 1024], F32).ap()
    io["g_all"] = nc.dram_tensor("g_all", [256, 1024], F32).ap()
    io["y"] = nc.dram_tensor("y", [T, D], F32, kind="ExternalOutput").ap()

    with ExitStack() as stack:
        block = stack.enter_context(nc.Block())
        ctx = Ctx(nc, stack, block)
        ctx.debug = False
        ctx.wbuf_free_tc = 0
        for s in ("PH", "GP", "SC", "VC", "TC", "ST", "CC"):
            ctx.sem(s)
        emit_common(ctx, io)
        ctx.lnT = ctx.sb("lnT", [128, ND * T], BF16)
        emit_load_resid(ctx, io, "x")
        barrier(ctx)
        emit_ln(ctx)
        barrier(ctx)
        emit_l0attn(ctx, io)
        barrier(ctx)
        emit_store_resid(ctx, io, "y")
    return nc



KVW = 16384 + 16 * 1040
KVWH = 8192 + 8 * 1040


def emit_l1attn(ctx, io, ln_stk):
    with ExitStack() as stk:
        bqk = ctx.sb("t_bqk", [128, 16], F32, stk)
        bvbc = ctx.sb("t_bvbc", [128, 1024], F32, stk)
        qT = ctx.sb("t_qT", [128, 8 * 2048], BF16, stk)
        sel128 = ctx.sb("t_sel", [128, 2], BF16, stk)
        ones1 = ctx.sb("t_ones1", [1, 128], BF16, stk)
        ones1f = ctx.sb("t_ones1f", [8, 128], F32, stk)
        KNt = ctx.sb("t_knt", [2, 32], F32, stk)
        QNt = ctx.sb("t_qnt", [2, 32], F32, stk)
        KN8 = ctx.sb("t_kn8", [2, 8], F32, stk)
        KN1 = ctx.sb("t_kn1", [1, 16], F32, stk)
        QN1 = ctx.sb("t_qn1", [1, 64], F32, stk)
        KN4 = ctx.sb("t_kn4", [1, 64], F32, stk)
        MN = ctx.sb("t_mn", [1, 64], F32, stk)
        MB = ctx.sb("t_mb", [128, 64], F32, stk)
        ZB = ctx.sb("t_zb", [128, 16], F32, stk)
        MBZ = ctx.sb("t_mbz", [128, 512], F32, stk)
        prj = [ctx.ps(f"t_prj{i}", [128, 512], F32, stk) for i in range(2)]
        prjO = [ctx.ps(f"t_prjO{i}", [128, 512], F32, stk) for i in range(2)]
        accE = [ctx.ps(f"t_accE{i}", [128, 512], F32, stk) for i in range(2)]
        accO = [ctx.ps(f"t_accO{i}", [128, 512], F32, stk) for i in range(2)]
        nrm = prjO[0]
        rbp = prjO[1]

        lnr = ctx.lnT_ready
        kvw = [io["kv_loc0"], io["kv_loc1"]]
        kvg = [io["kv_all0"], io["kv_all1"]]

        @ctx.block.sync
        def _(sync):
            ctx.inc("DTS", sync.dma_start(out=bqk[:, :], in_=io["bqk"][:, :]), 16)
            ctx.inc("DTS", sync.dma_start(out=bvbc[:, :], in_=io["bv_bc"][:, :]), 16)
            ctx.inc("DTS", sync.dma_start(out=ZB[:, :], in_=io["zb"][:, :]), 16)
        ctx.dsem("DTS")
        dts = ctx.n("DTS")

        @ctx.block.gpsimd
        def _(gpsimd):
            gpsimd.memset(ones1[:, :], 1.0)
            gpsimd.memset(ones1f[:, :], 1.0)
            gpsimd.memset(sel128[:, :], 0.0)
            gpsimd.drain()
            gpsimd.memset(sel128[0:64, 0:1], 1.0)
            gpsimd.memset(sel128[64:128, 1:2], 1.0)
            ctx.inc("GP", gpsimd.drain())
        gp_c = ctx.n("GP")

        ctx.dsem("DTW")
        ctx.dsem("DKV0")
        ctx.dsem("DKV1")
        ctx.dsem("DNB")

        with ExitStack() as pstk:
            wbuf = ctx.sb("t_wbuf", [128, ND * 1024], BF16, pstk)
            ksc = ctx.sb("t_ksc", [128, 512], BF16, pstk)
            vsc = ctx.sb("t_vsc", [128, 512], F32, pstk)
            vaH = ctx.sb("t_vaH", [128, 8 * 1040], BF16, pstk)
            sq2 = ctx.sb("t_sq2", [128, 2048], BF16, pstk)

            @ctx.block.gpsimd
            def _(gpsimd):
                gpsimd.memset(vaH[:, :], 1.0)
                ctx.inc("GP", gpsimd.drain())
            gp_va = ctx.n("GP")

            prj_free = [(0, 0), (0, 0)]
            ppar = 0
            ksc_free_dkv = 0
            va_free_dkv = 0
            knt_col = 0
            nrm_free_vc = 0
            wbuf_free = 0

            for wi, wname in ((0, "wq1"), (1, "wk1")):
                @ctx.block.gpsimd
                def _(gpsimd, wname=wname, war_tc=wbuf_free):
                    if war_tc:
                        gpsimd.wait_ge(ctx.sem("TC"), war_tc)
                    ctx.inc("DTW", gpsimd.dma_start(
                        out=wbuf[:, :].rearrange("p (rd f) -> p rd f", rd=ND),
                        in_=io[wname].rearrange("(rd p) f -> p rd f", p=128)), 16)
                dg_w = ctx.n("DTW")

                if wi < 2:
                    for m in range(8):
                        for t4 in range(4):
                            par = ppar % 2
                            ppar += 1

                            @ctx.block.tensor
                            def _(tensor, wi=wi, m=m, t4=t4, par=par, dg_w=dg_w,
                                  pf=tuple(prj_free[par])):
                                if m == 0 and t4 == 0:
                                    if wi == 0:
                                        tensor.wait_ge(ctx.sem("VC"), lnr)
                                    tensor.wait_ge(ctx.sem("DTW"), dg_w)
                                if pf[0]:
                                    tensor.wait_ge(ctx.sem("SC"), pf[0])
                                if pf[1]:
                                    tensor.wait_ge(ctx.sem("VC"), pf[1])
                                last = None
                                for rd in range(ND):
                                    last = tensor.matmul(
                                        prj[par][:, :],
                                        wbuf[:, rd * 1024 + m * 128: rd * 1024 + (m + 1) * 128],
                                        ctx.lnT[:, rd * T + t4 * 512: rd * T + (t4 + 1) * 512],
                                        start=(rd == 0), stop=(rd == ND - 1))
                                ctx.inc("TC", last)
                            tc_p = ctx.n("TC")

                            if wi == 0:
                                @ctx.block.scalar
                                def _(scalar, m=m, t4=t4, par=par, tc_p=tc_p,
                                      first=(m == 0 and t4 == 0)):
                                    if first:
                                        scalar.wait_ge(ctx.sem("DTS"), dts)
                                    scalar.wait_ge(ctx.sem("TC"), tc_p)
                                    scalar.activation(
                                        out=qT[:, m * 2048 + t4 * 512: m * 2048 + (t4 + 1) * 512],
                                        in_=prj[par][:, :], func=AF.Identity,
                                        bias=bqk[:, m:m + 1])
                                    ctx.inc("SC", scalar.drain())
                                prj_free[par] = (ctx.n("SC"), 0)
                            else:
                                @ctx.block.scalar
                                def _(scalar, m=m, par=par, tc_p=tc_p, kdkv=ksc_free_dkv,
                                      sqw=getattr(ctx, "t_sq2_free_tc", 0)):
                                    scalar.wait_ge(ctx.sem("TC"), tc_p)
                                    if kdkv:
                                        scalar.wait_ge(ctx.sem(kdkv[0]), kdkv[1])
                                    if sqw:
                                        scalar.wait_ge(ctx.sem("TC"), sqw)
                                    scalar.activation(out=ksc[:, :], in_=prj[par][:, :],
                                                      func=AF.Identity, bias=bqk[:, 8 + m:9 + m])
                                    scalar.drain()
                                    scalar.activation(out=sq2[:, 0:512], in_=ksc[:, :], func=AF.Square)
                                    ctx.inc("SC", scalar.drain())
                                sc_k = ctx.n("SC")
                                prj_free[par] = (sc_k, 0)

                                @ctx.block.sync
                                def _(sync, m=m, t4=t4, sc_k=sc_k):
                                    sync.wait_ge(ctx.sem("SC"), sc_k)
                                    kh, lm = m // 4, m % 4
                                    ctx.inc("DKV%d" % kh, sync.dma_start(
                                        out=kvw[kh][:, lm * 2048 + t4 * 512: lm * 2048 + (t4 + 1) * 512],
                                        in_=ksc[:, :]), 16)
                                ksc_free_dkv = ("DKV%d" % (m // 4), ctx.n("DKV%d" % (m // 4)))

                                @ctx.block.tensor
                                def _(tensor, sc_k=sc_k, wn=nrm_free_vc, gp_c=gp_c):
                                    tensor.wait_ge(ctx.sem("GP"), gp_c)
                                    tensor.wait_ge(ctx.sem("SC"), sc_k)
                                    if wn:
                                        tensor.wait_ge(ctx.sem("VC"), wn)
                                    ctx.inc("TC", tensor.matmul(nrm[0:2, 0:512], sel128[:, :],
                                                                sq2[:, 0:512], start=True, stop=True))
                                tc_n = ctx.n("TC")

                                @ctx.block.vector
                                def _(vector, tc_n=tc_n, col=knt_col):
                                    vector.wait_ge(ctx.sem("TC"), tc_n)
                                    vector.reduce_max(KNt[:, col:col + 1], nrm[0:2, 0:512], axis=AX.X)
                                    ctx.inc("VC", vector.drain())
                                nrm_free_vc = ctx.n("VC")
                                knt_col += 1
                                ctx.t_sq2_free_tc = tc_n
                wbuf_free = ctx.n("TC")

            sc_qsq0 = ctx.n("SC")

            tc_qn0 = ctx.n("TC")

            @ctx.block.scalar
            def _(scalar, tc_qn0=tc_qn0, kw=getattr(ctx, "t_sq2_free_tc", 0)):
                for m in range(8):
                    if m == 0:
                        if kw:
                            scalar.wait_ge(ctx.sem("TC"), kw)
                    else:
                        scalar.wait_ge(ctx.sem("TC"), tc_qn0 + 4 * m)
                    scalar.activation(out=sq2[:, :], in_=qT[:, m * 2048:(m + 1) * 2048],
                                      func=AF.Square)
                    ctx.inc("SC", scalar.drain())

            vc_qn0 = ctx.n("VC")

            @ctx.block.tensor
            def _(tensor, sc_qsq0=sc_qsq0, tc_qn0=tc_qn0, vc_qn0=vc_qn0, wn=nrm_free_vc):
                if wn:
                    tensor.wait_ge(ctx.sem("VC"), wn)
                i = 0
                for m in range(8):
                    tensor.wait_ge(ctx.sem("SC"), sc_qsq0 + m + 1)
                    for qgg in range(4):
                        if i >= 1:
                            tensor.wait_ge(ctx.sem("VC"), vc_qn0 + i)
                        ctx.inc("TC", tensor.matmul(nrm[0:2, 0:512], sel128[:, :],
                                                    sq2[:, qgg * 512:(qgg + 1) * 512],
                                                    start=True, stop=True))
                        i += 1

            @ctx.block.vector
            def _(vector, tc_qn0=tc_qn0):
                for i in range(32):
                    vector.wait_ge(ctx.sem("TC"), tc_qn0 + i + 1)
                    vector.reduce_max(QNt[:, i:i + 1], nrm[0:2, 0:512], axis=AX.X)
                    ctx.inc("VC", vector.drain())
                vector.reduce_max(KN8[:, :], KNt[:, :].rearrange("p (m c) -> p m c", m=8),
                                  axis=AX.X)
                ctx.inc("VC", vector.drain())
            vc_qn = ctx.n("VC")

            @ctx.block.sync
            def _(sync, vc_qn=vc_qn):
                sync.wait_ge(ctx.sem("VC"), vc_qn)
                ctx.inc("DNB", sync.dma_start(out=io["knloc"][:, :], in_=KN8[:, :]), 16)
                ctx.inc("DNB", sync.dma_start(out=io["nscr"][0:2, 0:32], in_=QNt[:, :]), 16)
            dnb = ctx.n("DNB")

            @ctx.block.gpsimd
            def _(gpsimd, dnb=dnb):
                gpsimd.wait_ge(ctx.sem("DNB"), dnb)
                ctx.inc("CC", gpsimd.collective_compute(
                    "AllReduce", ALU.max, replica_groups=GROUPS,
                    ins=[io["knloc"].opt()], outs=[io["nmax"].opt()]))
            cc_kn = ctx.n("CC")

            @ctx.block.sync
            def _(sync, cc_kn=cc_kn):
                sync.wait_ge(ctx.sem("CC"), cc_kn)
                ctx.inc("DNB", sync.dma_start(
                    out=KN1[:, :], in_=io["nmax"].rearrange("a b -> () (a b)")), 16)
                ctx.inc("DNB", sync.dma_start(
                    out=QN1[:, :].rearrange("p (m a q) -> p m a q", m=8, a=2),
                    in_=io["nscr"][0:2, 0:32].rearrange("a (m q) -> () m a q", m=8)), 16)
            dnb2 = ctx.n("DNB")

            @ctx.block.vector
            def _(vector, dnb2=dnb2):
                vector.wait_ge(ctx.sem("DNB"), dnb2)
                for qg in range(4):
                    vector.tensor_copy(
                        KN4[:, :].rearrange("p (h q) -> p h q", q=4)[:, :, qg:qg + 1],
                        KN1[:, :].rearrange("p h -> p h ()"))
                vector.drain()
                vector.tensor_mul(MN[:, :], QN1[:, :], KN4[:, :])
                ctx.inc("VC", vector.drain())
            vc_mn = ctx.n("VC")

            @ctx.block.scalar
            def _(scalar, vc_mn=vc_mn):
                scalar.wait_ge(ctx.sem("VC"), vc_mn)
                scalar.activation(out=MN[:, :], in_=MN[:, :], func=AF.Sqrt)
                ctx.inc("SC", scalar.drain())
            sc_mn = ctx.n("SC")

            @ctx.block.vector
            def _(vector, sc_mn=sc_mn):
                vector.wait_ge(ctx.sem("SC"), sc_mn)
                vector.tensor_scalar(MN[:, :], MN[:, :], 0.125, 1.0, ALU.mult, ALU.add)
                ctx.inc("VC", vector.drain())
            vc_mnf = ctx.n("VC")

            @ctx.block.tensor
            def _(tensor, vc_mnf=vc_mnf):
                tensor.wait_ge(ctx.sem("VC"), vc_mnf)
                ctx.inc("TC", tensor.matmul(rbp[:, 0:64], ones1f[0:1, :], MN[:, :],
                                            start=True, stop=True))
            tc_mb = ctx.n("TC")

            @ctx.block.vector
            def _(vector, tc_mb=tc_mb, dts=dts):
                vector.wait_ge(ctx.sem("TC"), tc_mb)
                vector.wait_ge(ctx.sem("DTS"), dts)
                vector.tensor_scalar_mul(MB[:, :], rbp[:, 0:64], -1.0)
                vector.drain()
                for gb in range(8):
                    for qb in range(2):
                        vector.tensor_scalar(
                            MBZ[:, gb * 64: (gb + 1) * 64]
                                .rearrange("p (h q) -> p h q", q=4)[:, :, 2 * qb: 2 * qb + 2],
                            MB[:, :].rearrange("p (h q) -> p h q", q=4)[:, :, 2 * qb: 2 * qb + 2],
                            ZB[:, qb * 8 + gb: qb * 8 + gb + 1], None, ALU.add)
                ctx.inc("VC", vector.drain())
            vc_mbz = ctx.n("VC")

            @ctx.block.gpsimd
            def _(gpsimd, war_tc=wbuf_free):
                if war_tc:
                    gpsimd.wait_ge(ctx.sem("TC"), war_tc)
                ctx.inc("DTW", gpsimd.dma_start(
                    out=wbuf[:, :].rearrange("p (rd f) -> p rd f", rd=ND),
                    in_=io["wv1"].rearrange("(rd p) f -> p rd f", p=128)), 16)
            dg_w = ctx.n("DTW")

            for fh in range(2):
                for tt in range(NT):
                    par = ppar % 2
                    ppar += 1

                    @ctx.block.tensor
                    def _(tensor, tt=tt, fh=fh, par=par, dg_w=dg_w,
                          pf=tuple(prj_free[par])):
                        if tt == 0 and fh == 0:
                            tensor.wait_ge(ctx.sem("DTW"), dg_w)
                        if pf[0]:
                            tensor.wait_ge(ctx.sem("SC"), pf[0])
                        if pf[1]:
                            tensor.wait_ge(ctx.sem("VC"), pf[1])
                        last = None
                        for rd in range(ND):
                            last = tensor.matmul(
                                prj[par][:, :],
                                ctx.lnT[:, rd * T + tt * 128: rd * T + (tt + 1) * 128],
                                wbuf[:, rd * 1024 + fh * 512: rd * 1024 + (fh + 1) * 512],
                                start=(rd == 0), stop=(rd == ND - 1))
                        ctx.inc("TC", last)
                    tc_p = ctx.n("TC")

                    @ctx.block.vector
                    def _(vector, tt=tt, fh=fh, par=par, tc_p=tc_p, wdkv=va_free_dkv,
                          gp_va=gp_va):
                        vector.wait_ge(ctx.sem("TC"), tc_p)
                        if tt == 0 and fh == 0:
                            vector.wait_ge(ctx.sem("GP"), gp_va)
                        if tt == 0 and wdkv:
                            vector.wait_ge(ctx.sem(wdkv[0]), wdkv[1])
                        vector.tensor_add(vsc[:, :], prj[par][:, :],
                                          bvbc[:, fh * 512:(fh + 1) * 512])
                        vector.drain()
                        for hh in range(8):
                            vector.tensor_copy(vaH[:, hh * 1040 + tt * 65: hh * 1040 + tt * 65 + 64],
                                               vsc[:, hh * 64:(hh + 1) * 64])
                        ctx.inc("VC", vector.drain())
                    vc_va = ctx.n("VC")
                    prj_free[par] = (0, vc_va)

                vc_vall = ctx.n("VC")

                @ctx.block.sync
                def _(sync, fh=fh, vc_vall=vc_vall):
                    sync.wait_ge(ctx.sem("VC"), vc_vall)
                    for hh in range(8):
                        ctx.inc("DKV%d" % fh, sync.dma_start(
                            out=kvw[fh][:, 8192 + hh * 1040: 8192 + (hh + 1) * 1040],
                            in_=vaH[:, hh * 1040:(hh + 1) * 1040]), 16)
                va_free_dkv = ("DKV%d" % fh, ctx.n("DKV%d" % fh))

                dkvh_all = ctx.n("DKV%d" % fh)

                @ctx.block.gpsimd
                def _(gpsimd, fh=fh, dkvh_all=dkvh_all):
                    gpsimd.wait_ge(ctx.sem("DKV%d" % fh), dkvh_all)
                    ctx.inc("CC", gpsimd.collective_compute(
                        "AllGather", ALU.bypass, replica_groups=GROUPS,
                        ins=[kvw[fh].opt()], outs=[kvg[fh].opt()]))
                if fh == 0:
                    cc_kv0 = ctx.n("CC")
                    dkv0_all = dkvh_all
                else:
                    cc_kv1 = ctx.n("CC")
                    dkv1_all = dkvh_all
            wbuf_free = ctx.n("TC")
            proj_end_tc = ctx.n("TC")

        ln_stk.close()

        with ExitStack() as fstk:
            mskf = ctx.sb("t_mskf", [128, 4 * 512], BF16, fstk)
            kT2 = ctx.sb("t_kT2", [128, 10 * 1024], BF16, fstk)
            vE = ctx.sb("t_vE", [128, 10 * 520], BF16, fstk)
            vO = ctx.sb("t_vO", [128, 10 * 520], BF16, fstk)
            pTE = [ctx.sb(f"t_pTE{i}", [128, 512], BF16, fstk) for i in range(2)]
            pTO = [ctx.sb(f"t_pTO{i}", [128, 512], BF16, fstk) for i in range(2)]
            rrow = ctx.sb("t_rrow", [1, 4096], F32, fstk)
            oT = ctx.sb("t_oT", [128, 8 * 2048], BF16, fstk)
            oS = ctx.sb("t_oS", [64, 2048], BF16, fstk)

            @ctx.block.gpsimd
            def _(gpsimd, proj_end_tc=proj_end_tc, vc_mbz=vc_mbz):
                gpsimd.wait_ge(ctx.sem("TC"), proj_end_tc)
                gpsimd.wait_ge(ctx.sem("VC"), vc_mbz)
                ctx.inc("DTW", gpsimd.dma_start(out=mskf[:, :], in_=io["mskf"][:, :]), 16)
            dts2 = ctx.n("DTW")

            ctx.dsem("DFH")
            ctx.dsem("DOT")
            pair_free_tc = 0
            oS_free_dot = 0
            exp_sc = [0, 0]
            acc_free = [0, 0]
            av_done = [0, 0]
            rb_free_vc = 0

            for m in range(8):
                hE, hO = 2 * m, 2 * m + 1

                @ctx.block.sync
                def _(sync, m=m, hE=hE, hO=hO, wtc=pair_free_tc, dts2=dts2,
                      cc_kv0=cc_kv0, cc_kv1=cc_kv1,
                      dkv0_all=dkv0_all, dkv1_all=dkv1_all):
                    kh, lm = m // 4, m % 4
                    if m == 0:
                        sync.wait_ge(ctx.sem("DKV0"), dkv0_all)
                        sync.wait_ge(ctx.sem("DTW"), dts2)
                    if m == 4:
                        sync.wait_ge(ctx.sem("DKV1"), dkv1_all)
                    if wtc:
                        sync.wait_ge(ctx.sem("TC"), wtc)
                    for qb in range(2):
                        ctx.inc("DFD", sync.dma_start(
                            out=kT2[:, (8 + qb) * 1024:(9 + qb) * 1024],
                            in_=kvw[kh][:, lm * 2048 + qb * 1024: lm * 2048 + (qb + 1) * 1024]), 16)
                        for hh, vt in ((hE, vE), (hO, vO)):
                            lh = hh % 8
                            ctx.inc("DFD", sync.dma_start(
                                out=vt[:, (8 + qb) * 520:(9 + qb) * 520],
                                in_=kvw[kh][:, 8192 + lh * 1040 + qb * 520: 8192 + lh * 1040 + (qb + 1) * 520]), 16)
                    if m == 0:
                        sync.wait_ge(ctx.sem("CC"), cc_kv0)
                    if m == 4:
                        sync.wait_ge(ctx.sem("CC"), cc_kv1)
                    for r in range(4):
                        for sub in range(2):
                            slot = r if sub == 0 else 7 - r
                            ctx.inc("DFH", sync.dma_start(
                                out=kT2[:, slot * 1024:(slot + 1) * 1024],
                                in_=kvg[kh][r * 128:(r + 1) * 128,
                                            lm * 2048 + sub * 1024: lm * 2048 + (sub + 1) * 1024]), 16)
                            for hh, vt in ((hE, vE), (hO, vO)):
                                lh = hh % 8
                                ctx.inc("DFH", sync.dma_start(
                                    out=vt[:, slot * 520:(slot + 1) * 520],
                                    in_=kvg[kh][r * 128:(r + 1) * 128,
                                                8192 + lh * 1040 + sub * 520: 8192 + lh * 1040 + (sub + 1) * 520]), 16)
                dfh = ctx.n("DFH")
                dfd = ctx.n("DFD")

                pend = [None]
                av_rec = {}

                def flush_av(tensor):
                    if pend[0] is None:
                        return
                    (par, vslot, c2i, aq, first, lastu, wsem, wval, uidx, af) = pend[0]
                    if first and af:
                        tensor.wait_ge(ctx.sem("VC"), af)
                    tensor.wait_ge(ctx.sem(wsem), wval)
                    ctx.inc("TC", tensor.matmul(
                        accE[aq][0:65, 0:512],
                        vE[:, vslot * 520 + c2i * 65: vslot * 520 + (c2i + 1) * 65],
                        pTE[par][:, :], start=first, stop=lastu))
                    ctx.inc("TC", tensor.matmul(
                        accO[aq][0:65, 0:512],
                        vO[:, vslot * 520 + c2i * 65: vslot * 520 + (c2i + 1) * 65],
                        pTO[par][:, :], start=first, stop=lastu))
                    av_done[par] = ctx.n("TC")
                    av_rec[uidx] = ctx.n("TC")
                    pend[0] = None

                ppi = 0
                for qblk in range(2):
                    units = []
                    for qg in range(2):
                        if qg == 0:
                            for c2 in range(4):
                                units.append((8 + qblk, qg, c2, "mask"))
                        else:
                            for c2 in range(4):
                                units.append((8 + qblk, qg, c2, "full"))
                            for c2 in range(4, 8):
                                units.append((8 + qblk, qg, c2, "mask"))
                    ngb = 4 if qblk == 0 else 8
                    for gb in range(ngb):
                        for qg in range(2):
                            for c2 in range(8):
                                units.append((gb, qg, c2, "plain"))
                    seen = {0: False, 1: False}
                    lastidx = {0: None, 1: None}
                    for idx, (gb, qg, c2, cls) in enumerate(units):
                        lastidx[qg] = idx

                    sc_base = ctx.n("SC")
                    vc_base = ctx.n("VC")
                    ppi0 = ppi
                    prev_av = {0: av_done[0], 1: av_done[1]}
                    nmask_before = []
                    nm = 0
                    for (gb, qg, c2, cls) in units:
                        nmask_before.append(nm)
                        if cls == "mask":
                            nm += 1
                    tc_scores = []
                    exp_meta = []

                    for idx, (gb, qg, c2, cls) in enumerate(units):
                        par = ppi % 2
                        ppi += 1
                        qgg = qblk * 2 + qg
                        qlo = m * 2048 + qblk * 1024 + qg * 512
                        first = not seen[qg]
                        seen[qg] = True
                        lastu = (idx == lastidx[qg])
                        if cls == "plain":
                            biasE = MBZ[:, gb * 64 + hE * 4 + qgg: gb * 64 + hE * 4 + qgg + 1]
                            biasO = MBZ[:, gb * 64 + hO * 4 + qgg: gb * 64 + hO * 4 + qgg + 1]
                        else:
                            biasE = MB[:, hE * 4 + qgg: hE * 4 + qgg + 1]
                            biasO = MB[:, hO * 4 + qgg: hO * 4 + qgg + 1]
                        if cls == "mask":
                            wsem, wval = "VC", vc_base + nmask_before[idx] + 1
                        else:
                            wsem, wval = "SC", sc_base + idx + 1
                        if idx >= 2:
                            w_prj = ("SC", sc_base + idx - 1)
                        else:
                            w_prj = ("SC", exp_sc[par]) if exp_sc[par] else None

                        @ctx.block.tensor
                        def _(tensor, gb=gb, c2=c2, par=par, dfh=dfh, dfd=dfd,
                              qlo=qlo, w_prj=w_prj, wfirst=(m == 0 and ppi == 1),
                              vc_mbz=vc_mbz,
                              first=first, isdiag=(gb >= 8)):
                            if wfirst:
                                tensor.wait_ge(ctx.sem("VC"), vc_mbz)
                            if isdiag:
                                tensor.wait_ge(ctx.sem("DFD"), dfd)
                            else:
                                tensor.wait_ge(ctx.sem("DFH"), dfh)
                            if w_prj:
                                tensor.wait_ge(ctx.sem(w_prj[0]), w_prj[1])
                            ctx.inc("TC", tensor.matmul(
                                prj[par][0:128, 0:512],
                                kT2[0:64, gb * 1024 + c2 * 128: gb * 1024 + (c2 + 1) * 128],
                                qT[0:64, qlo: qlo + 512],
                                start=True, stop=True))
                            ctx.inc("TC", tensor.matmul(
                                prjO[par][0:128, 0:512],
                                kT2[64:128, gb * 1024 + c2 * 128: gb * 1024 + (c2 + 1) * 128],
                                qT[64:128, qlo: qlo + 512],
                                start=True, stop=True))
                            tc_scores.append(ctx.n("TC"))
                            flush_av(tensor)
                        pend[0] = (par, gb, c2, qg, first, lastu, wsem, wval, idx,
                                   acc_free[qg] if first else 0)
                        exp_meta.append((par, biasE, biasO, idx))
                        exp_sc[par] = sc_base + idx + 1

                    @ctx.block.tensor
                    def _(tensor):
                        flush_av(tensor)
                    tc_last = ctx.n("TC")

                    @ctx.block.scalar
                    def _(scalar, exp_meta=list(exp_meta), tc_scores=list(tc_scores),
                          av_rec=av_rec, prev_av=dict(prev_av)):
                        for (par, biasE, biasO, idx) in exp_meta:
                            scalar.wait_ge(ctx.sem("TC"), tc_scores[idx])
                            if idx >= 2:
                                scalar.wait_ge(ctx.sem("TC"), av_rec[idx - 2])
                            elif prev_av.get(par):
                                scalar.wait_ge(ctx.sem("TC"), prev_av[par])
                            scalar.activation(out=pTE[par][:, :], in_=prj[par][0:128, 0:512],
                                              func=AF.Exp, scale=0.125, bias=biasE)
                            scalar.activation(out=pTO[par][:, :], in_=prjO[par][0:128, 0:512],
                                              func=AF.Exp, scale=0.125, bias=biasO)
                            ctx.inc("SC", scalar.drain())
                    assert ctx.n("SC") == sc_base + len(units)

                    masked = [(i, u) for i, u in enumerate(units) if u[3] == "mask"]
                    if masked:
                        @ctx.block.vector
                        def _(vector, masked=list(masked), sc_base=sc_base, ppi0=ppi0):
                            for (idx, (gb, qg, c2, cls)) in masked:
                                d = c2 if qg == 0 else c2 - 4
                                p = (ppi0 + idx) % 2
                                vector.wait_ge(ctx.sem("SC"), sc_base + idx + 1)
                                vector.tensor_mul(pTE[p][:, :], pTE[p][:, :],
                                                  mskf[:, d * 512:(d + 1) * 512])
                                vector.tensor_mul(pTO[p][:, :], pTO[p][:, :],
                                                  mskf[:, d * 512:(d + 1) * 512])
                                ctx.inc("VC", vector.drain())
                        assert ctx.n("VC") == vc_base + len(masked)

                    for (hh, accT, hp) in ((hE, accE, 0), (hO, accO, 1)):
                        @ctx.block.vector
                        def _(vector, accT=accT, hp=hp, m=m, qblk=qblk,
                              tc_last=tc_last, osf=oS_free_dot, isodd=(hp == 1)):
                            vector.wait_ge(ctx.sem("TC"), tc_last)
                            if isodd and qblk == 0 and osf:
                                vector.wait_ge(ctx.sem("DOT"), osf)
                            for qg in range(2):
                                col = qblk * 1024 + qg * 512
                                r = hp * 4 + qblk * 2 + qg
                                vector.reciprocal(
                                    rrow[0:1, r * 512:(r + 1) * 512],
                                    accT[qg][64:65, 0:512])
                                if not isodd:
                                    out = oT[0:64, m * 2048 + col: m * 2048 + col + 512]
                                else:
                                    out = oS[:, col:col + 512]
                                vector.tensor_copy(out, accT[qg][0:64, 0:512])
                                ctx.inc("VC", vector.drain())
                        if hp == 1:
                            acc_free[0] = ctx.n("VC") - 1
                            acc_free[1] = ctx.n("VC")

                vc_rr = ctx.n("VC")
                pair_free_tc = ctx.n("TC")

                div_vc = {}
                for u in range(8):
                    hp, qc = u // 4, u % 4
                    pb = (accE if hp == 0 else accO)[qc % 2]

                    @ctx.block.tensor
                    def _(tensor, u=u, pb=pb, vc_rr=vc_rr,
                          wmul=div_vc.get(u - 2, 0) if u % 4 >= 2 else 0):
                        if u == 0:
                            tensor.wait_ge(ctx.sem("VC"), vc_rr)
                        if wmul:
                            tensor.wait_ge(ctx.sem("VC"), wmul)
                        ctx.inc("TC", tensor.matmul(
                            pb[0:64, 0:512], ones1f[0:1, 0:64],
                            rrow[0:1, u * 512:(u + 1) * 512],
                            start=True, stop=True))
                    tc_rb_u = ctx.n("TC")

                    @ctx.block.vector
                    def _(vector, hp=hp, qc=qc, pb=pb, tc=tc_rb_u, m=m):
                        vector.wait_ge(ctx.sem("TC"), tc)
                        col = qc * 512
                        if hp == 0:
                            out = oT[0:64, m * 2048 + col: m * 2048 + col + 512]
                        else:
                            out = oS[:, col:col + 512]
                        vector.tensor_mul(out, out, pb[0:64, 0:512])
                        ctx.inc("VC", vector.drain())
                    div_vc[u] = ctx.n("VC")
                acc_free[0] = ctx.n("VC")
                acc_free[1] = ctx.n("VC")

                @ctx.block.sync
                def _(sync, m=m, vc_e=ctx.n("VC")):
                    sync.wait_ge(ctx.sem("VC"), vc_e)
                    ctx.inc("DOT", sync.dma_start(
                        out=oT[64:128, m * 2048:(m + 1) * 2048], in_=oS[:, :]), 16)
                oS_free_dot = ctx.n("DOT")

            flash_end_tc = ctx.n("TC")
            flash_end_vc = ctx.n("VC")
            dot_all = ctx.n("DOT")

        with ExitStack() as wstk:
            wosb = ctx.sb("t_wosb", [128, 8 * 1024], BF16, wstk)
            bobc = ctx.sb("t_bobc", [128, 1024], F32, wstk)
            gmbc = ctx.sb("t_gmbc", [128, 1024], F32, wstk)
            btbc = ctx.sb("t_btbc", [128, 1024], F32, wstk)
            ysc = ctx.sb("t_ysc", [128, 1024], F32, wstk)
            yh = ctx.sb("t_yh", [128, 1024], F32, wstk)
            ysq = ctx.sb("t_ysq", [128, 1024], F32, wstk)
            yst = ctx.sb("t_yst", [128, 8], F32, wstk)

            @ctx.block.sync
            def _(sync, flash_end_tc=flash_end_tc, flash_end_vc=flash_end_vc):
                sync.wait_ge(ctx.sem("TC"), flash_end_tc)
                sync.wait_ge(ctx.sem("VC"), flash_end_vc)
                ctx.inc("DTS", sync.dma_start(out=bobc[:, :], in_=io["bo_bc"][:, :]), 16)
                ctx.inc("DTS", sync.dma_start(out=gmbc[:, :], in_=io["gm_bc"][:, :]), 16)
                ctx.inc("DTS", sync.dma_start(out=btbc[:, :], in_=io["bt_bc"][:, :]), 16)
            dts3 = ctx.n("DTS")

            @ctx.block.gpsimd
            def _(gpsimd, flash_end_tc=flash_end_tc):
                gpsimd.wait_ge(ctx.sem("TC"), flash_end_tc)
                ctx.inc("DTW", gpsimd.dma_start(
                    out=wosb[:, :].rearrange("p (m d) -> p m d", m=8),
                    in_=io["wo1"].rearrange("(m p) d -> p m d", p=128)), 16)
            dg_wo = ctx.n("DTW")

            prev_evac_vc = 0
            sc_nm_prev = 0
            for tt in range(NT):
                @ctx.block.tensor
                def _(tensor, tt=tt, dg_wo=dg_wo, dot_all=dot_all, pv=prev_evac_vc,
                      fv=flash_end_vc):
                    if tt == 0:
                        tensor.wait_ge(ctx.sem("DTW"), dg_wo)
                        tensor.wait_ge(ctx.sem("DOT"), dot_all)
                        tensor.wait_ge(ctx.sem("VC"), fv)
                    if pv:
                        tensor.wait_ge(ctx.sem("VC"), pv)
                    for dh in range(2):
                        last = None
                        for mm in range(8):
                            last = tensor.matmul(
                                accE[dh][:, :],
                                oT[:, mm * 2048 + tt * 128: mm * 2048 + (tt + 1) * 128],
                                wosb[:, mm * 1024 + dh * 512: mm * 1024 + (dh + 1) * 512],
                                start=(mm == 0), stop=(mm == 7))
                        ctx.inc("TC", last)
                tc_wo = ctx.n("TC")

                @ctx.block.vector
                def _(vector, tt=tt, tc_wo=tc_wo, dts3=dts3, snp=sc_nm_prev):
                    if tt == 0:
                        vector.wait_ge(ctx.sem("DTS"), dts3)
                    if snp:
                        vector.wait_ge(ctx.sem("SC"), snp)
                    vector.wait_ge(ctx.sem("TC"), tc_wo - 1)
                    vector.tensor_add(ysc[:, 0:512], accE[0][:, :], bobc[:, 0:512])
                    vector.wait_ge(ctx.sem("TC"), tc_wo)
                    vector.tensor_add(ysc[:, 512:1024], accE[1][:, :], bobc[:, 512:1024])
                    vector.drain()
                    vector.reduce_sum(yst[:, 0:1], ysc[:, :], axis=AX.X)
                    ctx.inc("VC", vector.drain())
                vc_y = ctx.n("VC")

                @ctx.block.scalar
                def _(scalar, vc_y=vc_y):
                    scalar.wait_ge(ctx.sem("VC"), vc_y)
                    scalar.activation(out=ysq[:, :], in_=ysc[:, :], func=AF.Square,
                                      accum_out=yst[:, 1:2])
                    ctx.inc("SC", scalar.drain())
                sc_ys = ctx.n("SC")

                @ctx.block.vector
                def _(vector, sc_ys=sc_ys):
                    vector.wait_ge(ctx.sem("SC"), sc_ys)
                    vector.tensor_scalar_mul(yst[:, 2:3], yst[:, 0:1], 1.0 / D)
                    vector.tensor_scalar_mul(yst[:, 3:4], yst[:, 1:2], 1.0 / D)
                    vector.drain()
                    vector.tensor_mul(yst[:, 4:5], yst[:, 2:3], yst[:, 2:3])
                    vector.drain()
                    vector.tensor_sub(yst[:, 3:4], yst[:, 3:4], yst[:, 4:5])
                    ctx.inc("VC", vector.drain())
                vc_v = ctx.n("VC")

                @ctx.block.scalar
                def _(scalar, vc_v=vc_v):
                    scalar.wait_ge(ctx.sem("VC"), vc_v)
                    scalar.activation(out=yst[:, 5:6], in_=yst[:, 3:4], func=AF.Sqrt,
                                      bias=cst(ctx, EPS))
                    ctx.inc("SC", scalar.drain())
                sc_sd = ctx.n("SC")

                @ctx.block.vector
                def _(vector, sc_sd=sc_sd):
                    vector.wait_ge(ctx.sem("SC"), sc_sd)
                    vector.reciprocal(yst[:, 6:7], yst[:, 5:6])
                    vector.drain()
                    vector.scalar_tensor_tensor(yst[:, 7:8], yst[:, 2:3], -1.0,
                                                yst[:, 6:7], ALU.mult, ALU.mult)
                    ctx.inc("VC", vector.drain())
                vc_rs = ctx.n("VC")

                @ctx.block.scalar
                def _(scalar, vc_rs=vc_rs):
                    scalar.wait_ge(ctx.sem("VC"), vc_rs)
                    scalar.activation(out=yh[:, :], in_=ysc[:, :], func=AF.Identity,
                                      scale=yst[:, 6:7], bias=yst[:, 7:8])
                    ctx.inc("SC", scalar.drain())
                sc_nm = ctx.n("SC")
                sc_nm_prev = sc_nm

                @ctx.block.vector
                def _(vector, tt=tt, sc_nm=sc_nm):
                    vector.wait_ge(ctx.sem("SC"), sc_nm)
                    vector.tensor_mul(yh[:, :], yh[:, :], gmbc[:, :])
                    vector.drain()
                    vector.tensor_add(yh[:, :], yh[:, :], btbc[:, :])
                    vector.drain()
                    vector.tensor_add(ctx.resid[:, tt * D:(tt + 1) * D],
                                      ctx.resid[:, tt * D:(tt + 1) * D], yh[:, :])
                    ctx.inc("VC", vector.drain())
                prev_evac_vc = ctx.n("VC")

        @ctx.block.vector
        def _(vector):
            ctx.inc("VC", vector.drain())


def l1attn_host_prep(inp, core):
    g = core % 4
    zb = np.zeros((2, 8), np.float32)
    for qb, gbq in ((0, g), (1, 7 - g)):
        for gb in range(8):
            if gb >= gbq:
                zb[qb, gb] = -1e30
    mskf = np.zeros((4, 128, 512), np.float32)
    for d in range(4):
        for j in range(128):
            mskf[d, j, d * 128 + j:] = 1.0
    return dict(
        zb=np.ascontiguousarray(np.broadcast_to(zb.reshape(16), (128, 16))),
        mskf=np.ascontiguousarray(mskf.transpose(1, 0, 2).reshape(128, 2048)),
    )


def l1attn_host_prep_shared(inp):
    bq = inp["L1_attn_bq"]; bk = inp["L1_attn_bk"]
    bqk = np.concatenate([bq.reshape(8, 128).T, bk.reshape(8, 128).T], axis=1)
    return dict(
        wq1=np.ascontiguousarray(inp["L1_attn_Wq"]),
        wk1=np.ascontiguousarray(inp["L1_attn_Wk"]),
        wv1=np.ascontiguousarray(inp["L1_attn_Wv"]),
        wo1=np.ascontiguousarray(inp["L1_attn_Wo"]),
        bqk=np.ascontiguousarray(bqk),
        bv_bc=np.ascontiguousarray(np.broadcast_to(inp["L1_attn_bv"], (128, 1024))),
        bo_bc=np.ascontiguousarray(np.broadcast_to(inp["L1_attn_bo"], (128, 1024))),
        gm_bc=np.ascontiguousarray(np.broadcast_to(inp["L1_attn_gamma"], (128, 1024))),
        bt_bc=np.ascontiguousarray(np.broadcast_to(inp["L1_attn_beta"], (128, 1024))),
    )


def build_l1attn_program():
    nc = bass.Bass("TRN2", target_bir_lowering=False)
    io = {}
    io["x"] = nc.dram_tensor("x", [T, D], F32, kind="ExternalInput").ap()
    for nm in ("wq1", "wk1", "wv1", "wo1"):
        io[nm] = nc.dram_tensor(nm, [D, D], F32, kind="ExternalInput").ap()
    io["bqk"] = nc.dram_tensor("bqk", [128, 16], F32, kind="ExternalInput").ap()
    for nm in ("bv_bc", "bo_bc", "gm_bc", "bt_bc"):
        io[nm] = nc.dram_tensor(nm, [128, 1024], F32, kind="ExternalInput").ap()
    io["zb"] = nc.dram_tensor("zb", [128, 16], F32, kind="ExternalInput").ap()
    io["mskf"] = nc.dram_tensor("mskf", [128, 2048], F32, kind="ExternalInput").ap()
    io["iden"] = nc.dram_tensor("iden", [128, 128], F32, kind="ExternalInput").ap()
    io["kv_loc0"] = nc.dram_tensor("kv_loc0", [128, KVWH], BF16).ap()
    io["kv_loc1"] = nc.dram_tensor("kv_loc1", [128, KVWH], BF16).ap()
    io["kv_all0"] = nc.dram_tensor("kv_all0", [512, KVWH], BF16).ap()
    io["kv_all1"] = nc.dram_tensor("kv_all1", [512, KVWH], BF16).ap()
    io["nscr"] = nc.dram_tensor("nscr", [2, 32], F32).ap()
    io["knloc"] = nc.dram_tensor("knloc", [2, 8], F32).ap()
    io["nmax"] = nc.dram_tensor("nmax", [2, 8], F32).ap()
    io["y"] = nc.dram_tensor("y", [T, D], F32, kind="ExternalOutput").ap()

    with ExitStack() as stack:
        block = stack.enter_context(nc.Block())
        ctx = Ctx(nc, stack, block)
        ctx.debug = False
        for s in ("PH", "GP", "SC", "VC", "TC", "ST", "CC"):
            ctx.sem(s)
        emit_common(ctx, io)
        ln_stk = ExitStack()
        ctx.lnT = ctx.sb("lnT", [128, ND * T], BF16, ln_stk, side="right")
        emit_load_resid(ctx, io, "x")
        barrier(ctx)
        emit_ln(ctx)
        barrier(ctx)
        emit_l1attn(ctx, io, ln_stk)
        barrier(ctx)
        emit_store_resid(ctx, io, "y")
    return nc



def _shard_tokens(x):
    out = []
    for core in range(NCORES):
        b, g = core // 4, core % 4
        a = x[b, g * 1024:(g + 1) * 1024]
        c = x[b, (7 - g) * 1024:(8 - g) * 1024]
        out.append(np.ascontiguousarray(np.concatenate([a, c], axis=0)))
    return out


def _unshard_tokens(parts):
    d = parts[0].shape[-1]
    x = np.empty((2, 8192, d), dtype=parts[0].dtype)
    for core in range(NCORES):
        b, g = core // 4, core % 4
        x[b, g * 1024:(g + 1) * 1024] = parts[core][:1024]
        x[b, (7 - g) * 1024:(8 - g) * 1024] = parts[core][1024:]
    return x


def moe_host_prep(inp):
    gate_upT = np.ascontiguousarray(inp["L0_ffn_gate_up_proj"].transpose(0, 2, 1))
    downT = np.ascontiguousarray(inp["L0_ffn_down_proj"].transpose(0, 2, 1))
    wrT = np.ascontiguousarray(inp["L0_ffn_router_weight"].T)
    br = inp["L0_ffn_router_bias"]
    brbc = np.ascontiguousarray(np.broadcast_to(np.tile(br, NT), (128, NT * NE)))
    gub = inp["L0_ffn_gate_up_proj_bias"]
    bg = np.ascontiguousarray(gub[:, :2048].reshape(NE, 16, 128).transpose(2, 0, 1).reshape(128, 128))
    bu = np.ascontiguousarray(gub[:, 2048:].reshape(NE, 16, 128).transpose(2, 0, 1).reshape(128, 128))
    db = np.ascontiguousarray(inp["L0_ffn_down_proj_bias"])
    return dict(gate_upT=gate_upT, downT=downT, wrT=wrT, brbc=brbc, bg=bg, bu=bu, db=db,
                iden=np.eye(128, dtype=np.float32))



def build_moe_program(debug=False):
    nc = bass.Bass("TRN2", target_bir_lowering=False)
    io = {}
    io["x"] = nc.dram_tensor("x", [T, D], F32, kind="ExternalInput").ap()
    io["gate_upT"] = nc.dram_tensor("gate_upT", [NE, D, 4096], F32, kind="ExternalInput").ap()
    io["downT"] = nc.dram_tensor("downT", [NE, 2048, D], F32, kind="ExternalInput").ap()
    io["wrT"] = nc.dram_tensor("wrT", [D, NE], F32, kind="ExternalInput").ap()
    io["brbc"] = nc.dram_tensor("brbc", [128, NT * NE], F32, kind="ExternalInput").ap()
    io["bg"] = nc.dram_tensor("bg", [128, 128], F32, kind="ExternalInput").ap()
    io["bu"] = nc.dram_tensor("bu", [128, 128], F32, kind="ExternalInput").ap()
    io["db"] = nc.dram_tensor("db", [NE, D], F32, kind="ExternalInput").ap()
    io["iden"] = nc.dram_tensor("iden", [128, 128], F32, kind="ExternalInput").ap()
    io["y"] = nc.dram_tensor("y", [T, D], F32, kind="ExternalOutput").ap()
    if debug:
        io["dbg"] = nc.dram_tensor("dbg", [128, 3072], F32, kind="ExternalOutput").ap()

    with ExitStack() as stack:
        block = stack.enter_context(nc.Block())
        ctx = Ctx(nc, stack, block)
        ctx.debug = debug
        for s in ("PH", "GP", "SC", "VC", "TC", "ST"):
            ctx.sem(s)
        emit_common(ctx, io)
        ctx.lnT = ctx.sb("lnT", [128, ND * T], BF16)
        if debug:
            ctx.dbg = ctx.sb("dbgt", [128, 3072], F32)

            @ctx.block.gpsimd
            def _(gpsimd):
                gpsimd.memset(ctx.dbg[:, :], 0.0)
                ctx.inc("GP", gpsimd.drain())
        emit_load_resid(ctx, io, "x")
        barrier(ctx)
        emit_ln(ctx)
        barrier(ctx)
        emit_moe(ctx, io)
        barrier(ctx)
        emit_store_resid(ctx, io, "y")
        if debug:
            @ctx.block.sync
            def _(sync):
                ctx.inc("ST", sync.dma_start(out=io["dbg"][:, :], in_=ctx.dbg[:, :]), 16)
                sync.wait_ge(ctx.sem("ST"), ctx.n("ST"))
    return nc



def build_fused_program():
    nc = bass.Bass("TRN2", target_bir_lowering=False)
    io = {}
    io["x"] = nc.dram_tensor("x", [T, D], F32, kind="ExternalInput").ap()
    io["gate_upT"] = nc.dram_tensor("gate_upT", [NE, D, 4096], F32, kind="ExternalInput").ap()
    io["downT"] = nc.dram_tensor("downT", [NE, 2048, D], F32, kind="ExternalInput").ap()
    io["wrT"] = nc.dram_tensor("wrT", [D, NE], F32, kind="ExternalInput").ap()
    io["brbc"] = nc.dram_tensor("brbc", [128, NT * NE], F32, kind="ExternalInput").ap()
    io["bg"] = nc.dram_tensor("bg", [128, 128], F32, kind="ExternalInput").ap()
    io["bu"] = nc.dram_tensor("bu", [128, 128], F32, kind="ExternalInput").ap()
    io["db"] = nc.dram_tensor("db", [NE, D], F32, kind="ExternalInput").ap()
    for nm in ("wqT", "wkT", "wvT", "woT"):
        io[nm] = nc.dram_tensor(nm, [D, D], F32, kind="ExternalInput").ap()
    io["qklnp"] = nc.dram_tensor("qklnp", [64, 4], F32, kind="ExternalInput").ap()
    io["kgbc"] = nc.dram_tensor("kgbc", [128, 64], F32, kind="ExternalInput").ap()
    io["kbbc"] = nc.dram_tensor("kbbc", [128, 64], F32, kind="ExternalInput").ap()
    io["msk_lin"] = nc.dram_tensor("msk_lin", [128, 128], F32, kind="ExternalInput").ap()
    io["sel"] = nc.dram_tensor("sel", [64, 16], F32, kind="ExternalInput").ap()
    io["g_loc"] = nc.dram_tensor("g_loc", [64, 1024], F32).ap()
    io["g_all"] = nc.dram_tensor("g_all", [256, 1024], F32).ap()
    io["u1"] = nc.dram_tensor("u1", [D, 256], F32, kind="ExternalInput").ap()
    io["v1"] = nc.dram_tensor("v1", [256, 4096], F32, kind="ExternalInput").ap()
    io["u2"] = nc.dram_tensor("u2", [4096, 256], F32, kind="ExternalInput").ap()
    io["v2"] = nc.dram_tensor("v2", [256, 1024], F32, kind="ExternalInput").ap()
    io["b1t"] = nc.dram_tensor("b1t", [128, 32], F32, kind="ExternalInput").ap()
    io["b2bc"] = nc.dram_tensor("b2bc", [128, 1024], F32, kind="ExternalInput").ap()
    for nm in ("wq1", "wk1", "wv1", "wo1"):
        io[nm] = nc.dram_tensor(nm, [D, D], F32, kind="ExternalInput").ap()
    io["bqk"] = nc.dram_tensor("bqk", [128, 16], F32, kind="ExternalInput").ap()
    for nm in ("bv_bc", "bo_bc", "gm_bc", "bt_bc"):
        io[nm] = nc.dram_tensor(nm, [128, 1024], F32, kind="ExternalInput").ap()
    io["zb"] = nc.dram_tensor("zb", [128, 16], F32, kind="ExternalInput").ap()
    io["mskf"] = nc.dram_tensor("mskf", [128, 2048], F32, kind="ExternalInput").ap()
    io["kv_loc0"] = nc.dram_tensor("kv_loc0", [128, KVWH], BF16).ap()
    io["kv_loc1"] = nc.dram_tensor("kv_loc1", [128, KVWH], BF16).ap()
    io["kv_all0"] = nc.dram_tensor("kv_all0", [512, KVWH], BF16).ap()
    io["kv_all1"] = nc.dram_tensor("kv_all1", [512, KVWH], BF16).ap()
    io["nscr"] = nc.dram_tensor("nscr", [2, 32], F32).ap()
    io["knloc"] = nc.dram_tensor("knloc", [2, 8], F32).ap()
    io["nmax"] = nc.dram_tensor("nmax", [2, 8], F32).ap()
    io["iden"] = nc.dram_tensor("iden", [128, 128], F32, kind="ExternalInput").ap()
    io["y"] = nc.dram_tensor("y", [T, D], F32, kind="ExternalOutput").ap()

    with ExitStack() as stack:
        block = stack.enter_context(nc.Block())
        ctx = Ctx(nc, stack, block)
        ctx.debug = False
        ctx.wbuf_free_tc = 0
        for s in ("PH", "GP", "SC", "VC", "TC", "ST", "CC"):
            ctx.sem(s)
        emit_common(ctx, io)
        emit_load_resid(ctx, io, "x")
        barrier(ctx)

        def with_ln(i, fn, needs_scope=False):
            ln_stk = ExitStack()
            ctx.prefix = f"s{i}_"
            ctx.lnT = ctx.sb("lnT", [128, ND * T], BF16, ln_stk, side="right")
            emit_ln(ctx)
            barrier(ctx)
            if needs_scope:
                fn(ln_stk)
            else:
                fn()
                ln_stk.close()
            barrier(ctx)

        with_ln(0, lambda: emit_moe(ctx, io))
        with_ln(1, lambda: emit_l0attn(ctx, io))
        with_ln(2, lambda: emit_moe(ctx, io))
        with_ln(3, lambda: emit_l1ffn(ctx, io))
        with_ln(4, lambda ln_stk: emit_l1attn(ctx, io, ln_stk), needs_scope=True)
        with_ln(5, lambda: emit_l1ffn(ctx, io))
        ctx.prefix = ""
        emit_store_resid(ctx, io, "y")
    return nc


_PROGRAMS = {}


def _get_programs():
    if not _PROGRAMS:
        _PROGRAMS["fused"] = build_fused_program()
    return _PROGRAMS


def kernel(**inputs):
    progs = _get_programs()
    iden = np.eye(128, dtype=np.float32)

    shared = {}
    shared.update(moe_host_prep(inputs))
    shared.update(l0attn_host_prep_shared(inputs))
    shared.update(l1ffn_host_prep(inputs))
    shared.update(l1attn_host_prep_shared(inputs))
    shared["iden"] = iden
    a0_pc = [l0attn_host_prep(inputs, c) for c in range(NCORES)]
    a1_pc = [l1attn_host_prep(inputs, c) for c in range(NCORES)]

    xs = _shard_tokens(np.ascontiguousarray(inputs["L0_attn_x"]))
    in_maps = [dict(shared, x=xs[c], **a0_pc[c], **a1_pc[c]) for c in range(NCORES)]
    res = run_bass_kernel_spmd(progs["fused"], in_maps, core_ids=list(range(NCORES)))
    ys = [res.results[c]["y"] for c in range(NCORES)]
    return _unshard_tokens(ys).astype(np.float32)


# version 14 (final)
# speedup vs baseline: 1.0432x; 1.0020x over previous
import numpy as np
import ml_dtypes
import concourse.bass as bass
import concourse.mybir as mybir
from concourse.bass_utils import run_bass_kernel_spmd
from contextlib import ExitStack

F32 = mybir.dt.float32
BF16 = mybir.dt.bfloat16
AF = mybir.ActivationFunctionType
ALU = mybir.AluOpType
AX = mybir.AxisListType
BFNP = ml_dtypes.bfloat16

NCORES = 8
T = 2048
NT = T // 128
D = 1024
ND = D // 128
NE = 8
EPS = 1e-5
SWI_A = 1.702
WSCALE = 0.5 / SWI_A
C_M1 = -1.0
C_MA = -SWI_A
C_A7 = SWI_A * 7.0

CONST_VALS = [C_M1, C_MA, C_A7, EPS]



class Ctx:
    def __init__(self, nc, stack, block):
        self.nc = nc
        self.stack = stack
        self.block = block
        self.semh = {}
        self.semn = {}

    def sem(self, name):
        if name not in self.semh:
            self.semh[name] = self.stack.enter_context(self.nc.semaphore(name))
            self.semn[name] = 0
        return self.semh[name]

    def dsem(self, name):
        if not hasattr(self, "dma_sems"):
            self.dma_sems = set()
        self.dma_sems.add(name)
        return self.sem(name)

    def inc(self, name, instr, k=1):
        instr.then_inc(self.sem(name), k)
        self.semn[name] += k
        return self.semn[name]

    def n(self, name):
        self.sem(name)
        return self.semn[name]

    prefix = ""

    def sb(self, name, shape, dt, stk=None, side=None):
        name = self.prefix + name
        if side:
            return (stk or self.stack).enter_context(self.nc.sbuf_tensor(name, shape, dt, side=side))
        return (stk or self.stack).enter_context(self.nc.sbuf_tensor(name, shape, dt))

    def ps(self, name, shape, dt=F32, stk=None):
        return (stk or self.stack).enter_context(self.nc.psum_tensor(self.prefix + name, shape, dt))


ENG_NAMES = ("sync", "gpsimd", "scalar", "vector", "tensor")


def barrier(ctx):
    target = ctx.n("PH") + len(ENG_NAMES)
    for en in ENG_NAMES:
        sec = getattr(ctx.block, en)

        def mk(en):
            def _(eng):
                if en == "sync":
                    for s in sorted(getattr(ctx, "dma_sems", ())):
                        eng.wait_ge(ctx.sem(s), ctx.n(s))
                ctx.inc("PH", eng.drain())
                eng.wait_ge(ctx.sem("PH"), target)
            return _
        sec(mk(en))
    ctx.semn["PH"] = target


def cst(ctx, v):
    return ctx.ctile[:, ctx.cidx[v]:ctx.cidx[v] + 1]


def emit_common(ctx, io):
    ctx.cidx = {v: i for i, v in enumerate(CONST_VALS)}
    ctx.ctile = ctx.sb("ctile", [128, len(CONST_VALS)], F32)
    ctx.iden = ctx.sb("sb_iden", [128, 128], F32)
    ctx.idenb = ctx.sb("sb_idenb", [128, 128], BF16)

    @ctx.block.gpsimd
    def _(gpsimd):
        for v, i in ctx.cidx.items():
            gpsimd.memset(ctx.ctile[:, i:i + 1], float(v))
        ctx.inc("GP", gpsimd.drain())
    ctx.gp_consts = ctx.n("GP")

    @ctx.block.sync
    def _(sync):
        ctx.inc("DCM", sync.dma_start(out=ctx.iden[:, :], in_=io["iden"][:, :]), 16)
    ctx.dsem("DCM")

    @ctx.block.gpsimd
    def _(gpsimd):
        ctx.inc("DCMG", gpsimd.dma_start(out=ctx.idenb[:, :], in_=io["iden"][:, :]), 16)
    ctx.dsem("DCMG")


def emit_load_resid(ctx, io, name="x"):
    ctx.resid = ctx.sb("resid", [128, NT * D], F32)
    src = io[name].rearrange("(t p) d -> p t d", p=128)
    dst = ctx.resid[:, :].rearrange("p (t d) -> p t d", t=NT)

    @ctx.block.sync
    def _(sync):
        ctx.inc("DRES", sync.dma_start(out=dst, in_=src), 16)
    ctx.dsem("DRES")


def emit_store_resid(ctx, io, name="y"):
    dst = io[name].rearrange("(t p) d -> p t d", p=128)
    src = ctx.resid[:, :].rearrange("p (t d) -> p t d", t=NT)

    @ctx.block.sync
    def _(sync):
        sync.wait_ge(ctx.sem("PH"), ctx.n("PH"))
        ctx.inc("ST", sync.dma_start(out=dst, in_=src), 16)
        sync.wait_ge(ctx.sem("ST"), ctx.n("ST"))



def emit_ln(ctx):
    with ExitStack() as stk:
        sq = ctx.sb("ln_sq", [128, D], F32, stk)
        S1 = ctx.sb("ln_s1", [128, NT], F32, stk)
        S2 = ctx.sb("ln_s2", [128, NT], F32, stk)
        MU = ctx.sb("ln_mu", [128, NT], F32, stk)
        T1 = ctx.sb("ln_t1", [128, NT], F32, stk)
        RS = ctx.sb("ln_rs", [128, NT], F32, stk)
        NMU = ctx.sb("ln_nmu", [128, NT], F32, stk)
        lnmaj = ctx.sb("ln_maj", [128, 4 * D], BF16, stk)
        psT = [ctx.ps(f"ln_psT{i}", [128, 1024], BF16, stk) for i in range(2)]

        @ctx.block.scalar
        def _(scalar):
            scalar.wait_ge(ctx.sem("GP"), ctx.gp_consts)
            for t in range(NT):
                scalar.activation(out=sq[:, :], in_=ctx.resid[:, t * D:(t + 1) * D],
                                  func=AF.Square, accum_out=S2[:, t:t + 1])
                scalar.drain()
            ctx.inc("SC", scalar.drain())
        sc_stats = ctx.n("SC")

        @ctx.block.vector
        def _(vector):
            for t in range(NT):
                vector.reduce_sum(S1[:, t:t + 1], ctx.resid[:, t * D:(t + 1) * D], axis=AX.X)
            vector.drain()
            vector.tensor_scalar_mul(MU[:, :], S1[:, :], 1.0 / D)
            vector.wait_ge(ctx.sem("SC"), sc_stats)
            vector.tensor_scalar_mul(S2[:, :], S2[:, :], 1.0 / D)
            vector.drain()
            vector.tensor_mul(T1[:, :], MU[:, :], MU[:, :])
            vector.drain()
            vector.tensor_sub(S2[:, :], S2[:, :], T1[:, :])
            ctx.inc("VC", vector.drain())
        vc_var = ctx.n("VC")

        @ctx.block.scalar
        def _(scalar):
            scalar.wait_ge(ctx.sem("VC"), vc_var)
            scalar.activation(out=RS[:, :], in_=S2[:, :], func=AF.Sqrt, bias=cst(ctx, EPS))
            ctx.inc("SC", scalar.drain())
        sc_std = ctx.n("SC")

        @ctx.block.vector
        def _(vector):
            vector.wait_ge(ctx.sem("SC"), sc_std)
            vector.reciprocal(T1[:, :], RS[:, :])
            vector.drain()
            vector.scalar_tensor_tensor(NMU[:, :], MU[:, :], -1.0, T1[:, :], ALU.mult, ALU.mult)
            ctx.inc("VC", vector.drain())
        vc_rstd = ctx.n("VC")

        sc_base = ctx.n("SC")
        tc_base = ctx.n("TC")
        vc_base = ctx.n("VC")
        for g4 in range(4):
            @ctx.block.scalar
            def _(scalar, g4=g4):
                if g4 == 0:
                    scalar.wait_ge(ctx.sem("VC"), vc_rstd)
                if g4 >= 1:
                    scalar.wait_ge(ctx.sem("TC"), tc_base + g4 * 8)
                for tl in range(4):
                    t = g4 * 4 + tl
                    scalar.activation(out=lnmaj[:, tl * D:(tl + 1) * D],
                                      in_=ctx.resid[:, t * D:(t + 1) * D],
                                      func=AF.Identity, scale=T1[:, t:t + 1], bias=NMU[:, t:t + 1])
                ctx.inc("SC", scalar.drain())

            @ctx.block.tensor
            def _(tensor, g4=g4):
                tensor.wait_ge(ctx.sem("SC"), sc_base + g4 + 1)
                for r in range(ND):
                    par = r % 2
                    if g4 * 8 + r >= 2:
                        tensor.wait_ge(ctx.sem("VC"), vc_base + g4 * 8 + r - 1)
                    last = None
                    for tl in range(4):
                        last = tensor.transpose(psT[par][:, tl * 128:(tl + 1) * 128],
                                                lnmaj[:, tl * D + r * 128: tl * D + (r + 1) * 128],
                                                ctx.idenb[:, :])
                    ctx.inc("TC", last)

            @ctx.block.vector
            def _(vector, g4=g4):
                for r in range(ND):
                    par = r % 2
                    vector.wait_ge(ctx.sem("TC"), tc_base + g4 * 8 + r + 1)
                    ctx.inc("VC", vector.tensor_copy(
                        ctx.lnT[:, r * T + g4 * 512: r * T + (g4 + 1) * 512], psT[par][:, 0:512]))

        @ctx.block.vector
        def _(vector):
            if getattr(ctx, "debug", False):
                vector.tensor_copy(ctx.dbg[:, 256:272], MU[:, :])
                vector.tensor_copy(ctx.dbg[:, 272:288], S2[:, :])
                vector.tensor_copy(ctx.dbg[:, 288:304], T1[:, :])
                vector.tensor_copy(ctx.dbg[:, 1024:3072], ctx.lnT[:, T:2 * T])
            ctx.inc("VC", vector.drain())
    ctx.lnT_ready = ctx.n("VC")



def emit_moe(ctx, io):
    with ExitStack() as stk:
        wrT = ctx.sb("m_wrT", [128, ND * NE], BF16, stk)
        brbc = ctx.sb("m_brbc", [128, NT * NE], F32, stk)
        bg_raw = ctx.sb("m_bgraw", [128, 128], F32, stk)
        bu_raw = ctx.sb("m_buraw", [128, 128], F32, stk)
        GB7 = ctx.sb("m_gb7", [128, 128], F32, stk)
        BU1 = ctx.sb("m_bu1", [128, 128], F32, stk)
        LG = ctx.sb("m_lg", [128, NT * NE], F32, stk)
        WK = ctx.sb("m_wk", [128, NT * NE], F32, stk)
        EQ = ctx.sb("m_eq", [128, NT * NE], F32, stk)
        KP = ctx.sb("m_kp", [128, NT * NE], F32, stk)
        EX = ctx.sb("m_ex", [128, NT * NE], F32, stk)
        Ma = ctx.sb("m_ma", [128, NT], F32, stk)
        Mt = ctx.sb("m_mt", [128, NT], F32, stk)
        M4 = ctx.sb("m_m4", [128, NT], F32, stk)
        NM1 = ctx.sb("m_nm1", [128, NT], F32, stk)
        SM = ctx.sb("m_sm", [128, NT], F32, stk)
        RC = ctx.sb("m_rc", [128, NT], F32, stk)
        WT = ctx.sb("m_wt", [128, NT * NE], F32, stk)
        wg = [ctx.sb(f"m_wg{i}", [128, ND * 128], BF16, stk) for i in range(2)]
        wu = [ctx.sb(f"m_wu{i}", [128, ND * 128], BF16, stk) for i in range(2)]
        wd = ctx.sb("m_wd", [128, 8 * D], BF16, stk)
        dbf = [ctx.sb(f"m_dbf{i}", [1, D], F32, stk) for i in range(2)]
        dbb = [ctx.sb(f"m_dbb{i}", [1, D], BF16, stk) for i in range(2)]
        onesb = ctx.sb("m_ones", [1, 128], BF16, stk)
        rg = ctx.sb("m_rg", [128, T], F32, stk)
        Ssc = ctx.sb("m_S", [128, T], F32, stk)
        u1 = ctx.sb("m_u1", [128, T], F32, stk)
        act = ctx.sb("m_act", [128, 8 * T], BF16, stk)
        psG = [ctx.ps(f"m_psG{i}", [128, 1024], F32, stk) for i in range(2)]
        psU = [ctx.ps(f"m_psU{i}", [128, 1024], F32, stk) for i in range(2)]

        lnr = ctx.lnT_ready

        @ctx.block.sync
        def _(sync):
            ctx.inc("DMS", sync.dma_start(out=brbc[:, :], in_=io["brbc"][:, :]), 16)
            ctx.inc("DMS", sync.dma_start(out=bg_raw[:, :], in_=io["bg"][:, :]), 16)
            ctx.inc("DMS", sync.dma_start(out=bu_raw[:, :], in_=io["bu"][:, :]), 16)
        ctx.dsem("DMS")
        dw_static = ctx.n("DMS")

        @ctx.block.gpsimd
        def _(gpsimd):
            gpsimd.memset(onesb[:, :], 1.0)
            ctx.inc("GP", gpsimd.drain())
            ctx.inc("DMSG", gpsimd.dma_start(
                out=wrT[:, :].rearrange("p (r e) -> p r e", r=ND),
                in_=io["wrT"].rearrange("(r p) e -> p r e", p=128)), 16)
        ctx.dsem("DMSG")
        gp_ones = ctx.n("GP")
        dg_wrT = ctx.n("DMSG")

        @ctx.block.vector
        def _(vector):
            vector.wait_ge(ctx.sem("DMS"), dw_static)
            vector.tensor_scalar(GB7[:, :], bg_raw[:, :], -1.0, 7.0, ALU.mult, ALU.add)
            vector.tensor_scalar(BU1[:, :], bu_raw[:, :], 1.0, None, ALU.add)
            ctx.inc("VC", vector.drain())
        vc_bias = ctx.n("VC")

        @ctx.block.tensor
        def _(tensor):
            tensor.wait_ge(ctx.sem("VC"), lnr)
            tensor.wait_ge(ctx.sem("DMSG"), dg_wrT)
            last = None
            for t in range(NT):
                for r in range(ND):
                    last = tensor.matmul(psG[0][:, t * 8:(t + 1) * 8],
                                         ctx.lnT[:, r * T + t * 128: r * T + (t + 1) * 128],
                                         wrT[:, r * 8:(r + 1) * 8],
                                         start=(r == 0), stop=(r == ND - 1))
            ctx.inc("TC", last)
        tc_router = ctx.n("TC")

        @ctx.block.vector
        def _(vector):
            vector.wait_ge(ctx.sem("TC"), tc_router)
            vector.wait_ge(ctx.sem("VC"), vc_bias)
            if getattr(ctx, "debug", False):
                vector.tensor_copy(ctx.dbg[:, 0:128], psG[0][:, 0:128])
                vector.drain()
            vector.tensor_add(LG[:, :], psG[0][:, 0:NT * NE], brbc[:, :])
            vector.drain()
            vector.tensor_copy(WK[:, :], LG[:, :])
            for t in range(NT):
                vector.reduce_max(Ma[:, t:t + 1], LG[:, t * 8:(t + 1) * 8], axis=AX.X)
            vector.drain()
            for k in range(3):
                msrc = Ma if k == 0 else Mt
                for t in range(NT):
                    vector.tensor_scalar(EQ[:, t * 8:(t + 1) * 8], WK[:, t * 8:(t + 1) * 8],
                                         msrc[:, t:t + 1], None, ALU.is_equal)
                vector.drain()
                vector.scalar_tensor_tensor(WK[:, :], EQ[:, :], -1e30, WK[:, :], ALU.mult, ALU.add)
                vector.drain()
                mdst = M4 if k == 2 else Mt
                for t in range(NT):
                    vector.reduce_max(mdst[:, t:t + 1], WK[:, t * 8:(t + 1) * 8], axis=AX.X)
                vector.drain()
            for t in range(NT):
                vector.tensor_scalar(KP[:, t * 8:(t + 1) * 8], LG[:, t * 8:(t + 1) * 8],
                                     M4[:, t:t + 1], None, ALU.is_ge)
            vector.tensor_scalar_mul(NM1[:, :], Ma[:, :], -1.0)
            ctx.inc("VC", vector.drain())
        vc_keep = ctx.n("VC")

        @ctx.block.scalar
        def _(scalar):
            scalar.wait_ge(ctx.sem("VC"), vc_keep)
            for t in range(NT):
                scalar.activation(out=EX[:, t * 8:(t + 1) * 8], in_=LG[:, t * 8:(t + 1) * 8],
                                  func=AF.Exp, bias=NM1[:, t:t + 1])
            ctx.inc("SC", scalar.drain())
        sc_exp = ctx.n("SC")

        @ctx.block.vector
        def _(vector):
            vector.wait_ge(ctx.sem("SC"), sc_exp)
            vector.tensor_mul(EX[:, :], EX[:, :], KP[:, :])
            vector.drain()
            for t in range(NT):
                vector.reduce_sum(SM[:, t:t + 1], EX[:, t * 8:(t + 1) * 8], axis=AX.X)
            vector.drain()
            vector.reciprocal(RC[:, :], SM[:, :])
            vector.drain()
            for t in range(NT):
                vector.tensor_scalar(WT[:, t * 8:(t + 1) * 8], EX[:, t * 8:(t + 1) * 8],
                                     RC[:, t:t + 1], WSCALE, ALU.mult, ALU.mult)
            if getattr(ctx, "debug", False):
                vector.tensor_copy(ctx.dbg[:, 128:256], LG[:, :])
                vector.tensor_copy(ctx.dbg[:, 336:464], WT[:, :])
                vector.tensor_copy(ctx.dbg[:, 304:320], M4[:, :])
                vector.tensor_copy(ctx.dbg[:, 320:336], SM[:, :])
            ctx.inc("VC", vector.drain())
        vc_wt = ctx.n("VC")

        slot_par = 0
        wgbuf_free_tc = [0, 0]
        ps_slot_free = [{"sc": 0, "vc": 0}, {"sc": 0, "vc": 0}]
        ps_pool_free_vc = vc_wt
        wd_free_tc = 0
        db_free_tc = [0, 0]
        gelems = io["gate_upT"]
        delems = io["downT"]

        first_slot = True
        for e in range(NE):
            for half in range(2):
                dg_wgwu = []
                for i in range(2):
                    ctx.dsem(f"DWG{i}")
                    ctx.dsem(f"DWU{i}")
                    ctx.dsem(f"DDB{i}")
                ctx.dsem("DWD")

                tc_base = ctx.n("TC")

                @ctx.block.gpsimd
                def _(gpsimd, e=e, half=half, tc_base=tc_base):
                    if half == 0:
                        par = e % 2
                        if db_free_tc[par]:
                            gpsimd.wait_ge(ctx.sem("TC"), db_free_tc[par])
                        ctx.inc(f"DDB{par}", gpsimd.dma_start(out=dbf[par][:, :], in_=io["db"][e:e + 1, :]), 16)
                    for j in range(8):
                        jj = half * 8 + j
                        par = j % 2
                        if j >= 2:
                            gpsimd.wait_ge(ctx.sem("TC"), tc_base + 4 * (j - 2) + 4)
                        elif wgbuf_free_tc[par]:
                            gpsimd.wait_ge(ctx.sem("TC"), wgbuf_free_tc[par])
                        ctx.inc(f"DWG{par}", gpsimd.dma_start(
                            out=wg[par][:, :].rearrange("p (r f) -> p r f", r=ND),
                            in_=gelems[e, :, jj * 128:(jj + 1) * 128]
                                .rearrange("(r p) f -> p r f", p=128)), 16)
                        ctx.inc(f"DWU{par}", gpsimd.dma_start(
                            out=wu[par][:, :].rearrange("p (r f) -> p r f", r=ND),
                            in_=gelems[e, :, 2048 + jj * 128: 2048 + (jj + 1) * 128]
                                .rearrange("(r p) f -> p r f", p=128)), 16)
                        dg_wgwu.append((ctx.n(f"DWG{par}"), ctx.n(f"DWU{par}")))
                    if wd_free_tc:
                        gpsimd.wait_ge(ctx.sem("TC"), wd_free_tc)
                    for j in range(8):
                        jj = half * 8 + j
                        ctx.inc("DWD", gpsimd.dma_start(
                            out=wd[:, j * D:(j + 1) * D],
                            in_=delems[e, jj * 128:(jj + 1) * 128, :]), 16)
                dg_wd = ctx.n("DWD")
                dg_db = ctx.n(f"DDB{e % 2}")

                for j in range(8):
                    slotinfo = []
                    for th in range(2):
                        par = slot_par % 2
                        slot_par += 1
                        slot_pool_wait = ps_pool_free_vc if (j == 0 and th == 0) else 0
                        slot_sc_wait = ps_slot_free[par]["sc"]
                        slot_vc_wait = ps_slot_free[par]["vc"]

                        @ctx.block.tensor
                        def _(tensor, j=j, th=th, par=par, dgw=dg_wgwu[j],
                              spw=slot_pool_wait, ssw=slot_sc_wait, svw=slot_vc_wait):
                            tensor.wait_ge(ctx.sem(f"DWG{j % 2}"), dgw[0])
                            tensor.wait_ge(ctx.sem(f"DWU{j % 2}"), dgw[1])
                            if spw:
                                tensor.wait_ge(ctx.sem("VC"), spw)
                            if ssw:
                                tensor.wait_ge(ctx.sem("SC"), ssw)
                            if svw:
                                tensor.wait_ge(ctx.sem("VC"), svw)
                            last = None
                            for c in range(2):
                                tok0 = th * 1024 + c * 512
                                for r in range(ND):
                                    last = tensor.matmul(
                                        psG[par][:, c * 512:(c + 1) * 512],
                                        wg[j % 2][:, r * 128:(r + 1) * 128],
                                        ctx.lnT[:, r * T + tok0: r * T + tok0 + 512],
                                        start=(r == 0), stop=(r == ND - 1))
                            ctx.inc("TC", last)
                            last = None
                            for c in range(2):
                                tok0 = th * 1024 + c * 512
                                for r in range(ND):
                                    last = tensor.matmul(
                                        psU[par][:, c * 512:(c + 1) * 512],
                                        wu[j % 2][:, r * 128:(r + 1) * 128],
                                        ctx.lnT[:, r * T + tok0: r * T + tok0 + 512],
                                        start=(r == 0), stop=(r == ND - 1))
                            ctx.inc("TC", last)
                        slotinfo.append((par, ctx.n("TC") - 1, ctx.n("TC")))
                        if th == 1:
                            wgbuf_free_tc[j % 2] = ctx.n("TC")

                    jj = half * 8 + j
                    bcol = e * 16 + jj
                    sc_of_th = []

                    @ctx.block.scalar
                    def _(scalar, bcol=bcol, slotinfo=list(slotinfo), fs=first_slot):
                        if fs:
                            scalar.wait_ge(ctx.sem("VC"), vc_bias)
                        for th, (par, tcg, tcu) in enumerate(slotinfo):
                            scalar.wait_ge(ctx.sem("TC"), tcg)
                            scalar.activation(out=rg[:, th * 1024:(th + 1) * 1024],
                                              in_=psG[par][:, :],
                                              func=AF.Relu, scale=cst(ctx, C_M1),
                                              bias=GB7[:, bcol:bcol + 1])
                            scalar.drain()
                            scalar.activation(out=Ssc[:, th * 1024:(th + 1) * 1024],
                                              in_=rg[:, th * 1024:(th + 1) * 1024],
                                              func=AF.Silu, scale=cst(ctx, C_MA),
                                              bias=cst(ctx, C_A7))
                            ctx.inc("SC", scalar.drain())
                            sc_of_th.append(ctx.n("SC"))
                    first_slot = False

                    for th in range(2):
                        par, tcg, tcu = slotinfo[th]
                        ps_slot_free[par]["sc"] = sc_of_th[th]

                        @ctx.block.vector
                        def _(vector, par=par, th=th, j=j, bcol=bcol,
                              tcu=tcu, scs=sc_of_th[th]):
                            vector.wait_ge(ctx.sem("TC"), tcu)
                            vector.tensor_scalar(u1[:, th * 1024:(th + 1) * 1024],
                                                 psU[par][:, :],
                                                 BU1[:, bcol:bcol + 1], 8.0, ALU.add, ALU.min)
                            vector.drain()
                            vector.wait_ge(ctx.sem("SC"), scs)
                            vector.scalar_tensor_tensor(
                                act[:, j * T + th * 1024: j * T + (th + 1) * 1024],
                                u1[:, th * 1024:(th + 1) * 1024], -6.0,
                                Ssc[:, th * 1024:(th + 1) * 1024], ALU.max, ALU.mult)
                            ctx.inc("VC", vector.drain())
                        ps_slot_free[par]["vc"] = ctx.n("VC")

                if half == 0:
                    @ctx.block.vector
                    def _(vector, e=e, dg_db=dg_db):
                        par = e % 2
                        vector.wait_ge(ctx.sem(f"DDB{e % 2}"), dg_db)
                        vector.tensor_scalar_mul(dbf[par][:, :], dbf[par][:, :], SWI_A)
                        vector.drain()
                        vector.tensor_copy(dbb[par][:, :], dbf[par][:, :])
                        ctx.inc("VC", vector.drain())
                vc_db = ctx.n("VC")

                vc_act_done = ctx.n("VC")
                vc_evac_base = ctx.n("VC")
                tc_down = []

                @ctx.block.tensor
                def _(tensor, e=e, half=half, vc_act_done=vc_act_done,
                      dg_wd=dg_wd, vc_db=vc_db, base=vc_evac_base):
                    tensor.wait_ge(ctx.sem("VC"), vc_act_done)
                    tensor.wait_ge(ctx.sem("DWD"), dg_wd)
                    if half == 0:
                        tensor.wait_ge(ctx.sem("VC"), vc_db)
                    for c in range(NT):
                        par = c % 2
                        if c >= 2:
                            tensor.wait_ge(ctx.sem("VC"), base + (c - 2) + 1)
                        psD = psG[par]
                        first = True
                        if half == 0:
                            tensor.matmul(psD[:, 0:512], onesb[0:1, :], dbb[e % 2][0:1, 0:512],
                                          start=True, stop=False)
                            tensor.matmul(psD[:, 512:1024], onesb[0:1, :], dbb[e % 2][0:1, 512:1024],
                                          start=True, stop=False)
                            first = False
                        last = None
                        for j in range(8):
                            a = act[:, j * T + c * 128: j * T + (c + 1) * 128]
                            last = tensor.matmul(psD[:, 0:512], a, wd[:, j * D: j * D + 512],
                                                 start=first, stop=(j == 7))
                            last = tensor.matmul(psD[:, 512:1024], a, wd[:, j * D + 512: (j + 1) * D],
                                                 start=first, stop=(j == 7))
                            first = False
                        ctx.inc("TC", last)
                        tc_down.append(ctx.n("TC"))
                tc_down_end = ctx.n("TC")
                wd_free_tc = tc_down_end
                if half == 0:
                    db_free_tc[e % 2] = tc_down_end

                @ctx.block.vector
                def _(vector, e=e, tc_down=list(tc_down)):
                    vector.drain()
                    for c in range(NT):
                        par = c % 2
                        vector.wait_ge(ctx.sem("TC"), tc_down[c])
                        ctx.inc("VC", vector.scalar_tensor_tensor(
                            ctx.resid[:, c * D:(c + 1) * D],
                            psG[par][:, :], WT[:, c * 8 + e: c * 8 + e + 1],
                            ctx.resid[:, c * D:(c + 1) * D], ALU.mult, ALU.add))
                assert ctx.n("VC") == vc_evac_base + NT
                ps_pool_free_vc = ctx.n("VC")
                ps_slot_free = [{"sc": 0, "vc": 0}, {"sc": 0, "vc": 0}]

        @ctx.block.vector
        def _(vector):
            ctx.inc("VC", vector.drain())



def emit_l1ffn(ctx, io):
    with ExitStack() as stk:
        u1 = ctx.sb("f_u1", [128, ND * 256], BF16, stk)
        v1 = ctx.sb("f_v1", [128, 2 * 4096], BF16, stk)
        u2 = ctx.sb("f_u2", [128, 32 * 256], BF16, stk)
        v2 = ctx.sb("f_v2", [128, 2 * 1024], BF16, stk)
        b1s = ctx.sb("f_b1", [128, 32], F32, stk)
        b2bc = ctx.sb("f_b2bc", [128, 1024], F32, stk)
        Pbf = ctx.sb("f_P", [128, 1024], BF16, stk)
        Qbf = ctx.sb("f_Q", [128, 1024], BF16, stk)
        Hbf = ctx.sb("f_H", [128, 32 * 512], BF16, stk)
        osc = ctx.sb("f_osc", [128, 1024], F32, stk)
        psP = [ctx.ps(f"f_psP{i}", [128, 512], F32, stk) for i in range(2)]
        psH = [ctx.ps(f"f_psH{i}", [128, 512], F32, stk) for i in range(2)]
        psQ = [ctx.ps(f"f_psQ{i}", [128, 512], F32, stk) for i in range(2)]
        psO = ctx.ps("f_psO", [128, 1024], F32, stk)

        lnr = ctx.lnT_ready

        @ctx.block.gpsimd
        def _(gpsimd):
            ctx.inc("DFW", gpsimd.dma_start(
                out=u1[:, :].rearrange("p (rd r) -> p rd r", rd=ND),
                in_=io["u1"].rearrange("(rd p) r -> p rd r", p=128)), 16)
            ctx.inc("DFW", gpsimd.dma_start(
                out=v1[:, :].rearrange("p (rr f) -> p rr f", rr=2),
                in_=io["v1"].rearrange("(rr p) f -> p rr f", p=128)), 16)
            ctx.inc("DFW", gpsimd.dma_start(
                out=u2[:, :].rearrange("p (ft r) -> p ft r", ft=32),
                in_=io["u2"].rearrange("(ft p) r -> p ft r", p=128)), 16)
            ctx.inc("DFW", gpsimd.dma_start(
                out=v2[:, :].rearrange("p (rr d) -> p rr d", rr=2),
                in_=io["v2"].rearrange("(rr p) d -> p rr d", p=128)), 16)
        ctx.dsem("DFW")
        dg_w = ctx.n("DFW")

        @ctx.block.sync
        def _(sync):
            ctx.inc("DFS", sync.dma_start(out=b1s[:, :], in_=io["b1t"][:, :]), 16)
            ctx.inc("DFS", sync.dma_start(out=b2bc[:, :], in_=io["b2bc"][:, :]), 16)
        ctx.dsem("DFS")
        dw_b = ctx.n("DFS")

        for tg in range(4):
            @ctx.block.tensor
            def _(tensor, tg=tg, vce=ctx.n("VC")):
                if tg == 0:
                    tensor.wait_ge(ctx.sem("VC"), lnr)
                    tensor.wait_ge(ctx.sem("DFW"), dg_w)
                else:
                    tensor.wait_ge(ctx.sem("VC"), vce)
                for rr in range(2):
                    last = None
                    for rd in range(ND):
                        last = tensor.matmul(
                            psP[rr][:, :], u1[:, rd * 256 + rr * 128: rd * 256 + (rr + 1) * 128],
                            ctx.lnT[:, rd * T + tg * 512:(rd * T + (tg + 1) * 512)],
                            start=(rd == 0), stop=(rd == ND - 1))
                    ctx.inc("TC", last)
            tc_p = ctx.n("TC")

            @ctx.block.vector
            def _(vector, tc_p=tc_p):
                vector.wait_ge(ctx.sem("TC"), tc_p - 1)
                vector.tensor_copy(Pbf[:, 0:512], psP[0][:, :])
                vector.wait_ge(ctx.sem("TC"), tc_p)
                vector.tensor_copy(Pbf[:, 512:1024], psP[1][:, :])
                ctx.inc("VC", vector.drain())
            vc_p = ctx.n("VC")

            tc_h0 = ctx.n("TC")
            sc_h0 = ctx.n("SC")

            @ctx.block.tensor
            def _(tensor, vc_p=vc_p, tc_h0=tc_h0, sc_h0=sc_h0):
                tensor.wait_ge(ctx.sem("VC"), vc_p)
                for ft in range(32):
                    par = ft % 2
                    if ft >= 2:
                        tensor.wait_ge(ctx.sem("SC"), sc_h0 + ft - 1)
                    last = None
                    for rr in range(2):
                        last = tensor.matmul(psH[par][:, :],
                                             v1[:, rr * 4096 + ft * 128: rr * 4096 + (ft + 1) * 128],
                                             Pbf[:, rr * 512:(rr + 1) * 512],
                                             start=(rr == 0), stop=(rr == 1))
                    ctx.inc("TC", last)

            @ctx.block.scalar
            def _(scalar, tg=tg, tc_h0=tc_h0, dw_b=dw_b):
                if tg == 0:
                    scalar.wait_ge(ctx.sem("DFS"), dw_b)
                for ft in range(32):
                    par = ft % 2
                    scalar.wait_ge(ctx.sem("TC"), tc_h0 + ft + 1)
                    scalar.activation(out=Hbf[:, ft * 512:(ft + 1) * 512], in_=psH[par][:, :],
                                      func=AF.Gelu_apprx_tanh, bias=b1s[:, ft:ft + 1])
                    ctx.inc("SC", scalar.drain())
            sc_h = ctx.n("SC")

            @ctx.block.tensor
            def _(tensor, sc_h=sc_h):
                tensor.wait_ge(ctx.sem("SC"), sc_h)
                for rr in range(2):
                    last = None
                    for ft in range(32):
                        last = tensor.matmul(
                            psQ[rr][:, :], u2[:, ft * 256 + rr * 128: ft * 256 + (rr + 1) * 128],
                            Hbf[:, ft * 512:(ft + 1) * 512],
                            start=(ft == 0), stop=(ft == 31))
                    ctx.inc("TC", last)
            tc_q = ctx.n("TC")

            @ctx.block.vector
            def _(vector, tc_q=tc_q):
                vector.wait_ge(ctx.sem("TC"), tc_q - 1)
                vector.tensor_copy(Qbf[:, 0:512], psQ[0][:, :])
                vector.wait_ge(ctx.sem("TC"), tc_q)
                vector.tensor_copy(Qbf[:, 512:1024], psQ[1][:, :])
                ctx.inc("VC", vector.drain())
            vc_q = ctx.n("VC")

            vc_o0 = ctx.n("VC")
            for tt in range(4):
                @ctx.block.tensor
                def _(tensor, tt=tt, vc_q=vc_q, vc_o0=vc_o0):
                    if tt == 0:
                        tensor.wait_ge(ctx.sem("VC"), vc_q)
                    else:
                        tensor.wait_ge(ctx.sem("VC"), vc_o0 + 2 * tt - 1)
                    last = None
                    for dh in range(2):
                        for rr in range(2):
                            last = tensor.matmul(
                                psO[:, dh * 512:(dh + 1) * 512],
                                Qbf[:, rr * 512 + tt * 128: rr * 512 + (tt + 1) * 128],
                                v2[:, rr * 1024 + dh * 512: rr * 1024 + (dh + 1) * 512],
                                start=(rr == 0), stop=(rr == 1))
                    ctx.inc("TC", last)
                tc_o = ctx.n("TC")

                @ctx.block.vector
                def _(vector, tg=tg, tt=tt, tc_o=tc_o):
                    vector.wait_ge(ctx.sem("TC"), tc_o)
                    ctx.inc("VC", vector.tensor_add(osc[:, :], psO[:, :], b2bc[:, :]))
                    vector.drain()
                    t = tg * 4 + tt
                    vector.scalar_tensor_tensor(
                        ctx.resid[:, t * D:(t + 1) * D], osc[:, :], 0.5,
                        ctx.resid[:, t * D:(t + 1) * D], ALU.mult, ALU.add)
                    ctx.inc("VC", vector.drain())

        @ctx.block.vector
        def _(vector):
            ctx.inc("VC", vector.drain())


def l1ffn_host_prep(inp):
    b1 = inp["L1_ffn_b1"]
    b2 = inp["L1_ffn_b2"]
    return dict(
        u1=np.ascontiguousarray(inp["L1_ffn_U1"]),
        v1=np.ascontiguousarray(inp["L1_ffn_V1"]),
        u2=np.ascontiguousarray(inp["L1_ffn_U2"]),
        v2=np.ascontiguousarray(inp["L1_ffn_V2"]),
        b1t=np.ascontiguousarray(b1.reshape(32, 128).T),
        b2bc=np.ascontiguousarray(np.broadcast_to(b2, (128, 1024))),
    )


def build_l1ffn_program():
    nc = bass.Bass("TRN2", target_bir_lowering=False)
    io = {}
    io["x"] = nc.dram_tensor("x", [T, D], F32, kind="ExternalInput").ap()
    io["u1"] = nc.dram_tensor("u1", [D, 256], F32, kind="ExternalInput").ap()
    io["v1"] = nc.dram_tensor("v1", [256, 4096], F32, kind="ExternalInput").ap()
    io["u2"] = nc.dram_tensor("u2", [4096, 256], F32, kind="ExternalInput").ap()
    io["v2"] = nc.dram_tensor("v2", [256, 1024], F32, kind="ExternalInput").ap()
    io["b1t"] = nc.dram_tensor("b1t", [128, 32], F32, kind="ExternalInput").ap()
    io["b2bc"] = nc.dram_tensor("b2bc", [128, 1024], F32, kind="ExternalInput").ap()
    io["iden"] = nc.dram_tensor("iden", [128, 128], F32, kind="ExternalInput").ap()
    io["y"] = nc.dram_tensor("y", [T, D], F32, kind="ExternalOutput").ap()

    with ExitStack() as stack:
        block = stack.enter_context(nc.Block())
        ctx = Ctx(nc, stack, block)
        ctx.debug = False
        for s in ("PH", "GP", "SC", "VC", "TC", "ST"):
            ctx.sem(s)
        emit_common(ctx, io)
        ctx.lnT = ctx.sb("lnT", [128, ND * T], BF16)
        emit_load_resid(ctx, io, "x")
        barrier(ctx)
        emit_ln(ctx)
        barrier(ctx)
        emit_l1ffn(ctx, io)
        barrier(ctx)
        emit_store_resid(ctx, io, "y")
    return nc



GROUPS = [[0, 1, 2, 3], [4, 5, 6, 7]]


def emit_l0attn(ctx, io):
    with ExitStack() as stk:
        msk = ctx.sb("a_msk", [128, 128], F32, stk)
        lnp = ctx.sb("a_lnp", [64, 4], F32, stk)
        kgbc = ctx.sb("a_kgbc", [128, 64], F32, stk)
        kbbc = ctx.sb("a_kbbc", [128, 64], F32, stk)
        sel_sb = ctx.sb("a_sel", [64, 16], F32, stk)
        wbuf = ctx.sb("a_wbuf", [128, ND * 512], BF16, stk)
        wosb = ctx.sb("a_wosb", [128, 4 * 1024], BF16, stk)
        oTh = ctx.sb("a_oT", [128, 4 * 2048], BF16, stk)
        oS = ctx.sb("a_oS", [64, 2048], BF16, stk)
        qn = ctx.sb("a_qn", [128, 16 * 512], BF16, stk)
        kbf = ctx.sb("a_kbf", [128, 16 * 512], BF16, stk)
        vbf = ctx.sb("a_vbf", [128, 16 * 512], BF16, stk)
        qT_h = ctx.sb("a_qTh", [64, 2048], BF16, stk)
        kT_h = ctx.sb("a_kTh", [64, 2048], BF16, stk)
        ST1 = [ctx.sb(f"a_st{i}", [128, 128], BF16, stk) for i in range(2)]
        Gl = ctx.sb("a_gl", [64, 8 * 64], BF16, stk)
        Pb = [ctx.sb(f"a_pb{i}", [64, 64], BF16, stk) for i in range(2)]
        G_sb = ctx.sb("a_G", [64, 1024], F32, stk)
        gall = ctx.sb("a_gall", [64, 1024], F32, stk)
        Pfx = G_sb
        sqs = ctx.sb("a_sqs", [128, 512], BF16, stk)
        nscr = ctx.sb("a_nscr", [128, 512], F32, stk)
        tscr = ctx.sb("a_tscr", [128, 64], F32, stk)
        QS1 = ctx.sb("a_qs1", [128, 16], F32, stk)
        QS2 = ctx.sb("a_qs2", [128, 16], F32, stk)
        QMU = ctx.sb("a_qmu", [128, 16], F32, stk)
        QT1 = ctx.sb("a_qt1", [128, 16], F32, stk)
        QNM = ctx.sb("a_qnm", [128, 16], F32, stk)
        prj = [ctx.ps(f"a_prj{i}", [128, 512], F32, stk) for i in range(2)]
        trp = [ctx.ps(f"a_trp{i}", [128, 1024], BF16, stk) for i in range(2)]
        scp = [ctx.ps(f"a_scp{i}", [128, 512], F32, stk) for i in range(2)]
        opp = [ctx.ps(f"a_opp{i}", [128, 512], F32, stk) for i in range(2)]

        lnr = ctx.lnT_ready

        @ctx.block.sync
        def _(sync):
            ctx.inc("DAS", sync.dma_start(out=msk[:, :], in_=io["msk_lin"][:, :]), 16)
            ctx.inc("DAS", sync.dma_start(out=lnp[:, :], in_=io["qklnp"][:, :]), 16)
            ctx.inc("DAS", sync.dma_start(out=kgbc[:, :], in_=io["kgbc"][:, :]), 16)
            ctx.inc("DAS", sync.dma_start(out=kbbc[:, :], in_=io["kbbc"][:, :]), 16)
            ctx.inc("DAS", sync.dma_start(out=sel_sb[:, :], in_=io["sel"][:, :]), 16)
        ctx.dsem("DAS")
        dw_as = ctx.n("DAS")

        ctx.dsem("DQW")
        ctx.dsem("DAW")
        ctx.dsem("DGX")
        ctx.dsem("DGA")
        ctx.dsem("DOT")

        prj_war = [(0, 0), (0, 0)]
        prj_evac_free = [0, 0]
        nscr_free_vc = 0
        wo_free_tc = 0
        oS_free_dot = 0

        for half in range(2):
            hoff = half * 512
            proj_t0 = ctx.n("TC")

            for wi, wname in ((0, "wqT"), (1, "wkT"), (2, "wvT")):
                war_tc = ctx.wbuf_free_tc

                @ctx.block.gpsimd
                def _(gpsimd, half=half, wname=wname, war_tc=war_tc):
                    if war_tc:
                        gpsimd.wait_ge(ctx.sem("TC"), war_tc)
                    ctx.inc("DQW", gpsimd.dma_start(
                        out=wbuf[:, :].rearrange("p (rd f) -> p rd f", rd=ND),
                        in_=io[wname][:, half * 512:(half + 1) * 512]
                            .rearrange("(rd p) f -> p rd f", p=128)), 16)
                dg_w = ctx.n("DQW")

                for tt in range(NT):
                    par = (wi * NT + tt) % 2

                    @ctx.block.tensor
                    def _(tensor, tt=tt, wi=wi, par=par, dg_w=dg_w, lnr=lnr,
                          pw=tuple(prj_war[par])):
                        if tt == 0:
                            if wi == 0 and half == 0:
                                tensor.wait_ge(ctx.sem("VC"), lnr)
                            tensor.wait_ge(ctx.sem("DQW"), dg_w)
                        if pw[0]:
                            tensor.wait_ge(ctx.sem("SC"), pw[0])
                        if pw[1]:
                            tensor.wait_ge(ctx.sem("VC"), pw[1])
                        last = None
                        for rd in range(ND):
                            last = tensor.matmul(
                                prj[par][:, :],
                                ctx.lnT[:, rd * T + tt * 128: rd * T + (tt + 1) * 128],
                                wbuf[:, rd * 512:(rd + 1) * 512],
                                start=(rd == 0), stop=(rd == ND - 1))
                        ctx.inc("TC", last)
                    tc_prj = ctx.n("TC")

                    if wi == 2:
                        @ctx.block.vector
                        def _(vector, tt=tt, par=par, tc_prj=tc_prj):
                            vector.wait_ge(ctx.sem("TC"), tc_prj)
                            vector.tensor_copy(vbf[:, tt * 512:(tt + 1) * 512], prj[par][:, :])
                            ctx.inc("VC", vector.drain())
                        prj_war[par] = (0, ctx.n("VC"))
                    else:
                        c0 = par * 8

                        @ctx.block.scalar
                        def _(scalar, par=par, c0=c0, tc_prj=tc_prj,
                              first=(half == 0 and wi == 0 and tt == 0)):
                            if first:
                                scalar.wait_ge(ctx.sem("DAS"), dw_as)
                            scalar.wait_ge(ctx.sem("TC"), tc_prj)
                            for hh in range(8):
                                scalar.activation(out=sqs[:, hh * 64:(hh + 1) * 64],
                                                  in_=prj[par][:, hh * 64:(hh + 1) * 64],
                                                  func=AF.Square,
                                                  accum_out=QS2[:, c0 + hh: c0 + hh + 1])
                            ctx.inc("SC", scalar.drain())
                        sc_st = ctx.n("SC")

                        @ctx.block.vector
                        def _(vector, par=par, c0=c0, tc_prj=tc_prj, sc_st=sc_st):
                            vector.wait_ge(ctx.sem("TC"), tc_prj)
                            for hh in range(8):
                                vector.reduce_sum(QS1[:, c0 + hh: c0 + hh + 1],
                                                  prj[par][:, hh * 64:(hh + 1) * 64], axis=AX.X)
                            vector.drain()
                            vector.tensor_scalar_mul(QMU[:, c0:c0 + 8], QS1[:, c0:c0 + 8], 1.0 / 64)
                            vector.wait_ge(ctx.sem("SC"), sc_st)
                            vector.tensor_scalar_mul(QS2[:, c0:c0 + 8], QS2[:, c0:c0 + 8], 1.0 / 64)
                            vector.drain()
                            vector.tensor_mul(QT1[:, c0:c0 + 8], QMU[:, c0:c0 + 8], QMU[:, c0:c0 + 8])
                            vector.drain()
                            vector.tensor_sub(QS2[:, c0:c0 + 8], QS2[:, c0:c0 + 8], QT1[:, c0:c0 + 8])
                            ctx.inc("VC", vector.drain())
                        vc_var = ctx.n("VC")

                        sc_sd = ctx.n("SC") + 1
                        vc_rs = ctx.n("VC") + 1

                        @ctx.block.scalar
                        def _(scalar, tt=tt, wi=wi, par=par, c0=c0, vc_var=vc_var,
                              vc_rs=vc_rs, nfw=nscr_free_vc):
                            scalar.wait_ge(ctx.sem("VC"), vc_var)
                            scalar.activation(out=QT1[:, c0:c0 + 8], in_=QS2[:, c0:c0 + 8],
                                              func=AF.Sqrt, bias=cst(ctx, EPS))
                            ctx.inc("SC", scalar.drain())
                            scalar.wait_ge(ctx.sem("VC"), vc_rs)
                            if wi == 1 and nfw:
                                scalar.wait_ge(ctx.sem("VC"), nfw)
                            for hh in range(8):
                                if wi == 0:
                                    out = qn[:, tt * 512 + hh * 64: tt * 512 + (hh + 1) * 64]
                                else:
                                    out = nscr[:, hh * 64:(hh + 1) * 64]
                                scalar.activation(out=out, in_=prj[par][:, hh * 64:(hh + 1) * 64],
                                                  func=AF.Identity,
                                                  scale=QS1[:, c0 + hh: c0 + hh + 1],
                                                  bias=QNM[:, c0 + hh: c0 + hh + 1])
                            ctx.inc("SC", scalar.drain())
                        assert ctx.n("SC") == sc_sd + 1

                        @ctx.block.vector
                        def _(vector, c0=c0, sc_sd=sc_sd):
                            vector.wait_ge(ctx.sem("SC"), sc_sd)
                            vector.reciprocal(QS1[:, c0:c0 + 8], QT1[:, c0:c0 + 8])
                            vector.drain()
                            vector.scalar_tensor_tensor(QNM[:, c0:c0 + 8], QMU[:, c0:c0 + 8], -1.0,
                                                        QS1[:, c0:c0 + 8], ALU.mult, ALU.mult)
                            ctx.inc("VC", vector.drain())
                        assert ctx.n("VC") == vc_rs
                        sc_nm = ctx.n("SC")
                        prj_evac_free[par] = sc_nm
                        prj_war[par] = (sc_nm, vc_var)

                        if wi == 1:
                            @ctx.block.vector
                            def _(vector, tt=tt, sc_nm=sc_nm):
                                vector.wait_ge(ctx.sem("SC"), sc_nm)
                                for hh in range(8):
                                    vector.tensor_mul(tscr[:, :], nscr[:, hh * 64:(hh + 1) * 64], kgbc[:, :])
                                    vector.drain()
                                    vector.tensor_add(kbf[:, tt * 512 + hh * 64: tt * 512 + (hh + 1) * 64],
                                                      tscr[:, :], kbbc[:, :])
                                    vector.drain()
                                ctx.inc("VC", vector.drain())
                            nscr_free_vc = ctx.n("VC")
                ctx.wbuf_free_tc = ctx.n("TC")

            vc_proj = ctx.n("VC")
            sc_proj = ctx.n("SC")

            @ctx.block.gpsimd
            def _(gpsimd, half=half, wtc=wo_free_tc):
                if wtc:
                    gpsimd.wait_ge(ctx.sem("TC"), wtc)
                ctx.inc("DAW", gpsimd.dma_start(
                    out=wosb[:, :].rearrange("p (m d) -> p m d", m=4),
                    in_=io["woT"][half * 512:(half + 1) * 512, :]
                        .rearrange("(m p) d -> p m d", p=128)), 16)
            dg_wo = ctx.n("DAW")

            @ctx.block.tensor
            def _(tensor, vc_proj=vc_proj, sc_proj=sc_proj):
                tensor.wait_ge(ctx.sem("VC"), vc_proj)
                tensor.wait_ge(ctx.sem("SC"), sc_proj)
                last = None
                for i in range(16):
                    hh, blk = i // 2, i % 2
                    ph, cg = i // 8, i % 8
                    for cl in range(8):
                        col = (blk * 8 + cl) * 512 + hh * 64
                        last = tensor.matmul(
                            opp[0][ph * 64:(ph + 1) * 64, cg * 64:(cg + 1) * 64],
                            kbf[:, col:col + 64], vbf[:, col:col + 64],
                            start=(cl == 0), stop=(cl == 7))
                ctx.inc("TC", last)
            tc_g = ctx.n("TC")

            @ctx.block.vector
            def _(vector, tc_g=tc_g):
                vector.wait_ge(ctx.sem("TC"), tc_g)
                vector.tensor_copy(G_sb[:, 0:512], opp[0][0:64, 0:512])
                vector.drain()
                vector.tensor_copy(G_sb[:, 512:1024], opp[0][64:128, 0:512])
                ctx.inc("VC", vector.drain())
            vc_g = ctx.n("VC")

            @ctx.block.sync
            def _(sync, vc_g=vc_g):
                sync.wait_ge(ctx.sem("VC"), vc_g)
                ctx.inc("DGX", sync.dma_start(out=io["g_loc"][:, :], in_=G_sb[:, :]), 16)
            dgx = ctx.n("DGX")

            @ctx.block.gpsimd
            def _(gpsimd, dgx=dgx):
                gpsimd.wait_ge(ctx.sem("DGX"), dgx)
                ctx.inc("CC", gpsimd.collective_compute(
                    "AllGather", ALU.bypass, replica_groups=GROUPS,
                    ins=[io["g_loc"].opt()], outs=[io["g_all"].opt()]))
            cc_n = ctx.n("CC")

            @ctx.block.gpsimd
            def _(gpsimd, pfw=getattr(ctx, "pfx_free_vc", 0), dgx=dgx):
                if pfw:
                    gpsimd.wait_ge(ctx.sem("VC"), pfw)
                gpsimd.wait_ge(ctx.sem("DGX"), dgx)
                gpsimd.memset(Pfx[:, :], 0.0)
                ctx.inc("GP", gpsimd.drain())
            gp_pfx = ctx.n("GP")

            vc_pfx_base = ctx.n("VC")
            for r in range(4):
                @ctx.block.sync
                def _(sync, r=r, cc_n=cc_n, base=vc_pfx_base):
                    if r == 0:
                        sync.wait_ge(ctx.sem("CC"), cc_n)
                    else:
                        sync.wait_ge(ctx.sem("VC"), base + r)
                    ctx.inc("DGA", sync.dma_start(out=gall[:, :],
                                                  in_=io["g_all"][r * 64:(r + 1) * 64, :]), 16)
                dga = ctx.n("DGA")

                @ctx.block.vector
                def _(vector, r=r, dga=dga, gp_pfx=gp_pfx):
                    vector.wait_ge(ctx.sem("DGA"), dga)
                    if r == 0:
                        vector.wait_ge(ctx.sem("GP"), gp_pfx)
                    for sub in range(2):
                        sgb = r if sub == 0 else 7 - r
                        for hh in range(8):
                            for blk in range(2):
                                pcol = (hh * 2 + blk) * 64
                                vector.scalar_tensor_tensor(
                                    Pfx[:, pcol:pcol + 64],
                                    gall[:, hh * 128 + sub * 64: hh * 128 + (sub + 1) * 64],
                                    sel_sb[:, sgb * 2 + blk: sgb * 2 + blk + 1],
                                    Pfx[:, pcol:pcol + 64], ALU.mult, ALU.add)
                        vector.drain()
                    ctx.inc("VC", vector.drain())
            vc_pfx = ctx.n("VC")
            ctx.pfx_free_vc = 0

            prev_head_t3 = getattr(ctx, "l0_prev_head_t3", 0)
            t3_hist = getattr(ctx, "l0_t3_hist", {})
            ctx.l0_t3_hist = t3_hist
            for hh in range(8):
                h = half * 8 + hh
                mloc = hh // 2
                odd = hh % 2

                head_t0 = ctx.n("TC")

                head_sc0 = ctx.n("SC")

                @ctx.block.tensor
                def _(tensor, hh=hh, sc_prev=ctx.n("SC"), head_sc0=head_sc0,
                      wvp=(vc_proj if hh == 0 else 0)):
                    tensor.wait_ge(ctx.sem("SC"), sc_prev)
                    if wvp:
                        tensor.wait_ge(ctx.sem("VC"), wvp)
                    i = 0
                    for si, srcb in ((0, qn), (1, kbf)):
                        for q4 in range(4):
                            tpar = i % 2
                            if i >= 2:
                                tensor.wait_ge(ctx.sem("SC"), head_sc0 + i - 1)
                            last = None
                            for tl in range(4):
                                tt = q4 * 4 + tl
                                last = tensor.transpose(
                                    trp[tpar][0:64, tl * 128:(tl + 1) * 128],
                                    srcb[:, tt * 512 + hh * 64: tt * 512 + (hh + 1) * 64],
                                    ctx.idenb[:, :])
                            ctx.inc("TC", last)
                            i += 1

                @ctx.block.scalar
                def _(scalar, hh=hh, head_t0=head_t0):
                    i = 0
                    for si, dst in ((0, qT_h), (1, kT_h)):
                        for q4 in range(4):
                            tpar = i % 2
                            i += 1
                            scalar.wait_ge(ctx.sem("TC"), head_t0 + i)
                            if si == 0:
                                scalar.activation(out=dst[:, q4 * 512:(q4 + 1) * 512],
                                                  in_=trp[tpar][0:64, 0:512], func=AF.Identity,
                                                  scale=lnp[:, 0:1], bias=lnp[:, 1:2])
                            else:
                                scalar.activation(out=dst[:, q4 * 512:(q4 + 1) * 512],
                                                  in_=trp[tpar][0:64, 0:512], func=AF.Identity)
                            ctx.inc("SC", scalar.drain())
                sc_tr = ctx.n("SC")

                sT_ready_vc = {}
                t3_tc = {}
                oev_sc = {}
                gl_vc = {}

                for c in range(16):
                    blk, cl = c // 8, c % 8
                    cpar = c % 2

                    if cl == 0:
                        @ctx.block.vector
                        def _(vector, hh=hh, blk=blk, c=c, sc_tr=sc_tr,
                              w1=t3_tc.get(c - 1, prev_head_t3), vc_pfx=vc_pfx):
                            if blk == 0:
                                vector.wait_ge(ctx.sem("SC"), sc_tr)
                                if hh == 0:
                                    vector.wait_ge(ctx.sem("VC"), vc_pfx)
                            if w1:
                                vector.wait_ge(ctx.sem("TC"), w1)
                            pcol = (hh * 2 + blk) * 64
                            vector.tensor_copy(Pb[blk][:, :], Pfx[:, pcol:pcol + 64])
                            ctx.inc("VC", vector.drain())

                    vc_sT = ctx.n("VC") + 1
                    sT_ready_vc[c] = vc_sT
                    tc_0 = ctx.n("TC")
                    t1 = tc_0 + 1
                    t2 = tc_0 + 2 if cl != 7 else None
                    t3 = tc_0 + (3 if cl != 7 else 2)

                    @ctx.block.tensor
                    def _(tensor, hh=hh, c=c, cl=cl, blk=blk, cpar=cpar, sc_tr=sc_tr,
                          wsc=sT_ready_vc.get(c - 2, 0),
                          wgl=gl_vc.get(c - 2, 0),
                          wop=oev_sc.get(c - 2, 0),
                          glv=gl_vc.get(c - 1, 0) if cl >= 1 else 0,
                          vc_sT=vc_sT):
                        col = c * 512 + hh * 64
                        if c == 0:
                            tensor.wait_ge(ctx.sem("SC"), sc_tr)
                        if wsc:
                            tensor.wait_ge(ctx.sem("VC"), wsc)
                        ctx.inc("TC", tensor.matmul(scp[cpar][0:128, 0:128],
                                                    kT_h[:, c * 128:(c + 1) * 128],
                                                    qT_h[:, c * 128:(c + 1) * 128],
                                                    start=True, stop=True))
                        if cl != 7:
                            if wgl:
                                tensor.wait_ge(ctx.sem("VC"), wgl)
                            ctx.inc("TC", tensor.matmul(prj[cpar][0:64, 0:64],
                                                        kbf[:, col:col + 64], vbf[:, col:col + 64],
                                                        start=True, stop=True))
                        tensor.wait_ge(ctx.sem("VC"), vc_sT)
                        if glv:
                            tensor.wait_ge(ctx.sem("VC"), glv)
                        if wop:
                            tensor.wait_ge(ctx.sem("VC"), wop)
                        tensor.matmul(opp[cpar][0:64, 0:128], vbf[:, col:col + 64],
                                      ST1[cpar][:, :], start=True, stop=False)
                        mm = tensor.matmul(opp[cpar][0:64, 0:128], Pb[blk][:, :],
                                           qT_h[:, c * 128:(c + 1) * 128],
                                           start=False, stop=(cl == 0))
                        for j in range(cl):
                            mm = tensor.matmul(opp[cpar][0:64, 0:128], Gl[:, j * 64:(j + 1) * 64],
                                               qT_h[:, c * 128:(c + 1) * 128],
                                               start=False, stop=(j == cl - 1))
                        ctx.inc("TC", mm)
                    assert ctx.n("TC") == t3
                    t3_tc[c] = t3

                    @ctx.block.vector
                    def _(vector, hh=hh, c=c, cpar=cpar, t1=t1, t2=t2,
                          t3=t3, odd=odd, mloc=mloc, osf=oS_free_dot,
                          wtw=t3_tc.get(c - 2, 0), cl=cl):
                        vector.wait_ge(ctx.sem("TC"), t1)
                        if wtw:
                            vector.wait_ge(ctx.sem("TC"), wtw)
                        vector.tensor_mul(ST1[cpar][:, :], scp[cpar][0:128, 0:128], msk[:, :])
                        ctx.inc("VC", vector.drain())
                        if cl != 7:
                            vector.wait_ge(ctx.sem("TC"), t2)
                            vector.tensor_copy(Gl[:, cl * 64:(cl + 1) * 64],
                                               prj[cpar][0:64, 0:64])
                            ctx.inc("VC", vector.drain())
                        vector.wait_ge(ctx.sem("TC"), t3)
                        if odd and c == 0 and osf:
                            vector.wait_ge(ctx.sem("DOT"), osf)
                        if odd:
                            out = oS[:, c * 128:(c + 1) * 128]
                        else:
                            out = oTh[0:64, mloc * 2048 + c * 128: mloc * 2048 + (c + 1) * 128]
                        vector.tensor_copy(out, opp[cpar][0:64, 0:128])
                        ctx.inc("VC", vector.drain())
                    if cl != 7:
                        gl_vc[c] = vc_sT + 1
                    oev_sc[c] = ctx.n("VC")

                prev_head_t3 = ctx.n("TC")
                ctx.l0_prev_head_t3 = prev_head_t3
                t3_hist[half * 8 + hh] = prev_head_t3

                if odd:
                    @ctx.block.sync
                    def _(sync, mloc=mloc, vc_last=ctx.n("VC")):
                        sync.wait_ge(ctx.sem("VC"), vc_last)
                        ctx.inc("DOT", sync.dma_start(
                            out=oTh[64:128, mloc * 2048:(mloc + 1) * 2048], in_=oS[:, :]), 16)
                    oS_free_dot = ctx.n("DOT")

            dot_n = ctx.n("DOT")
            vc_chunks = ctx.n("VC")
            sc_chunks = ctx.n("SC")
            wo_t0 = ctx.n("TC")
            vc_wo0 = ctx.n("VC")

            @ctx.block.tensor
            def _(tensor, dg_wo=dg_wo, dot_n=dot_n, sc_chunks=sc_chunks,
                  vc_wo0=vc_wo0, vc_chunks=vc_chunks):
                tensor.wait_ge(ctx.sem("DAW"), dg_wo)
                tensor.wait_ge(ctx.sem("DOT"), dot_n)
                tensor.wait_ge(ctx.sem("SC"), sc_chunks)
                tensor.wait_ge(ctx.sem("VC"), vc_chunks)
                for tt in range(NT):
                    for dh in range(2):
                        if tt >= 1:
                            tensor.wait_ge(ctx.sem("VC"), vc_wo0 + (tt - 1) * 2 + dh + 1)
                        last = None
                        for mm in range(4):
                            last = tensor.matmul(
                                prj[dh][:, :],
                                oTh[:, mm * 2048 + tt * 128: mm * 2048 + (tt + 1) * 128],
                                wosb[:, mm * 1024 + dh * 512: mm * 1024 + (dh + 1) * 512],
                                start=(mm == 0), stop=(mm == 3))
                        ctx.inc("TC", last)

            @ctx.block.vector
            def _(vector, wo_t0=wo_t0):
                vector.drain()
                for tt in range(NT):
                    for dh in range(2):
                        vector.wait_ge(ctx.sem("TC"), wo_t0 + tt * 2 + dh + 1)
                        ctx.inc("VC", vector.tensor_add(
                            ctx.resid[:, tt * D + dh * 512: tt * D + (dh + 1) * 512],
                            ctx.resid[:, tt * D + dh * 512: tt * D + (dh + 1) * 512],
                            prj[dh][:, :]))
            vc_wo = ctx.n("VC")
            assert vc_wo == vc_wo0 + 32
            wo_free_tc = ctx.n("TC")
            ctx.pfx_free_vc = vc_chunks
            ctx.wbuf_free_tc = ctx.n("TC")
            prj_war = [(0, vc_wo), (0, vc_wo)]

        @ctx.block.vector
        def _(vector):
            ctx.inc("VC", vector.drain())


def l0attn_host_prep(inp, core):
    b, g = core // 4, core % 4
    sel = np.zeros((16,), np.float32)
    for blk, gb in ((0, g), (1, 7 - g)):
        for sgb in range(8):
            if sgb < gb:
                sel[sgb * 2 + blk] = 1.0
    mask = np.tril(np.ones((128, 128), np.float32)).T
    return dict(
        sel=np.ascontiguousarray(np.broadcast_to(sel, (64, 16))),
        msk_lin=np.ascontiguousarray(mask),
    )


def l0attn_host_prep_shared(inp):
    qg = inp["L0_attn_q_gamma"]; qb = inp["L0_attn_q_beta"]
    kg = inp["L0_attn_k_gamma"]; kb = inp["L0_attn_k_beta"]
    return dict(
        wqT=np.ascontiguousarray(inp["L0_attn_Wq"].T),
        wkT=np.ascontiguousarray(inp["L0_attn_Wk"].T),
        wvT=np.ascontiguousarray(inp["L0_attn_Wv"].T),
        woT=np.ascontiguousarray(inp["L0_attn_Wo"].T),
        qklnp=np.ascontiguousarray(np.stack([qg, qb, kg, kb], axis=1)),
        kgbc=np.ascontiguousarray(np.broadcast_to(kg, (128, 64))),
        kbbc=np.ascontiguousarray(np.broadcast_to(kb, (128, 64))),
    )


def build_l0attn_program():
    nc = bass.Bass("TRN2", target_bir_lowering=False)
    io = {}
    io["x"] = nc.dram_tensor("x", [T, D], F32, kind="ExternalInput").ap()
    for nm in ("wqT", "wkT", "wvT", "woT"):
        io[nm] = nc.dram_tensor(nm, [D, D], F32, kind="ExternalInput").ap()
    io["qklnp"] = nc.dram_tensor("qklnp", [64, 4], F32, kind="ExternalInput").ap()
    io["kgbc"] = nc.dram_tensor("kgbc", [128, 64], F32, kind="ExternalInput").ap()
    io["kbbc"] = nc.dram_tensor("kbbc", [128, 64], F32, kind="ExternalInput").ap()
    io["msk_lin"] = nc.dram_tensor("msk_lin", [128, 128], F32, kind="ExternalInput").ap()
    io["sel"] = nc.dram_tensor("sel", [64, 16], F32, kind="ExternalInput").ap()
    io["iden"] = nc.dram_tensor("iden", [128, 128], F32, kind="ExternalInput").ap()
    io["g_loc"] = nc.dram_tensor("g_loc", [64, 1024], F32).ap()
    io["g_all"] = nc.dram_tensor("g_all", [256, 1024], F32).ap()
    io["y"] = nc.dram_tensor("y", [T, D], F32, kind="ExternalOutput").ap()

    with ExitStack() as stack:
        block = stack.enter_context(nc.Block())
        ctx = Ctx(nc, stack, block)
        ctx.debug = False
        ctx.wbuf_free_tc = 0
        for s in ("PH", "GP", "SC", "VC", "TC", "ST", "CC"):
            ctx.sem(s)
        emit_common(ctx, io)
        ctx.lnT = ctx.sb("lnT", [128, ND * T], BF16)
        emit_load_resid(ctx, io, "x")
        barrier(ctx)
        emit_ln(ctx)
        barrier(ctx)
        emit_l0attn(ctx, io)
        barrier(ctx)
        emit_store_resid(ctx, io, "y")
    return nc



KVW = 16384 + 16 * 1040
KVWH = 8192 + 8 * 1040


def emit_l1attn(ctx, io, ln_stk):
    with ExitStack() as stk:
        bqk = ctx.sb("t_bqk", [128, 16], F32, stk)
        bvbc = ctx.sb("t_bvbc", [128, 1024], F32, stk)
        qT = ctx.sb("t_qT", [128, 8 * 2048], BF16, stk)
        sel128 = ctx.sb("t_sel", [128, 2], BF16, stk)
        ones1 = ctx.sb("t_ones1", [1, 128], BF16, stk)
        ones1f = ctx.sb("t_ones1f", [8, 128], F32, stk)
        KNt = ctx.sb("t_knt", [2, 32], F32, stk)
        QNt = ctx.sb("t_qnt", [2, 32], F32, stk)
        KN8 = ctx.sb("t_kn8", [2, 8], F32, stk)
        KN1 = ctx.sb("t_kn1", [1, 16], F32, stk)
        QN1 = ctx.sb("t_qn1", [1, 64], F32, stk)
        KN4 = ctx.sb("t_kn4", [1, 64], F32, stk)
        MN = ctx.sb("t_mn", [1, 64], F32, stk)
        MB = ctx.sb("t_mb", [128, 64], F32, stk)
        ZB = ctx.sb("t_zb", [128, 16], F32, stk)
        MBZ = ctx.sb("t_mbz", [128, 512], F32, stk)
        prj = [ctx.ps(f"t_prj{i}", [128, 512], F32, stk) for i in range(2)]
        prjO = [ctx.ps(f"t_prjO{i}", [128, 512], F32, stk) for i in range(2)]
        accE = [ctx.ps(f"t_accE{i}", [128, 512], F32, stk) for i in range(2)]
        accO = [ctx.ps(f"t_accO{i}", [128, 512], F32, stk) for i in range(2)]
        nrm = prjO[0]
        rbp = prjO[1]

        lnr = ctx.lnT_ready
        kvw = [io["kv_loc0"], io["kv_loc1"]]
        kvg = [io["kv_all0"], io["kv_all1"]]

        @ctx.block.sync
        def _(sync):
            ctx.inc("DTS", sync.dma_start(out=bqk[:, :], in_=io["bqk"][:, :]), 16)
            ctx.inc("DTS", sync.dma_start(out=bvbc[:, :], in_=io["bv_bc"][:, :]), 16)
            ctx.inc("DTS", sync.dma_start(out=ZB[:, :], in_=io["zb"][:, :]), 16)
        ctx.dsem("DTS")
        dts = ctx.n("DTS")

        @ctx.block.gpsimd
        def _(gpsimd):
            gpsimd.memset(ones1[:, :], 1.0)
            gpsimd.memset(ones1f[:, :], 1.0)
            gpsimd.memset(sel128[:, :], 0.0)
            gpsimd.drain()
            gpsimd.memset(sel128[0:64, 0:1], 1.0)
            gpsimd.memset(sel128[64:128, 1:2], 1.0)
            ctx.inc("GP", gpsimd.drain())
        gp_c = ctx.n("GP")

        ctx.dsem("DTW")
        ctx.dsem("DKV0")
        ctx.dsem("DKV1")
        ctx.dsem("DNB")

        with ExitStack() as pstk:
            wbuf = ctx.sb("t_wbuf", [128, ND * 1024], BF16, pstk)
            ksc = ctx.sb("t_ksc", [128, 512], BF16, pstk)
            vsc = ctx.sb("t_vsc", [128, 512], F32, pstk)
            vaH = ctx.sb("t_vaH", [128, 8 * 1040], BF16, pstk)
            sq2 = ctx.sb("t_sq2", [128, 2048], BF16, pstk)

            @ctx.block.gpsimd
            def _(gpsimd):
                gpsimd.memset(vaH[:, :], 1.0)
                ctx.inc("GP", gpsimd.drain())
            gp_va = ctx.n("GP")

            prj_free = [(0, 0), (0, 0)]
            ppar = 0
            ksc_free_dkv = 0
            va_free_dkv = 0
            knt_col = 0
            nrm_free_vc = 0
            wbuf_free = 0

            for wi, wname in ((0, "wq1"), (1, "wk1")):
                @ctx.block.gpsimd
                def _(gpsimd, wname=wname, war_tc=wbuf_free):
                    if war_tc:
                        gpsimd.wait_ge(ctx.sem("TC"), war_tc)
                    ctx.inc("DTW", gpsimd.dma_start(
                        out=wbuf[:, :].rearrange("p (rd f) -> p rd f", rd=ND),
                        in_=io[wname].rearrange("(rd p) f -> p rd f", p=128)), 16)
                dg_w = ctx.n("DTW")

                if wi < 2:
                    for m in range(8):
                        for t4 in range(4):
                            par = ppar % 2
                            ppar += 1

                            @ctx.block.tensor
                            def _(tensor, wi=wi, m=m, t4=t4, par=par, dg_w=dg_w,
                                  pf=tuple(prj_free[par])):
                                if m == 0 and t4 == 0:
                                    if wi == 0:
                                        tensor.wait_ge(ctx.sem("VC"), lnr)
                                    tensor.wait_ge(ctx.sem("DTW"), dg_w)
                                if pf[0]:
                                    tensor.wait_ge(ctx.sem("SC"), pf[0])
                                if pf[1]:
                                    tensor.wait_ge(ctx.sem("VC"), pf[1])
                                last = None
                                for rd in range(ND):
                                    last = tensor.matmul(
                                        prj[par][:, :],
                                        wbuf[:, rd * 1024 + m * 128: rd * 1024 + (m + 1) * 128],
                                        ctx.lnT[:, rd * T + t4 * 512: rd * T + (t4 + 1) * 512],
                                        start=(rd == 0), stop=(rd == ND - 1))
                                ctx.inc("TC", last)
                            tc_p = ctx.n("TC")

                            if wi == 0:
                                @ctx.block.scalar
                                def _(scalar, m=m, t4=t4, par=par, tc_p=tc_p,
                                      first=(m == 0 and t4 == 0)):
                                    if first:
                                        scalar.wait_ge(ctx.sem("DTS"), dts)
                                    scalar.wait_ge(ctx.sem("TC"), tc_p)
                                    scalar.activation(
                                        out=qT[:, m * 2048 + t4 * 512: m * 2048 + (t4 + 1) * 512],
                                        in_=prj[par][:, :], func=AF.Identity,
                                        bias=bqk[:, m:m + 1])
                                    ctx.inc("SC", scalar.drain())
                                prj_free[par] = (ctx.n("SC"), 0)
                            else:
                                @ctx.block.scalar
                                def _(scalar, m=m, par=par, tc_p=tc_p, kdkv=ksc_free_dkv,
                                      sqw=getattr(ctx, "t_sq2_free_tc", 0)):
                                    scalar.wait_ge(ctx.sem("TC"), tc_p)
                                    if kdkv:
                                        scalar.wait_ge(ctx.sem(kdkv[0]), kdkv[1])
                                    if sqw:
                                        scalar.wait_ge(ctx.sem("TC"), sqw)
                                    scalar.activation(out=ksc[:, :], in_=prj[par][:, :],
                                                      func=AF.Identity, bias=bqk[:, 8 + m:9 + m])
                                    scalar.drain()
                                    scalar.activation(out=sq2[:, 0:512], in_=ksc[:, :], func=AF.Square)
                                    ctx.inc("SC", scalar.drain())
                                sc_k = ctx.n("SC")
                                prj_free[par] = (sc_k, 0)

                                @ctx.block.sync
                                def _(sync, m=m, t4=t4, sc_k=sc_k):
                                    sync.wait_ge(ctx.sem("SC"), sc_k)
                                    kh, lm = m // 4, m % 4
                                    ctx.inc("DKV%d" % kh, sync.dma_start(
                                        out=kvw[kh][:, lm * 2048 + t4 * 512: lm * 2048 + (t4 + 1) * 512],
                                        in_=ksc[:, :]), 16)
                                ksc_free_dkv = ("DKV%d" % (m // 4), ctx.n("DKV%d" % (m // 4)))

                                @ctx.block.tensor
                                def _(tensor, sc_k=sc_k, wn=nrm_free_vc, gp_c=gp_c):
                                    tensor.wait_ge(ctx.sem("GP"), gp_c)
                                    tensor.wait_ge(ctx.sem("SC"), sc_k)
                                    if wn:
                                        tensor.wait_ge(ctx.sem("VC"), wn)
                                    ctx.inc("TC", tensor.matmul(nrm[0:2, 0:512], sel128[:, :],
                                                                sq2[:, 0:512], start=True, stop=True))
                                tc_n = ctx.n("TC")

                                @ctx.block.vector
                                def _(vector, tc_n=tc_n, col=knt_col):
                                    vector.wait_ge(ctx.sem("TC"), tc_n)
                                    vector.reduce_max(KNt[:, col:col + 1], nrm[0:2, 0:512], axis=AX.X)
                                    ctx.inc("VC", vector.drain())
                                nrm_free_vc = ctx.n("VC")
                                knt_col += 1
                                ctx.t_sq2_free_tc = tc_n
                wbuf_free = ctx.n("TC")

            sc_qsq0 = ctx.n("SC")

            tc_qn0 = ctx.n("TC")

            @ctx.block.scalar
            def _(scalar, tc_qn0=tc_qn0, kw=getattr(ctx, "t_sq2_free_tc", 0)):
                for m in range(8):
                    if m == 0:
                        if kw:
                            scalar.wait_ge(ctx.sem("TC"), kw)
                    else:
                        scalar.wait_ge(ctx.sem("TC"), tc_qn0 + 4 * m)
                    scalar.activation(out=sq2[:, :], in_=qT[:, m * 2048:(m + 1) * 2048],
                                      func=AF.Square)
                    ctx.inc("SC", scalar.drain())

            vc_qn0 = ctx.n("VC")

            @ctx.block.tensor
            def _(tensor, sc_qsq0=sc_qsq0, tc_qn0=tc_qn0, vc_qn0=vc_qn0, wn=nrm_free_vc):
                if wn:
                    tensor.wait_ge(ctx.sem("VC"), wn)
                i = 0
                for m in range(8):
                    tensor.wait_ge(ctx.sem("SC"), sc_qsq0 + m + 1)
                    for qgg in range(4):
                        if i >= 1:
                            tensor.wait_ge(ctx.sem("VC"), vc_qn0 + i)
                        ctx.inc("TC", tensor.matmul(nrm[0:2, 0:512], sel128[:, :],
                                                    sq2[:, qgg * 512:(qgg + 1) * 512],
                                                    start=True, stop=True))
                        i += 1

            @ctx.block.vector
            def _(vector, tc_qn0=tc_qn0):
                for i in range(32):
                    vector.wait_ge(ctx.sem("TC"), tc_qn0 + i + 1)
                    vector.reduce_max(QNt[:, i:i + 1], nrm[0:2, 0:512], axis=AX.X)
                    ctx.inc("VC", vector.drain())
                vector.reduce_max(KN8[:, :], KNt[:, :].rearrange("p (m c) -> p m c", m=8),
                                  axis=AX.X)
                ctx.inc("VC", vector.drain())
            vc_qn = ctx.n("VC")

            @ctx.block.sync
            def _(sync, vc_qn=vc_qn):
                sync.wait_ge(ctx.sem("VC"), vc_qn)
                ctx.inc("DNB", sync.dma_start(out=io["knloc"][:, :], in_=KN8[:, :]), 16)
                ctx.inc("DNB", sync.dma_start(out=io["nscr"][0:2, 0:32], in_=QNt[:, :]), 16)
            dnb = ctx.n("DNB")

            @ctx.block.gpsimd
            def _(gpsimd, dnb=dnb):
                gpsimd.wait_ge(ctx.sem("DNB"), dnb)
                ctx.inc("CC", gpsimd.collective_compute(
                    "AllReduce", ALU.max, replica_groups=GROUPS,
                    ins=[io["knloc"].opt()], outs=[io["nmax"].opt()]))
            cc_kn = ctx.n("CC")

            @ctx.block.sync
            def _(sync, cc_kn=cc_kn):
                sync.wait_ge(ctx.sem("CC"), cc_kn)
                ctx.inc("DNB", sync.dma_start(
                    out=KN1[:, :], in_=io["nmax"].rearrange("a b -> () (a b)")), 16)
                ctx.inc("DNB", sync.dma_start(
                    out=QN1[:, :].rearrange("p (m a q) -> p m a q", m=8, a=2),
                    in_=io["nscr"][0:2, 0:32].rearrange("a (m q) -> () m a q", m=8)), 16)
            dnb2 = ctx.n("DNB")

            @ctx.block.vector
            def _(vector, dnb2=dnb2):
                vector.wait_ge(ctx.sem("DNB"), dnb2)
                for qg in range(4):
                    vector.tensor_copy(
                        KN4[:, :].rearrange("p (h q) -> p h q", q=4)[:, :, qg:qg + 1],
                        KN1[:, :].rearrange("p h -> p h ()"))
                vector.drain()
                vector.tensor_mul(MN[:, :], QN1[:, :], KN4[:, :])
                ctx.inc("VC", vector.drain())
            vc_mn = ctx.n("VC")

            @ctx.block.scalar
            def _(scalar, vc_mn=vc_mn):
                scalar.wait_ge(ctx.sem("VC"), vc_mn)
                scalar.activation(out=MN[:, :], in_=MN[:, :], func=AF.Sqrt)
                ctx.inc("SC", scalar.drain())
            sc_mn = ctx.n("SC")

            @ctx.block.vector
            def _(vector, sc_mn=sc_mn):
                vector.wait_ge(ctx.sem("SC"), sc_mn)
                vector.tensor_scalar(MN[:, :], MN[:, :], 0.125, 1.0, ALU.mult, ALU.add)
                ctx.inc("VC", vector.drain())
            vc_mnf = ctx.n("VC")

            @ctx.block.tensor
            def _(tensor, vc_mnf=vc_mnf):
                tensor.wait_ge(ctx.sem("VC"), vc_mnf)
                ctx.inc("TC", tensor.matmul(rbp[:, 0:64], ones1f[0:1, :], MN[:, :],
                                            start=True, stop=True))
            tc_mb = ctx.n("TC")

            @ctx.block.vector
            def _(vector, tc_mb=tc_mb, dts=dts):
                vector.wait_ge(ctx.sem("TC"), tc_mb)
                vector.wait_ge(ctx.sem("DTS"), dts)
                vector.tensor_scalar_mul(MB[:, :], rbp[:, 0:64], -1.0)
                vector.drain()
                for gb in range(8):
                    for qb in range(2):
                        vector.tensor_scalar(
                            MBZ[:, gb * 64: (gb + 1) * 64]
                                .rearrange("p (h q) -> p h q", q=4)[:, :, 2 * qb: 2 * qb + 2],
                            MB[:, :].rearrange("p (h q) -> p h q", q=4)[:, :, 2 * qb: 2 * qb + 2],
                            ZB[:, qb * 8 + gb: qb * 8 + gb + 1], None, ALU.add)
                ctx.inc("VC", vector.drain())
            vc_mbz = ctx.n("VC")

            @ctx.block.gpsimd
            def _(gpsimd, war_tc=wbuf_free):
                if war_tc:
                    gpsimd.wait_ge(ctx.sem("TC"), war_tc)
                ctx.inc("DTW", gpsimd.dma_start(
                    out=wbuf[:, :].rearrange("p (rd f) -> p rd f", rd=ND),
                    in_=io["wv1"].rearrange("(rd p) f -> p rd f", p=128)), 16)
            dg_w = ctx.n("DTW")

            for fh in range(2):
                for tt in range(NT):
                    par = ppar % 2
                    ppar += 1

                    @ctx.block.tensor
                    def _(tensor, tt=tt, fh=fh, par=par, dg_w=dg_w,
                          pf=tuple(prj_free[par])):
                        if tt == 0 and fh == 0:
                            tensor.wait_ge(ctx.sem("DTW"), dg_w)
                        if pf[0]:
                            tensor.wait_ge(ctx.sem("SC"), pf[0])
                        if pf[1]:
                            tensor.wait_ge(ctx.sem("VC"), pf[1])
                        last = None
                        for rd in range(ND):
                            last = tensor.matmul(
                                prj[par][:, :],
                                ctx.lnT[:, rd * T + tt * 128: rd * T + (tt + 1) * 128],
                                wbuf[:, rd * 1024 + fh * 512: rd * 1024 + (fh + 1) * 512],
                                start=(rd == 0), stop=(rd == ND - 1))
                        ctx.inc("TC", last)
                    tc_p = ctx.n("TC")

                    @ctx.block.vector
                    def _(vector, tt=tt, fh=fh, par=par, tc_p=tc_p, wdkv=va_free_dkv,
                          gp_va=gp_va):
                        vector.wait_ge(ctx.sem("TC"), tc_p)
                        if tt == 0 and fh == 0:
                            vector.wait_ge(ctx.sem("GP"), gp_va)
                        if tt == 0 and wdkv:
                            vector.wait_ge(ctx.sem(wdkv[0]), wdkv[1])
                        vector.tensor_add(vsc[:, :], prj[par][:, :],
                                          bvbc[:, fh * 512:(fh + 1) * 512])
                        vector.drain()
                        for hh in range(8):
                            vector.tensor_copy(vaH[:, hh * 1040 + tt * 65: hh * 1040 + tt * 65 + 64],
                                               vsc[:, hh * 64:(hh + 1) * 64])
                        ctx.inc("VC", vector.drain())
                    vc_va = ctx.n("VC")
                    prj_free[par] = (0, vc_va)

                vc_vall = ctx.n("VC")

                @ctx.block.sync
                def _(sync, fh=fh, vc_vall=vc_vall):
                    sync.wait_ge(ctx.sem("VC"), vc_vall)
                    for hh in range(8):
                        ctx.inc("DKV%d" % fh, sync.dma_start(
                            out=kvw[fh][:, 8192 + hh * 1040: 8192 + (hh + 1) * 1040],
                            in_=vaH[:, hh * 1040:(hh + 1) * 1040]), 16)
                va_free_dkv = ("DKV%d" % fh, ctx.n("DKV%d" % fh))

                dkvh_all = ctx.n("DKV%d" % fh)

                @ctx.block.gpsimd
                def _(gpsimd, fh=fh, dkvh_all=dkvh_all):
                    gpsimd.wait_ge(ctx.sem("DKV%d" % fh), dkvh_all)
                    ctx.inc("CC", gpsimd.collective_compute(
                        "AllGather", ALU.bypass, replica_groups=GROUPS,
                        ins=[kvw[fh].opt()], outs=[kvg[fh].opt()]))
                if fh == 0:
                    cc_kv0 = ctx.n("CC")
                    dkv0_all = dkvh_all
                else:
                    cc_kv1 = ctx.n("CC")
                    dkv1_all = dkvh_all
            wbuf_free = ctx.n("TC")
            proj_end_tc = ctx.n("TC")

        ln_stk.close()

        with ExitStack() as fstk:
            mskf = ctx.sb("t_mskf", [128, 4 * 512], BF16, fstk)
            kT2 = ctx.sb("t_kT2", [128, 10 * 1024], BF16, fstk)
            vE = ctx.sb("t_vE", [128, 10 * 520], BF16, fstk)
            vO = ctx.sb("t_vO", [128, 10 * 520], BF16, fstk)
            pTE = [ctx.sb(f"t_pTE{i}", [128, 512], BF16, fstk) for i in range(2)]
            pTO = [ctx.sb(f"t_pTO{i}", [128, 512], BF16, fstk) for i in range(2)]
            rrow = ctx.sb("t_rrow", [1, 4096], F32, fstk)
            oT = ctx.sb("t_oT", [128, 8 * 2048], BF16, fstk)
            oS = ctx.sb("t_oS", [64, 2048], BF16, fstk)

            @ctx.block.gpsimd
            def _(gpsimd, proj_end_tc=proj_end_tc, vc_mbz=vc_mbz):
                gpsimd.wait_ge(ctx.sem("TC"), proj_end_tc)
                gpsimd.wait_ge(ctx.sem("VC"), vc_mbz)
                ctx.inc("DTW", gpsimd.dma_start(out=mskf[:, :], in_=io["mskf"][:, :]), 16)
            dts2 = ctx.n("DTW")

            ctx.dsem("DFH")
            ctx.dsem("DOT")
            pair_free_tc = 0
            oS_free_dot = 0
            exp_sc = [0, 0]
            acc_free = [0, 0]
            av_done = [0, 0]
            rb_free_vc = 0

            for m in range(8):
                hE, hO = 2 * m, 2 * m + 1

                @ctx.block.sync
                def _(sync, m=m, hE=hE, hO=hO, wtc=pair_free_tc, dts2=dts2,
                      cc_kv0=cc_kv0, cc_kv1=cc_kv1,
                      dkv0_all=dkv0_all, dkv1_all=dkv1_all):
                    kh, lm = m // 4, m % 4
                    if m == 0:
                        sync.wait_ge(ctx.sem("DKV0"), dkv0_all)
                        sync.wait_ge(ctx.sem("DTW"), dts2)
                    if m == 4:
                        sync.wait_ge(ctx.sem("DKV1"), dkv1_all)
                    if wtc:
                        sync.wait_ge(ctx.sem("TC"), wtc)
                    for qb in range(2):
                        ctx.inc("DFD", sync.dma_start(
                            out=kT2[:, (8 + qb) * 1024:(9 + qb) * 1024],
                            in_=kvw[kh][:, lm * 2048 + qb * 1024: lm * 2048 + (qb + 1) * 1024]), 16)
                        for hh, vt in ((hE, vE), (hO, vO)):
                            lh = hh % 8
                            ctx.inc("DFD", sync.dma_start(
                                out=vt[:, (8 + qb) * 520:(9 + qb) * 520],
                                in_=kvw[kh][:, 8192 + lh * 1040 + qb * 520: 8192 + lh * 1040 + (qb + 1) * 520]), 16)
                    if m == 0:
                        sync.wait_ge(ctx.sem("CC"), cc_kv0)
                    if m == 4:
                        sync.wait_ge(ctx.sem("CC"), cc_kv1)
                    for r in range(4):
                        for sub in range(2):
                            slot = r if sub == 0 else 7 - r
                            ctx.inc("DFH", sync.dma_start(
                                out=kT2[:, slot * 1024:(slot + 1) * 1024],
                                in_=kvg[kh][r * 128:(r + 1) * 128,
                                            lm * 2048 + sub * 1024: lm * 2048 + (sub + 1) * 1024]), 16)
                            for hh, vt in ((hE, vE), (hO, vO)):
                                lh = hh % 8
                                ctx.inc("DFH", sync.dma_start(
                                    out=vt[:, slot * 520:(slot + 1) * 520],
                                    in_=kvg[kh][r * 128:(r + 1) * 128,
                                                8192 + lh * 1040 + sub * 520: 8192 + lh * 1040 + (sub + 1) * 520]), 16)
                dfh = ctx.n("DFH")
                dfd = ctx.n("DFD")

                pend = [None]
                av_rec = {}

                def flush_av(tensor):
                    if pend[0] is None:
                        return
                    (par, vslot, c2i, aq, first, lastu, wsem, wval, uidx) = pend[0]
                    tensor.wait_ge(ctx.sem(wsem), wval)
                    ctx.inc("TC", tensor.matmul(
                        accE[aq][0:65, 0:512],
                        vE[:, vslot * 520 + c2i * 65: vslot * 520 + (c2i + 1) * 65],
                        pTE[par][:, :], start=first, stop=lastu))
                    ctx.inc("TC", tensor.matmul(
                        accO[aq][0:65, 0:512],
                        vO[:, vslot * 520 + c2i * 65: vslot * 520 + (c2i + 1) * 65],
                        pTO[par][:, :], start=first, stop=lastu))
                    av_done[par] = ctx.n("TC")
                    av_rec[uidx] = ctx.n("TC")
                    pend[0] = None

                ppi = 0
                for qblk in range(2):
                    units = []
                    for qg in range(2):
                        if qg == 0:
                            for c2 in range(4):
                                units.append((8 + qblk, qg, c2, "mask"))
                        else:
                            for c2 in range(4):
                                units.append((8 + qblk, qg, c2, "full"))
                            for c2 in range(4, 8):
                                units.append((8 + qblk, qg, c2, "mask"))
                    ngb = 4 if qblk == 0 else 8
                    for gb in range(ngb):
                        for qg in range(2):
                            for c2 in range(8):
                                units.append((gb, qg, c2, "plain"))
                    seen = {0: False, 1: False}
                    lastidx = {0: None, 1: None}
                    for idx, (gb, qg, c2, cls) in enumerate(units):
                        lastidx[qg] = idx

                    sc_base = ctx.n("SC")
                    vc_base = ctx.n("VC")
                    ppi0 = ppi
                    prev_av = {0: av_done[0], 1: av_done[1]}
                    nmask_before = []
                    nm = 0
                    for (gb, qg, c2, cls) in units:
                        nmask_before.append(nm)
                        if cls == "mask":
                            nm += 1
                    tc_scores = []
                    exp_meta = []

                    for idx, (gb, qg, c2, cls) in enumerate(units):
                        par = ppi % 2
                        ppi += 1
                        qgg = qblk * 2 + qg
                        qlo = m * 2048 + qblk * 1024 + qg * 512
                        first = not seen[qg]
                        seen[qg] = True
                        lastu = (idx == lastidx[qg])
                        if cls == "plain":
                            biasE = MBZ[:, gb * 64 + hE * 4 + qgg: gb * 64 + hE * 4 + qgg + 1]
                            biasO = MBZ[:, gb * 64 + hO * 4 + qgg: gb * 64 + hO * 4 + qgg + 1]
                        else:
                            biasE = MB[:, hE * 4 + qgg: hE * 4 + qgg + 1]
                            biasO = MB[:, hO * 4 + qgg: hO * 4 + qgg + 1]
                        if cls == "mask":
                            wsem, wval = "VC", vc_base + nmask_before[idx] + 1
                        else:
                            wsem, wval = "SC", sc_base + idx + 1
                        if idx >= 2:
                            w_prj = ("SC", sc_base + idx - 1)
                        else:
                            w_prj = ("SC", exp_sc[par]) if exp_sc[par] else None

                        @ctx.block.tensor
                        def _(tensor, gb=gb, c2=c2, par=par, dfh=dfh, dfd=dfd,
                              qlo=qlo, w_prj=w_prj, wfirst=(m == 0 and ppi == 1),
                              vc_mbz=vc_mbz, af=acc_free[qg] if first else 0,
                              rbf=rb_free_vc if (qblk == 0 and idx < 2) else 0,
                              first=first, isdiag=(gb >= 8)):
                            if wfirst:
                                tensor.wait_ge(ctx.sem("VC"), vc_mbz)
                            if rbf:
                                tensor.wait_ge(ctx.sem("VC"), rbf)
                            if first and af:
                                tensor.wait_ge(ctx.sem("VC"), af)
                            if isdiag:
                                tensor.wait_ge(ctx.sem("DFD"), dfd)
                            else:
                                tensor.wait_ge(ctx.sem("DFH"), dfh)
                            if w_prj:
                                tensor.wait_ge(ctx.sem(w_prj[0]), w_prj[1])
                            ctx.inc("TC", tensor.matmul(
                                prj[par][0:128, 0:512],
                                kT2[0:64, gb * 1024 + c2 * 128: gb * 1024 + (c2 + 1) * 128],
                                qT[0:64, qlo: qlo + 512],
                                start=True, stop=True))
                            ctx.inc("TC", tensor.matmul(
                                prjO[par][0:128, 0:512],
                                kT2[64:128, gb * 1024 + c2 * 128: gb * 1024 + (c2 + 1) * 128],
                                qT[64:128, qlo: qlo + 512],
                                start=True, stop=True))
                            tc_scores.append(ctx.n("TC"))
                            flush_av(tensor)
                        pend[0] = (par, gb, c2, qg, first, lastu, wsem, wval, idx)
                        exp_meta.append((par, biasE, biasO, idx))
                        exp_sc[par] = sc_base + idx + 1

                    @ctx.block.tensor
                    def _(tensor):
                        flush_av(tensor)
                    tc_last = ctx.n("TC")

                    @ctx.block.scalar
                    def _(scalar, exp_meta=list(exp_meta), tc_scores=list(tc_scores),
                          av_rec=av_rec, prev_av=dict(prev_av)):
                        for (par, biasE, biasO, idx) in exp_meta:
                            scalar.wait_ge(ctx.sem("TC"), tc_scores[idx])
                            if idx >= 2:
                                scalar.wait_ge(ctx.sem("TC"), av_rec[idx - 2])
                            elif prev_av.get(par):
                                scalar.wait_ge(ctx.sem("TC"), prev_av[par])
                            scalar.activation(out=pTE[par][:, :], in_=prj[par][0:128, 0:512],
                                              func=AF.Exp, scale=0.125, bias=biasE)
                            scalar.activation(out=pTO[par][:, :], in_=prjO[par][0:128, 0:512],
                                              func=AF.Exp, scale=0.125, bias=biasO)
                            ctx.inc("SC", scalar.drain())
                    assert ctx.n("SC") == sc_base + len(units)

                    masked = [(i, u) for i, u in enumerate(units) if u[3] == "mask"]
                    if masked:
                        @ctx.block.vector
                        def _(vector, masked=list(masked), sc_base=sc_base, ppi0=ppi0):
                            for (idx, (gb, qg, c2, cls)) in masked:
                                d = c2 if qg == 0 else c2 - 4
                                p = (ppi0 + idx) % 2
                                vector.wait_ge(ctx.sem("SC"), sc_base + idx + 1)
                                vector.tensor_mul(pTE[p][:, :], pTE[p][:, :],
                                                  mskf[:, d * 512:(d + 1) * 512])
                                vector.tensor_mul(pTO[p][:, :], pTO[p][:, :],
                                                  mskf[:, d * 512:(d + 1) * 512])
                                ctx.inc("VC", vector.drain())
                        assert ctx.n("VC") == vc_base + len(masked)

                    for (hh, accT, hp) in ((hE, accE, 0), (hO, accO, 1)):
                        @ctx.block.vector
                        def _(vector, accT=accT, hp=hp, m=m, qblk=qblk,
                              tc_last=tc_last, osf=oS_free_dot, isodd=(hp == 1)):
                            vector.wait_ge(ctx.sem("TC"), tc_last)
                            if isodd and qblk == 0 and osf:
                                vector.wait_ge(ctx.sem("DOT"), osf)
                            for qg in range(2):
                                col = qblk * 1024 + qg * 512
                                r = hp * 4 + qblk * 2 + qg
                                vector.reciprocal(
                                    rrow[0:1, r * 512:(r + 1) * 512],
                                    accT[qg][64:65, 0:512])
                                if not isodd:
                                    out = oT[0:64, m * 2048 + col: m * 2048 + col + 512]
                                else:
                                    out = oS[:, col:col + 512]
                                vector.tensor_copy(out, accT[qg][0:64, 0:512])
                                ctx.inc("VC", vector.drain())
                        if hp == 1:
                            acc_free[0] = ctx.n("VC") - 1
                            acc_free[1] = ctx.n("VC")

                vc_rr = ctx.n("VC")

                rbps = [prj[0], prj[1], prjO[0], prjO[1]]
                div_vc = {}
                for u in range(8):
                    hp, qc = u // 4, u % 4
                    pb = rbps[u % 4]

                    @ctx.block.tensor
                    def _(tensor, u=u, pb=pb, vc_rr=vc_rr,
                          wmul=div_vc.get(u - 4, 0)):
                        if u == 0:
                            tensor.wait_ge(ctx.sem("VC"), vc_rr)
                        if wmul:
                            tensor.wait_ge(ctx.sem("VC"), wmul)
                        ctx.inc("TC", tensor.matmul(
                            pb[0:64, 0:512], ones1f[0:1, 0:64],
                            rrow[0:1, u * 512:(u + 1) * 512],
                            start=True, stop=True))
                    tc_rb_u = ctx.n("TC")

                    @ctx.block.vector
                    def _(vector, hp=hp, qc=qc, pb=pb, tc=tc_rb_u, m=m):
                        vector.wait_ge(ctx.sem("TC"), tc)
                        col = qc * 512
                        if hp == 0:
                            out = oT[0:64, m * 2048 + col: m * 2048 + col + 512]
                        else:
                            out = oS[:, col:col + 512]
                        vector.tensor_mul(out, out, pb[0:64, 0:512])
                        ctx.inc("VC", vector.drain())
                    div_vc[u] = ctx.n("VC")
                rb_free_vc = ctx.n("VC")

                pair_free_tc = ctx.n("TC")

                @ctx.block.sync
                def _(sync, m=m, vc_e=ctx.n("VC")):
                    sync.wait_ge(ctx.sem("VC"), vc_e)
                    ctx.inc("DOT", sync.dma_start(
                        out=oT[64:128, m * 2048:(m + 1) * 2048], in_=oS[:, :]), 16)
                oS_free_dot = ctx.n("DOT")

            flash_end_tc = ctx.n("TC")
            flash_end_vc = ctx.n("VC")
            dot_all = ctx.n("DOT")

        with ExitStack() as wstk:
            wosb = ctx.sb("t_wosb", [128, 8 * 1024], BF16, wstk)
            bobc = ctx.sb("t_bobc", [128, 1024], F32, wstk)
            gmbc = ctx.sb("t_gmbc", [128, 1024], F32, wstk)
            btbc = ctx.sb("t_btbc", [128, 1024], F32, wstk)
            ysc = ctx.sb("t_ysc", [128, 1024], F32, wstk)
            yh = ctx.sb("t_yh", [128, 1024], F32, wstk)
            ysq = ctx.sb("t_ysq", [128, 1024], F32, wstk)
            yst = ctx.sb("t_yst", [128, 8], F32, wstk)

            @ctx.block.sync
            def _(sync, flash_end_tc=flash_end_tc, flash_end_vc=flash_end_vc):
                sync.wait_ge(ctx.sem("TC"), flash_end_tc)
                sync.wait_ge(ctx.sem("VC"), flash_end_vc)
                ctx.inc("DTS", sync.dma_start(out=bobc[:, :], in_=io["bo_bc"][:, :]), 16)
                ctx.inc("DTS", sync.dma_start(out=gmbc[:, :], in_=io["gm_bc"][:, :]), 16)
                ctx.inc("DTS", sync.dma_start(out=btbc[:, :], in_=io["bt_bc"][:, :]), 16)
            dts3 = ctx.n("DTS")

            @ctx.block.gpsimd
            def _(gpsimd, flash_end_tc=flash_end_tc):
                gpsimd.wait_ge(ctx.sem("TC"), flash_end_tc)
                ctx.inc("DTW", gpsimd.dma_start(
                    out=wosb[:, :].rearrange("p (m d) -> p m d", m=8),
                    in_=io["wo1"].rearrange("(m p) d -> p m d", p=128)), 16)
            dg_wo = ctx.n("DTW")

            prev_evac_vc = 0
            sc_nm_prev = 0
            for tt in range(NT):
                @ctx.block.tensor
                def _(tensor, tt=tt, dg_wo=dg_wo, dot_all=dot_all, pv=prev_evac_vc,
                      fv=flash_end_vc):
                    if tt == 0:
                        tensor.wait_ge(ctx.sem("DTW"), dg_wo)
                        tensor.wait_ge(ctx.sem("DOT"), dot_all)
                        tensor.wait_ge(ctx.sem("VC"), fv)
                    if pv:
                        tensor.wait_ge(ctx.sem("VC"), pv)
                    for dh in range(2):
                        last = None
                        for mm in range(8):
                            last = tensor.matmul(
                                accE[dh][:, :],
                                oT[:, mm * 2048 + tt * 128: mm * 2048 + (tt + 1) * 128],
                                wosb[:, mm * 1024 + dh * 512: mm * 1024 + (dh + 1) * 512],
                                start=(mm == 0), stop=(mm == 7))
                        ctx.inc("TC", last)
                tc_wo = ctx.n("TC")

                @ctx.block.vector
                def _(vector, tt=tt, tc_wo=tc_wo, dts3=dts3, snp=sc_nm_prev):
                    if tt == 0:
                        vector.wait_ge(ctx.sem("DTS"), dts3)
                    if snp:
                        vector.wait_ge(ctx.sem("SC"), snp)
                    vector.wait_ge(ctx.sem("TC"), tc_wo - 1)
                    vector.tensor_add(ysc[:, 0:512], accE[0][:, :], bobc[:, 0:512])
                    vector.wait_ge(ctx.sem("TC"), tc_wo)
                    vector.tensor_add(ysc[:, 512:1024], accE[1][:, :], bobc[:, 512:1024])
                    vector.drain()
                    vector.reduce_sum(yst[:, 0:1], ysc[:, :], axis=AX.X)
                    ctx.inc("VC", vector.drain())
                vc_y = ctx.n("VC")

                @ctx.block.scalar
                def _(scalar, vc_y=vc_y):
                    scalar.wait_ge(ctx.sem("VC"), vc_y)
                    scalar.activation(out=ysq[:, :], in_=ysc[:, :], func=AF.Square,
                                      accum_out=yst[:, 1:2])
                    ctx.inc("SC", scalar.drain())
                sc_ys = ctx.n("SC")

                @ctx.block.vector
                def _(vector, sc_ys=sc_ys):
                    vector.wait_ge(ctx.sem("SC"), sc_ys)
                    vector.tensor_scalar_mul(yst[:, 2:3], yst[:, 0:1], 1.0 / D)
                    vector.tensor_scalar_mul(yst[:, 3:4], yst[:, 1:2], 1.0 / D)
                    vector.drain()
                    vector.tensor_mul(yst[:, 4:5], yst[:, 2:3], yst[:, 2:3])
                    vector.drain()
                    vector.tensor_sub(yst[:, 3:4], yst[:, 3:4], yst[:, 4:5])
                    ctx.inc("VC", vector.drain())
                vc_v = ctx.n("VC")

                @ctx.block.scalar
                def _(scalar, vc_v=vc_v):
                    scalar.wait_ge(ctx.sem("VC"), vc_v)
                    scalar.activation(out=yst[:, 5:6], in_=yst[:, 3:4], func=AF.Sqrt,
                                      bias=cst(ctx, EPS))
                    ctx.inc("SC", scalar.drain())
                sc_sd = ctx.n("SC")

                @ctx.block.vector
                def _(vector, sc_sd=sc_sd):
                    vector.wait_ge(ctx.sem("SC"), sc_sd)
                    vector.reciprocal(yst[:, 6:7], yst[:, 5:6])
                    vector.drain()
                    vector.scalar_tensor_tensor(yst[:, 7:8], yst[:, 2:3], -1.0,
                                                yst[:, 6:7], ALU.mult, ALU.mult)
                    ctx.inc("VC", vector.drain())
                vc_rs = ctx.n("VC")

                @ctx.block.scalar
                def _(scalar, vc_rs=vc_rs):
                    scalar.wait_ge(ctx.sem("VC"), vc_rs)
                    scalar.activation(out=yh[:, :], in_=ysc[:, :], func=AF.Identity,
                                      scale=yst[:, 6:7], bias=yst[:, 7:8])
                    ctx.inc("SC", scalar.drain())
                sc_nm = ctx.n("SC")
                sc_nm_prev = sc_nm

                @ctx.block.vector
                def _(vector, tt=tt, sc_nm=sc_nm):
                    vector.wait_ge(ctx.sem("SC"), sc_nm)
                    vector.tensor_mul(yh[:, :], yh[:, :], gmbc[:, :])
                    vector.drain()
                    vector.tensor_add(yh[:, :], yh[:, :], btbc[:, :])
                    vector.drain()
                    vector.tensor_add(ctx.resid[:, tt * D:(tt + 1) * D],
                                      ctx.resid[:, tt * D:(tt + 1) * D], yh[:, :])
                    ctx.inc("VC", vector.drain())
                prev_evac_vc = ctx.n("VC")

        @ctx.block.vector
        def _(vector):
            ctx.inc("VC", vector.drain())


def l1attn_host_prep(inp, core):
    g = core % 4
    zb = np.zeros((2, 8), np.float32)
    for qb, gbq in ((0, g), (1, 7 - g)):
        for gb in range(8):
            if gb >= gbq:
                zb[qb, gb] = -1e30
    mskf = np.zeros((4, 128, 512), np.float32)
    for d in range(4):
        for j in range(128):
            mskf[d, j, d * 128 + j:] = 1.0
    return dict(
        zb=np.ascontiguousarray(np.broadcast_to(zb.reshape(16), (128, 16))),
        mskf=np.ascontiguousarray(mskf.transpose(1, 0, 2).reshape(128, 2048)),
    )


def l1attn_host_prep_shared(inp):
    bq = inp["L1_attn_bq"]; bk = inp["L1_attn_bk"]
    bqk = np.concatenate([bq.reshape(8, 128).T, bk.reshape(8, 128).T], axis=1)
    return dict(
        wq1=np.ascontiguousarray(inp["L1_attn_Wq"]),
        wk1=np.ascontiguousarray(inp["L1_attn_Wk"]),
        wv1=np.ascontiguousarray(inp["L1_attn_Wv"]),
        wo1=np.ascontiguousarray(inp["L1_attn_Wo"]),
        bqk=np.ascontiguousarray(bqk),
        bv_bc=np.ascontiguousarray(np.broadcast_to(inp["L1_attn_bv"], (128, 1024))),
        bo_bc=np.ascontiguousarray(np.broadcast_to(inp["L1_attn_bo"], (128, 1024))),
        gm_bc=np.ascontiguousarray(np.broadcast_to(inp["L1_attn_gamma"], (128, 1024))),
        bt_bc=np.ascontiguousarray(np.broadcast_to(inp["L1_attn_beta"], (128, 1024))),
    )


def build_l1attn_program():
    nc = bass.Bass("TRN2", target_bir_lowering=False)
    io = {}
    io["x"] = nc.dram_tensor("x", [T, D], F32, kind="ExternalInput").ap()
    for nm in ("wq1", "wk1", "wv1", "wo1"):
        io[nm] = nc.dram_tensor(nm, [D, D], F32, kind="ExternalInput").ap()
    io["bqk"] = nc.dram_tensor("bqk", [128, 16], F32, kind="ExternalInput").ap()
    for nm in ("bv_bc", "bo_bc", "gm_bc", "bt_bc"):
        io[nm] = nc.dram_tensor(nm, [128, 1024], F32, kind="ExternalInput").ap()
    io["zb"] = nc.dram_tensor("zb", [128, 16], F32, kind="ExternalInput").ap()
    io["mskf"] = nc.dram_tensor("mskf", [128, 2048], F32, kind="ExternalInput").ap()
    io["iden"] = nc.dram_tensor("iden", [128, 128], F32, kind="ExternalInput").ap()
    io["kv_loc0"] = nc.dram_tensor("kv_loc0", [128, KVWH], BF16).ap()
    io["kv_loc1"] = nc.dram_tensor("kv_loc1", [128, KVWH], BF16).ap()
    io["kv_all0"] = nc.dram_tensor("kv_all0", [512, KVWH], BF16).ap()
    io["kv_all1"] = nc.dram_tensor("kv_all1", [512, KVWH], BF16).ap()
    io["nscr"] = nc.dram_tensor("nscr", [2, 32], F32).ap()
    io["knloc"] = nc.dram_tensor("knloc", [2, 8], F32).ap()
    io["nmax"] = nc.dram_tensor("nmax", [2, 8], F32).ap()
    io["y"] = nc.dram_tensor("y", [T, D], F32, kind="ExternalOutput").ap()

    with ExitStack() as stack:
        block = stack.enter_context(nc.Block())
        ctx = Ctx(nc, stack, block)
        ctx.debug = False
        for s in ("PH", "GP", "SC", "VC", "TC", "ST", "CC"):
            ctx.sem(s)
        emit_common(ctx, io)
        ln_stk = ExitStack()
        ctx.lnT = ctx.sb("lnT", [128, ND * T], BF16, ln_stk, side="right")
        emit_load_resid(ctx, io, "x")
        barrier(ctx)
        emit_ln(ctx)
        barrier(ctx)
        emit_l1attn(ctx, io, ln_stk)
        barrier(ctx)
        emit_store_resid(ctx, io, "y")
    return nc



def _shard_tokens(x):
    out = []
    for core in range(NCORES):
        b, g = core // 4, core % 4
        a = x[b, g * 1024:(g + 1) * 1024]
        c = x[b, (7 - g) * 1024:(8 - g) * 1024]
        out.append(np.ascontiguousarray(np.concatenate([a, c], axis=0)))
    return out


def _unshard_tokens(parts):
    d = parts[0].shape[-1]
    x = np.empty((2, 8192, d), dtype=parts[0].dtype)
    for core in range(NCORES):
        b, g = core // 4, core % 4
        x[b, g * 1024:(g + 1) * 1024] = parts[core][:1024]
        x[b, (7 - g) * 1024:(8 - g) * 1024] = parts[core][1024:]
    return x


def moe_host_prep(inp):
    gate_upT = np.ascontiguousarray(inp["L0_ffn_gate_up_proj"].transpose(0, 2, 1))
    downT = np.ascontiguousarray(inp["L0_ffn_down_proj"].transpose(0, 2, 1))
    wrT = np.ascontiguousarray(inp["L0_ffn_router_weight"].T)
    br = inp["L0_ffn_router_bias"]
    brbc = np.ascontiguousarray(np.broadcast_to(np.tile(br, NT), (128, NT * NE)))
    gub = inp["L0_ffn_gate_up_proj_bias"]
    bg = np.ascontiguousarray(gub[:, :2048].reshape(NE, 16, 128).transpose(2, 0, 1).reshape(128, 128))
    bu = np.ascontiguousarray(gub[:, 2048:].reshape(NE, 16, 128).transpose(2, 0, 1).reshape(128, 128))
    db = np.ascontiguousarray(inp["L0_ffn_down_proj_bias"])
    return dict(gate_upT=gate_upT, downT=downT, wrT=wrT, brbc=brbc, bg=bg, bu=bu, db=db,
                iden=np.eye(128, dtype=np.float32))



def build_moe_program(debug=False):
    nc = bass.Bass("TRN2", target_bir_lowering=False)
    io = {}
    io["x"] = nc.dram_tensor("x", [T, D], F32, kind="ExternalInput").ap()
    io["gate_upT"] = nc.dram_tensor("gate_upT", [NE, D, 4096], F32, kind="ExternalInput").ap()
    io["downT"] = nc.dram_tensor("downT", [NE, 2048, D], F32, kind="ExternalInput").ap()
    io["wrT"] = nc.dram_tensor("wrT", [D, NE], F32, kind="ExternalInput").ap()
    io["brbc"] = nc.dram_tensor("brbc", [128, NT * NE], F32, kind="ExternalInput").ap()
    io["bg"] = nc.dram_tensor("bg", [128, 128], F32, kind="ExternalInput").ap()
    io["bu"] = nc.dram_tensor("bu", [128, 128], F32, kind="ExternalInput").ap()
    io["db"] = nc.dram_tensor("db", [NE, D], F32, kind="ExternalInput").ap()
    io["iden"] = nc.dram_tensor("iden", [128, 128], F32, kind="ExternalInput").ap()
    io["y"] = nc.dram_tensor("y", [T, D], F32, kind="ExternalOutput").ap()
    if debug:
        io["dbg"] = nc.dram_tensor("dbg", [128, 3072], F32, kind="ExternalOutput").ap()

    with ExitStack() as stack:
        block = stack.enter_context(nc.Block())
        ctx = Ctx(nc, stack, block)
        ctx.debug = debug
        for s in ("PH", "GP", "SC", "VC", "TC", "ST"):
            ctx.sem(s)
        emit_common(ctx, io)
        ctx.lnT = ctx.sb("lnT", [128, ND * T], BF16)
        if debug:
            ctx.dbg = ctx.sb("dbgt", [128, 3072], F32)

            @ctx.block.gpsimd
            def _(gpsimd):
                gpsimd.memset(ctx.dbg[:, :], 0.0)
                ctx.inc("GP", gpsimd.drain())
        emit_load_resid(ctx, io, "x")
        barrier(ctx)
        emit_ln(ctx)
        barrier(ctx)
        emit_moe(ctx, io)
        barrier(ctx)
        emit_store_resid(ctx, io, "y")
        if debug:
            @ctx.block.sync
            def _(sync):
                ctx.inc("ST", sync.dma_start(out=io["dbg"][:, :], in_=ctx.dbg[:, :]), 16)
                sync.wait_ge(ctx.sem("ST"), ctx.n("ST"))
    return nc



def build_fused_program():
    nc = bass.Bass("TRN2", target_bir_lowering=False)
    io = {}
    io["x"] = nc.dram_tensor("x", [T, D], F32, kind="ExternalInput").ap()
    io["gate_upT"] = nc.dram_tensor("gate_upT", [NE, D, 4096], F32, kind="ExternalInput").ap()
    io["downT"] = nc.dram_tensor("downT", [NE, 2048, D], F32, kind="ExternalInput").ap()
    io["wrT"] = nc.dram_tensor("wrT", [D, NE], F32, kind="ExternalInput").ap()
    io["brbc"] = nc.dram_tensor("brbc", [128, NT * NE], F32, kind="ExternalInput").ap()
    io["bg"] = nc.dram_tensor("bg", [128, 128], F32, kind="ExternalInput").ap()
    io["bu"] = nc.dram_tensor("bu", [128, 128], F32, kind="ExternalInput").ap()
    io["db"] = nc.dram_tensor("db", [NE, D], F32, kind="ExternalInput").ap()
    for nm in ("wqT", "wkT", "wvT", "woT"):
        io[nm] = nc.dram_tensor(nm, [D, D], F32, kind="ExternalInput").ap()
    io["qklnp"] = nc.dram_tensor("qklnp", [64, 4], F32, kind="ExternalInput").ap()
    io["kgbc"] = nc.dram_tensor("kgbc", [128, 64], F32, kind="ExternalInput").ap()
    io["kbbc"] = nc.dram_tensor("kbbc", [128, 64], F32, kind="ExternalInput").ap()
    io["msk_lin"] = nc.dram_tensor("msk_lin", [128, 128], F32, kind="ExternalInput").ap()
    io["sel"] = nc.dram_tensor("sel", [64, 16], F32, kind="ExternalInput").ap()
    io["g_loc"] = nc.dram_tensor("g_loc", [64, 1024], F32).ap()
    io["g_all"] = nc.dram_tensor("g_all", [256, 1024], F32).ap()
    io["u1"] = nc.dram_tensor("u1", [D, 256], F32, kind="ExternalInput").ap()
    io["v1"] = nc.dram_tensor("v1", [256, 4096], F32, kind="ExternalInput").ap()
    io["u2"] = nc.dram_tensor("u2", [4096, 256], F32, kind="ExternalInput").ap()
    io["v2"] = nc.dram_tensor("v2", [256, 1024], F32, kind="ExternalInput").ap()
    io["b1t"] = nc.dram_tensor("b1t", [128, 32], F32, kind="ExternalInput").ap()
    io["b2bc"] = nc.dram_tensor("b2bc", [128, 1024], F32, kind="ExternalInput").ap()
    for nm in ("wq1", "wk1", "wv1", "wo1"):
        io[nm] = nc.dram_tensor(nm, [D, D], F32, kind="ExternalInput").ap()
    io["bqk"] = nc.dram_tensor("bqk", [128, 16], F32, kind="ExternalInput").ap()
    for nm in ("bv_bc", "bo_bc", "gm_bc", "bt_bc"):
        io[nm] = nc.dram_tensor(nm, [128, 1024], F32, kind="ExternalInput").ap()
    io["zb"] = nc.dram_tensor("zb", [128, 16], F32, kind="ExternalInput").ap()
    io["mskf"] = nc.dram_tensor("mskf", [128, 2048], F32, kind="ExternalInput").ap()
    io["kv_loc0"] = nc.dram_tensor("kv_loc0", [128, KVWH], BF16).ap()
    io["kv_loc1"] = nc.dram_tensor("kv_loc1", [128, KVWH], BF16).ap()
    io["kv_all0"] = nc.dram_tensor("kv_all0", [512, KVWH], BF16).ap()
    io["kv_all1"] = nc.dram_tensor("kv_all1", [512, KVWH], BF16).ap()
    io["nscr"] = nc.dram_tensor("nscr", [2, 32], F32).ap()
    io["knloc"] = nc.dram_tensor("knloc", [2, 8], F32).ap()
    io["nmax"] = nc.dram_tensor("nmax", [2, 8], F32).ap()
    io["iden"] = nc.dram_tensor("iden", [128, 128], F32, kind="ExternalInput").ap()
    io["y"] = nc.dram_tensor("y", [T, D], F32, kind="ExternalOutput").ap()

    with ExitStack() as stack:
        block = stack.enter_context(nc.Block())
        ctx = Ctx(nc, stack, block)
        ctx.debug = False
        ctx.wbuf_free_tc = 0
        for s in ("PH", "GP", "SC", "VC", "TC", "ST", "CC"):
            ctx.sem(s)
        emit_common(ctx, io)
        emit_load_resid(ctx, io, "x")
        barrier(ctx)

        def with_ln(i, fn, needs_scope=False):
            ln_stk = ExitStack()
            ctx.prefix = f"s{i}_"
            ctx.lnT = ctx.sb("lnT", [128, ND * T], BF16, ln_stk, side="right")
            emit_ln(ctx)
            barrier(ctx)
            if needs_scope:
                fn(ln_stk)
            else:
                fn()
                ln_stk.close()
            barrier(ctx)

        with_ln(0, lambda: emit_moe(ctx, io))
        with_ln(1, lambda: emit_l0attn(ctx, io))
        with_ln(2, lambda: emit_moe(ctx, io))
        with_ln(3, lambda: emit_l1ffn(ctx, io))
        with_ln(4, lambda ln_stk: emit_l1attn(ctx, io, ln_stk), needs_scope=True)
        with_ln(5, lambda: emit_l1ffn(ctx, io))
        ctx.prefix = ""
        emit_store_resid(ctx, io, "y")
    return nc


_PROGRAMS = {}


def _get_programs():
    if not _PROGRAMS:
        _PROGRAMS["fused"] = build_fused_program()
    return _PROGRAMS


def kernel(**inputs):
    progs = _get_programs()
    iden = np.eye(128, dtype=np.float32)

    shared = {}
    shared.update(moe_host_prep(inputs))
    shared.update(l0attn_host_prep_shared(inputs))
    shared.update(l1ffn_host_prep(inputs))
    shared.update(l1attn_host_prep_shared(inputs))
    shared["iden"] = iden
    a0_pc = [l0attn_host_prep(inputs, c) for c in range(NCORES)]
    a1_pc = [l1attn_host_prep(inputs, c) for c in range(NCORES)]

    xs = _shard_tokens(np.ascontiguousarray(inputs["L0_attn_x"]))
    in_maps = [dict(shared, x=xs[c], **a0_pc[c], **a1_pc[c]) for c in range(NCORES)]
    res = run_bass_kernel_spmd(progs["fused"], in_maps, core_ids=list(range(NCORES)))
    ys = [res.results[c]["y"] for c in range(NCORES)]
    return _unshard_tokens(ys).astype(np.float32)
